# version 12
# baseline (speedup 1.0000x reference)
import contextlib
import numpy as np
import ml_dtypes
import concourse.bass as bass
import concourse.mybir as mybir
from concourse.bass_utils import run_bass_kernel_spmd
from concourse.bass_core_rust import IndirectOffsetOnAxis

F32 = mybir.dt.float32
BF16 = mybir.dt.bfloat16
U32 = mybir.dt.uint32
AF = mybir.ActivationFunctionType
ALU = mybir.AluOpType
AX = mybir.AxisListType

PAIRS = [(0, 1), (0, 2), (0, 3), (1, 2), (1, 3), (2, 3)]
SGN6 = [1.0, -1.0, 1.0, 1.0, -1.0, 1.0]
SCALE = 6.0 ** -0.5
D = 1024
H = 16
E = 8
FF = 4096
NDT = D // 128
NFT = FF // 128
EPS_LN = 1e-5
EPS_NRM = 1e-24


class Sem:
    def __init__(self, nc, es, name):
        self.h = es.enter_context(nc.semaphore(name))
        self.n = 0
        self.name = name
        import os
        if os.environ.get("SEM_DEBUG"):
            print("SEM", self.h.num, name)

    def at(self):
        return (self, self.n)


class Plan:
    ENGINES = ("sync", "gpsimd", "tensor", "scalar", "vector")

    def __init__(self):
        self.ops = {e: [] for e in self.ENGINES}

    def add(self, eng, fn, waits=(), inc=None, drain=False):
        if inc is not None:
            inc[0].n += inc[1]
        ws = []
        for s, v in waits:
            if v > 0:
                ws.append((s, v))
        self.ops[eng].append((fn, ws, inc, drain))

    def barrier(self, sems):
        marks = [(s, s.n) for s in sems if s.n > 0]
        for e in self.ENGINES:
            self.add(e, None, waits=marks)

    def emit(self, blk):
        def mk(name):
            def run(eng):
                for fn, waits, inc, drain in self.ops[name]:
                    if drain:
                        eng.drain()
                    for s, v in waits:
                        eng.wait_ge(s.h, v)
                    if fn is None:
                        continue
                    instr = fn(eng)
                    if inc is not None:
                        instr.then_inc(inc[0].h, inc[1])
            return run
        blk.sync(mk("sync"))
        blk.gpsimd(mk("gpsimd"))
        blk.tensor(mk("tensor"))
        blk.scalar(mk("scalar"))
        blk.vector(mk("vector"))


def make_consts(T):
    c = {}
    c["idn_f"] = np.eye(128, dtype=np.float32)
    c["idn_bf"] = np.eye(128, dtype=np.float32)
    m4 = np.zeros((128, 4 * 512), np.float32)
    for d_ in range(4):
        delta = d_ * 128
        r = np.arange(128)[:, None]
        q = np.arange(512)[None, :]
        m4[:, d_ * 512:(d_ + 1) * 512] = (r + delta <= q).astype(np.float32)
    c["masks4"] = m4
    sqi = np.zeros((64, 96), np.float32)
    sqj = np.zeros((64, 96), np.float32)
    ski = np.zeros((64, 96), np.float32)
    skj = np.zeros((64, 96), np.float32)
    for h in range(H):
        for cc in range(6):
            i, j = PAIRS[cc]
            ir, jr = PAIRS[5 - cc]
            sqi[4 * h + i, 6 * h + cc] = 1
            sqj[4 * h + j, 6 * h + cc] = 1
            ski[4 * h + ir, 6 * h + cc] = 1
            skj[4 * h + jr, 6 * h + cc] = 1
    c["sqi"], c["sqj"], c["ski"], c["skj"] = sqi, sqj, ski, skj
    o96 = np.zeros((96, 16), np.float32)
    for h in range(H):
        o96[6 * h:6 * h + 6, h] = 1
    c["o96"] = o96
    pg = np.zeros((96, 4 * 128), np.float32)
    s16 = np.zeros((16, 4 * 128), np.float32)
    for g in range(4):
        for j in range(4):
            for cc in range(6):
                pg[6 * (4 * g + j) + cc, g * 128 + 32 * j + cc] = 1
                s16[4 * g + j, g * 128 + 32 * j + cc] = 1
    c["pg"], c["s16"] = pg, s16
    ones8 = np.zeros((8, 3), np.float32)
    ones8[:, 0] = 1
    ones8[:, 1] = np.arange(8)
    ones8[:, 2] = np.arange(8) * (T // 4)
    c["ones8"] = ones8
    c["onescol"] = np.ones((1, 128), np.float32)
    sgn = np.ones((128, 1), np.float32)
    for j in range(4):
        for cc in range(6):
            sgn[32 * j + cc, 0] = SGN6[cc]
    c["sgnpad"] = sgn
    return c


CONST_SPECS = {
    "idn_f": ([128, 128], F32), "idn_bf": ([128, 128], BF16),
    "masks4": ([128, 2048], BF16),
    "sqi": ([64, 96], BF16), "sqj": ([64, 96], BF16),
    "ski": ([64, 96], BF16), "skj": ([64, 96], BF16),
    "o96": ([96, 16], BF16), "pg": ([96, 512], BF16), "s16": ([16, 512], BF16),
    "ones8": ([8, 3], F32),
    "onescol": ([1, 128], BF16), "sgnpad": ([128, 1], F32),
}


def build_program(T=2048, NB=4):
    KT = T // 128
    NQC = T // 512
    NTH = max(1, T // 1024)
    TH = T // NTH
    NTCQ = TH // 512
    NCHUNK = T // 128

    nc = bass.Bass("TRN2", target_bir_lowering=False)

    x_in = nc.dram_tensor("x", [NB * T, D], F32, kind="ExternalInput")
    out_t = nc.dram_tensor("out", [NB * T, D], F32, kind="ExternalOutput")
    wq = {p: nc.dram_tensor(p, [64, D], F32, kind="ExternalInput")
          for p in ("w1q", "w2q", "w1k", "w2k")}
    wv_t = nc.dram_tensor("wv", [D, D], F32, kind="ExternalInput")
    wo_t = nc.dram_tensor("wo", [D, D], F32, kind="ExternalInput")
    wr_t = nc.dram_tensor("wr", [128, NDT * 8], F32, kind="ExternalInput")
    br_t = nc.dram_tensor("br", [1, 8], F32, kind="ExternalInput")
    w1_t = nc.dram_tensor("w1", [E, D, FF], F32, kind="ExternalInput")
    w2_t = nc.dram_tensor("w2", [E, FF, D], F32, kind="ExternalInput")
    bvc_t = nc.dram_tensor("bvc", [128, NDT], F32, kind="ExternalInput")
    boc_t = nc.dram_tensor("boc", [128, NDT], F32, kind="ExternalInput")
    b1c_t = nc.dram_tensor("b1c", [128, E * NFT], F32, kind="ExternalInput")
    b2c_t = nc.dram_tensor("b2c", [128, E * NDT], F32, kind="ExternalInput")
    cin = {n: nc.dram_tensor("c_" + n, [sh[0], sh[1]], F32, kind="ExternalInput")
           for n, (sh, dt) in CONST_SPECS.items()}
    w1bf = nc.dram_tensor("w1bf", [E, D, FF], BF16)
    w2bf = nc.dram_tensor("w2bf", [E, FF, D], BF16)
    wvt_s = nc.dram_tensor("wvt_s", [128, NDT * 1024], BF16)
    wot_s = nc.dram_tensor("wot_s", [128, NDT * 1024], BF16)
    CAP = T // 4
    xnat_d = nc.dram_tensor("xnat_d", [NB * T, D], BF16)
    xperm_d = nc.dram_tensor("xperm_d", [NB * 8 * CAP, D], BF16)
    yperm_d = nc.dram_tensor("yperm_d", [NB * 8 * CAP, D], BF16)
    lists_d = nc.dram_tensor("lists_d", [NB * 8 * CAP, 1], U32)
    offs_d = nc.dram_tensor("offs_d", [NB * (T // 128), 128], U32)
    iotat_t = nc.dram_tensor("iotat", [NB * (T // 128), 128], U32,
                             kind="ExternalInput")

    es = contextlib.ExitStack()

    def sbuf(name, shape, dtype):
        return es.enter_context(nc.sbuf_tensor(name, shape, dtype))

    XN = sbuf("XN", [128, NDT * T], BF16)
    ACC = sbuf("ACC", [128, NDT * T], BF16)
    _CAP = T // 4
    _CPW = (_CAP // 128) * 1024
    ARW = max(8 * T + KT * 1040 + 8192 + 6656,
              32 * _CAP + 2 * _CPW + 2 * NDT * _CAP + 4 * 4096)
    assert ARW <= 49600, ARW
    AR = sbuf("AR", [128, ARW], BF16)
    XST = sbuf("XST", [128, 2 * 1024], F32)
    XNF = sbuf("XNF", [128, 2 * 1024], F32)
    XNLO = sbuf("XNLO", [128, 1024], BF16)
    TRASH = sbuf("TRASH", [128, 1024], BF16)
    LNST = sbuf("LNST", [128, 16], F32)
    MASK8 = sbuf("MASK8", [8, T], BF16)
    XNB = sbuf("XNB", [128, 2 * 1024], BF16)
    OFFS = sbuf("OFFS", [128, 16], U32)
    HUGES = sbuf("HUGES", [128, 32], U32)
    STG = sbuf("STG", [128, 8], U32)
    WQT = sbuf("WQT", [128, 4 * 512], BF16)
    WRHI = sbuf("WRHI", [128, NDT * 8], BF16)
    WRLO = sbuf("WRLO", [128, NDT * 8], BF16)
    WRF = sbuf("WRF", [128, NDT * 8], F32)
    WRT1 = sbuf("WRT1", [128, NDT * 8], F32)
    BRHI = sbuf("BRHI", [1, 8], BF16)
    BRLO = sbuf("BRLO", [1, 8], BF16)
    BRF = sbuf("BRF", [1, 8], F32)
    BRT1 = sbuf("BRT1", [1, 8], F32)
    BVC = sbuf("BVC", [128, NDT], F32)
    BOC = sbuf("BOC", [128, NDT], F32)
    B1C = sbuf("B1C", [128, E * NFT], F32)
    B2C = sbuf("B2C", [128, E * NDT], F32)
    ONESF = sbuf("ONESF", [1, 128], F32)
    EPS1 = sbuf("EPS1", [128, 1], F32)
    EPS2 = sbuf("EPS2", [128, 1], F32)
    CONST = {n: sbuf("k_" + n, sh, dt) for n, (sh, dt) in CONST_SPECS.items()}

    PF = es.enter_context(nc.psum_tensor("PF", [128, 3072], F32))
    PB = es.enter_context(nc.psum_tensor("PB", [128, 2048], BF16))

    def pf(bank, width=512, rows=128):
        return PF[0:rows, bank * 512: bank * 512 + width]

    def pbf(half, rows=128, width=128):
        return PB[0:rows, half * 1024: half * 1024 + width]

    def ar(off, width, rows=128, r0=0):
        return AR[r0:r0 + rows, off:off + width]

    PADQ_OFF = 0
    PADK_OFF = 4 * T
    VT_OFF = 0
    VAUG_OFF = 8 * T
    WT_OFF = VAUG_OFF + KT * 1040
    SD_OFF = WT_OFF + 8192
    HS_OFF = 0
    CPW = (CAP // 128) * 1024
    XGN_OFF = 32 * CAP
    XGT_OFF = XGN_OFF + CPW
    YT_OFF = XGT_OFF + NDT * CAP
    YN_OFF = YT_OFF + NDT * CAP
    W1OFF = YN_OFF + CPW
    W2OFF = W1OFF + 2 * 4096
    assert W2OFF + 2 * 4096 <= ARW, (W2OFF, ARW)


    plan = Plan()
    sems = {}

    def S(name):
        if name not in sems:
            sems[name] = Sem(nc, es, name)
        return sems[name]

    sPE, sACT, sDVE = S("sPE"), S("sACT"), S("sDVE")
    s_cst = S("s_cst")
    s_cstg = S("s_cstg")
    s_cv = [S("s_cv0"), S("s_cv1")]
    s_x = [S("s_x0"), S("s_x1")]
    s_wt = [S("s_wt0"), S("s_wt1")]
    s_w1 = [S("s_w1a"), S("s_w1b")]
    s_w2 = [S("s_w2a"), S("s_w2b")]
    s_o = [S("s_o0"), S("s_o1")]
    s_wts = S("s_wts")
    s_xn = [S("s_xn0"), S("s_xn1")]
    s_stg = S("s_stg")
    s_l1 = S("s_l1")
    s_g2 = [S("s_g2a")]
    s_st2 = [S("s_st2a")]
    s_l3 = [S("s_l3a")]
    s_yc = S("s_yc")
    s_il = S("s_il")
    s_xg = [S("s_xg0"), S("s_xg1")]
    s_yp = [S("s_yp0"), S("s_yp1")]
    s_p1 = S("s_p1")
    s_p3 = S("s_p3")

    NOBAR = {"s_cv0", "s_cv1", "s_sv0", "s_sv1",
             "s_stg", "s_l1", "s_g2a", "s_g2b", "s_st2a", "s_st2b",
             "s_l3a", "s_l3b", "s_il", "s_p1", "s_p3", "s_yc"}

    def bar():
        plan.barrier([s for n, s in sems.items() if n not in NOBAR])

    def dma(q, dst, src, sem, waits=()):
        plan.add(q, lambda e, a=dst, b=src: e.dma_start(a, b), waits=waits,
                 inc=(sem, 16))
        return sem.at()

    def mm(lhsT, rhs, out, start, stop, waits=(), inc=False, tile_position=None):
        kw = {}
        if tile_position is not None:
            kw["tile_position"] = tile_position
        plan.add("tensor",
                 lambda e, a=out, l=lhsT, r=rhs, s1=start, s2=stop, k=dict(kw):
                 e.matmul(a, l, r, start=s1, stop=s2, **k),
                 waits=waits, inc=(sPE, 1) if inc else None)
        return sPE.at()

    def tr(out, in_, idn, waits=(), inc=True):
        plan.add("tensor", lambda e, a=out, b=in_, c=idn: e.transpose(a, b, c),
                 waits=waits, inc=(sPE, 1) if inc else None)
        return sPE.at()

    def act(out, in_, func, scale=1.0, bias=None, accum=None, waits=(),
            inc=True, drain=False):
        def f(e, a=out, b=in_, fn=func, sc=scale, bi=bias, ac=accum):
            kw = {"scale": sc}
            if bi is not None:
                kw["bias"] = bi
            if ac is not None:
                kw["accum_out"] = ac
            return e.activation(a, b, fn, **kw)
        plan.add("scalar", f, waits=waits, inc=(sACT, 1) if inc else None,
                 drain=drain)
        return sACT.at()

    def vop(fn, waits=(), inc=True, drain=False):
        plan.add("vector", fn, waits=waits, inc=(sDVE, 1) if inc else None,
                 drain=drain)
        return sDVE.at()

    for n, (sh, dt) in CONST_SPECS.items():
        q = "sync" if dt == F32 else "gpsimd"
        dma(q, CONST[n][:, :], cin[n][:, :], s_cst if q == "sync" else s_cstg)
    dma("sync", WRF[:, :], wr_t[:, :], s_cst)
    dma("sync", BRF[:, :], br_t[:, :], s_cst)
    dma("sync", BVC[:, :], bvc_t[:, :], s_cst)
    dma("sync", BOC[:, :], boc_t[:, :], s_cst)
    dma("sync", B1C[:, :], b1c_t[:, :], s_cst)
    dma("sync", B2C[:, :], b2c_t[:, :], s_cst)
    bar()
    vop(lambda e: e.memset(HUGES[:, :], 0x0FFFFFFF))
    vop(lambda e: e.memset(XNB[:, :], 0.0))
    bar()
    vop(lambda e: e.memset(ONESF[:, :], 1.0))
    vop(lambda e: e.memset(EPS1[:, :], EPS_LN))
    vop(lambda e: e.memset(EPS2[:, :], EPS_NRM))
    act(WRHI[:, :], WRF[:, :], AF.Copy)
    act(BRHI[:, :], BRF[:, :], AF.Copy)
    bar()
    vop(lambda e: e.tensor_copy(WRT1[:, :], WRHI[:, :]))
    vop(lambda e: e.tensor_copy(BRT1[:, :], BRHI[:, :]), drain=True)
    vop(lambda e: e.tensor_sub(WRT1[:, :], WRF[:, :], WRT1[:, :]), drain=True)
    vop(lambda e: e.tensor_sub(BRT1[:, :], BRF[:, :], BRT1[:, :]), drain=True)
    bar()
    act(WRLO[:, :], WRT1[:, :], AF.Copy)
    act(BRLO[:, :], BRT1[:, :], AF.Copy)
    bar()

    for pi, pn in enumerate(("w1q", "w2q", "w1k", "w2k")):
        w = dma("gpsimd", TRASH[0:64, 0:1024], wq[pn][:, :], s_cstg)
        for dt_ in range(NDT):
            tr(pbf(dt_ % 2, width=64), TRASH[0:64, dt_ * 128:(dt_ + 1) * 128],
               CONST["idn_bf"][0:64, 0:64], waits=[w, sACT.at()])
            act(WQT[:, pi * 512 + dt_ * 64: pi * 512 + (dt_ + 1) * 64],
                pbf(dt_ % 2, width=64), AF.Copy, waits=[sPE.at()])
        bar()


    for wsrc, wdst in ((wv_t, wvt_s), (wo_t, wot_s)):
        for jt in range(NDT):
            dma("gpsimd", ar(VT_OFF + jt * 1024, 1024),
                wsrc[jt * 128:(jt + 1) * 128, :], s_cstg)
        ldw = s_cstg.at()
        for dt_ in range(NDT):
            for jt in range(NDT):
                k = dt_ * NDT + jt
                tr(pbf(k % 2), ar(VT_OFF + jt * 1024 + dt_ * 128, 128),
                   CONST["idn_bf"][:, :], waits=[ldw, sACT.at()])
                act(ar(WT_OFF + dt_ * 1024 + jt * 128, 128), pbf(k % 2), AF.Copy,
                    waits=[sPE.at()])
        bar()
        dma("sync", wdst[:, :], ar(WT_OFF, NDT * 1024), s_wts)
        bar()


    w1conv_done = {}
    w2conv_done = {}
    _last = [None, None]
    _k = 0
    for ei in range(E):
        for (wsrc_t, wdst_t, rows, cols, dones) in (
                (w1_t, w1bf, D, FF, w1conv_done),
                (w2_t, w2bf, FF, D, w2conv_done)):
            rows_per = (8192 // cols) * 128
            for ch in range(rows // rows_per):
                bi = _k % 2
                _k += 1
                r0 = ch * rows_per
                w = [_last[bi]] if _last[bi] is not None else []
                _last[bi] = dma("gpsimd", wdst_t[ei, r0:r0 + rows_per, :],
                                wsrc_t[ei, r0:r0 + rows_per, :],
                                s_cv[bi], waits=w)
            dones[ei] = [m for m in _last if m is not None]
    conv_staging_free = []


    NCH = T // 128
    NCH2 = 8 * CAP // 128

    def lall_fn(g):
        g.sem_inc(s_yc.h, 1)
        with g.register("gb") as rb, g.register("gi") as ri, \
             g.register("gw") as rw, g.register("gj") as rj, \
             g.register("gr") as rr:
            with g.Fori(0, NB) as b_:
                g.reg_mov(rb, b_)
                g.reg_add(rw, rb, 1)
                g.wait_ge(s_p1.h, rw)
                with g.Fori(0, NCH) as i_:
                    g.reg_mov(ri, i_)
                    g.reg_mul(rj, rb, NCH)
                    g.reg_add(rj, rj, ri)
                    g.reg_mul(rw, rj, 16)
                    g.wait_ge(s_l1.h, rw)
                    g.reg_mul(rw, rj, 128)
                    g.dma_start(STG[:, 0:1],
                                bass.AP(offs_d, rw, [[1, 128]])
                                ).then_inc(s_stg.h, 16)
                    g.dma_start(STG[:, 1:2],
                                bass.AP(iotat_t, rw, [[1, 128]])
                                ).then_inc(s_stg.h, 16)
                    g.reg_mul(rw, rj, 32)
                    g.reg_add(rw, rw, 32)
                    g.wait_ge(s_stg.h, rw)
                    g.indirect_dma_start(
                        lists_d[:, :],
                        IndirectOffsetOnAxis(ap=STG[:, 0:1], axis=0),
                        STG[:, 1:2], None,
                        bounds_check=NB * 8 * CAP - 1,
                        oob_is_err=False).then_inc(s_l1.h, 16)
                g.reg_add(rw, rb, 1)
                g.reg_mul(rw, rw, 16 * NCH)
                g.wait_ge(s_l1.h, rw)
                with g.Fori(0, NCH2) as i_:
                    g.reg_mov(ri, i_)
                    g.reg_mul(rj, rb, NCH2)
                    g.reg_add(rj, rj, ri)
                    g.reg_mul(rw, rj, 16)
                    g.wait_ge(s_st2[0].h, rw)
                    g.reg_mul(rw, rj, 128)
                    g.dma_start(STG[:, 2:3],
                                bass.AP(lists_d, rw, [[1, 128]])
                                ).then_inc(s_g2[0].h, 16)
                    g.reg_mul(rw, rj, 32)
                    g.reg_add(rw, rw, 16)
                    g.wait_ge(s_g2[0].h, rw)
                    g.indirect_dma_start(
                        XST[:, 0:1024], None,
                        xnat_d[:, :],
                        IndirectOffsetOnAxis(ap=STG[:, 2:3], axis=0),
                        bounds_check=NB * T - 1,
                        oob_is_err=False).then_inc(s_g2[0].h, 16)
                    g.reg_mul(rw, rj, 32)
                    g.reg_add(rw, rw, 32)
                    g.wait_ge(s_g2[0].h, rw)
                    g.reg_mul(rr, rj, 128 * 1024)
                    g.dma_start(
                        bass.AP(xperm_d, rr, [[1024, 128], [1, 1024]]),
                        XST[:, 0:1024],
                    ).then_inc(s_st2[0].h, 16)
                g.reg_add(rw, rb, 1)
                g.wait_ge(s_p3.h, rw)
                with g.Fori(0, NCH) as i_:
                    g.reg_mov(ri, i_)
                    g.reg_mul(rj, rb, NCH)
                    g.reg_add(rj, rj, ri)
                    g.reg_add(rw, rj, 1)
                    g.wait_ge(s_yc.h, rw)
                    g.reg_mul(rw, rj, 32)
                    g.wait_ge(s_l3[0].h, rw)
                    g.reg_mul(rw, rj, 128)
                    g.dma_start(
                        STG[:, 4:5],
                        bass.AP(offs_d, rw, [[1, 128]]),
                    ).then_inc(s_l3[0].h, 16)
                    g.reg_mul(rw, rj, 32)
                    g.reg_add(rw, rw, 16)
                    g.wait_ge(s_l3[0].h, rw)
                    g.indirect_dma_start(
                        XNF[:, 0:1024], None,
                        yperm_d[:, :],
                        IndirectOffsetOnAxis(ap=STG[:, 4:5], axis=0),
                        bounds_check=NB * 8 * CAP - 1,
                        oob_is_err=False).then_inc(s_l3[0].h, 16)
    plan.add("gpsimd", lall_fn)
    s_stg.n += 32 * NB * NCH
    s_l1.n += 16 * NB * NCH
    s_g2[0].n += 32 * NB * NCH2
    s_st2[0].n += 16 * NB * NCH2
    s_yc.n += 1
    s_l3[0].n += 32 * NB * NCH

    def pace(sem, waits):
        plan.add("vector", lambda e, s=sem: e.sem_inc(s.h, 1), waits=waits)
        sem.n += 1

    xst_free = [None, None]
    xnb_free = [None, None]
    pR = PF[0:128, 4 * 512: 4 * 512 + 8]
    pR_free = None

    for b in range(NB):
        f32bank_free = [None] * 4
        pbf_free = [None, None]
        for tt in range(NCHUNK):
            bi = tt % 2
            xst = XST[:, bi * 1024:(bi + 1) * 1024]
            xnf = XNF[:, bi * 1024:(bi + 1) * 1024]
            ln = LNST[:, bi * 8:(bi + 1) * 8]
            s1, s2c, mcol, tmpc, varc, rstd = (ln[:, i:i + 1] for i in range(6))
            w = [xst_free[bi]] if xst_free[bi] is not None else []
            ldx = dma("sync", xst,
                      x_in[b * T + tt * 128: b * T + (tt + 1) * 128, :],
                      s_x[bi], waits=w)
            act(TRASH[:, 0:1024], xst, AF.Copy, accum=s1, waits=[ldx], drain=True)
            a_s2 = act(TRASH[:, 0:1024], xst, AF.Square, accum=s2c, drain=True)
            vop(lambda e, a=mcol, b_=s1: e.tensor_scalar_mul(a, b_, 1.0 / D),
                waits=[a_s2], drain=True)
            vop(lambda e, a=tmpc, b_=mcol: e.tensor_mul(a, b_, b_), drain=True)
            v_var = vop(lambda e, a=varc, b_=s2c, c=tmpc:
                        e.scalar_tensor_tensor(a, b_, 1.0 / D, c, ALU.mult,
                                               ALU.subtract), drain=True)
            act(varc, varc, AF.Ln, bias=EPS1[:, 0:1], waits=[v_var], drain=True)
            a_rstd = act(rstd, varc, AF.Exp, scale=-0.5, drain=True)
            v_xnf = vop(lambda e, a=xnf, b_=xst, m_=mcol, r_=rstd:
                        e.tensor_scalar(a, b_, m_, r_, ALU.subtract, ALU.mult),
                        waits=[a_rstd], drain=True)
            xnb = XNB[:, bi * 1024:(bi + 1) * 1024]
            w = [v_xnf]
            if xnb_free[bi] is not None:
                w.append(xnb_free[bi])
            a_xnb = act(xnb, xnf, AF.Copy, waits=w)
            xnb_free[bi] = dma("sync",
                               xnat_d[b * T + tt * 128: b * T + (tt + 1) * 128, :],
                               xnb, s_xn[bi], waits=[a_xnb])
            xn_marks = []
            lo_marks = []
            for k in range(2 * NDT):
                isxn = k < NDT
                dt_ = k % NDT
                bank = k % 4
                src = xnf if isxn else xst
                wt = [v_xnf] if isxn else [ldx]
                if f32bank_free[bank] is not None:
                    wt += f32bank_free[bank]
                tr(pf(bank, 128), src[:, dt_ * 128:(dt_ + 1) * 128],
                   CONST["idn_f"][:, :], waits=wt)
                spe_t = sPE.at()
                dst = (XN if isxn else ACC)[:, dt_ * T + tt * 128:
                                            dt_ * T + (tt + 1) * 128]
                a_ev = act(dst, pf(bank, 128), AF.Copy, waits=[spe_t])
                if isxn:
                    v_lo = vop(lambda e, a=XNLO[:, dt_ * 128:(dt_ + 1) * 128],
                               p_=pf(bank, 128), x_=dst: e.tensor_sub(a, p_, x_),
                               waits=[a_ev], drain=True)
                    f32bank_free[bank] = [a_ev, v_lo]
                    xn_marks.append(a_ev)
                    lo_marks.append(v_lo)
                else:
                    f32bank_free[bank] = [a_ev]
            xn_ready = xn_marks[-1]
            lo_ready = lo_marks[-1]
            xst_free[bi] = sPE.at()
            rw = [xn_ready]
            if pR_free is not None:
                rw.append(pR_free)
            mm(CONST["onescol"][0:1, :], BRHI[0:1, :], pR, True, False, waits=rw)
            mm(CONST["onescol"][0:1, :], BRLO[0:1, :], pR, False, False)
            for dt_ in range(NDT):
                xnhi = XN[:, dt_ * T + tt * 128: dt_ * T + (tt + 1) * 128]
                xnlo = XNLO[:, dt_ * 128:(dt_ + 1) * 128]
                last = dt_ == NDT - 1
                mm(xnhi, WRHI[:, dt_ * 8:(dt_ + 1) * 8], pR, False, False)
                mm(xnhi, WRLO[:, dt_ * 8:(dt_ + 1) * 8], pR, False, False)
                spe_r = mm(xnlo, WRHI[:, dt_ * 8:(dt_ + 1) * 8], pR, False, last,
                           waits=[lo_ready] if dt_ == 0 else (), inc=last)
            v_mx = vop(lambda e, a=tmpc, p_=pR: e.reduce_max(a, p_, axis=AX.X),
                       waits=[spe_r], drain=True)
            v_msk = vop(lambda e, a=TRASH[0:128, 0:8], p_=pR, m_=tmpc:
                        e.tensor_single_scalar(a, p_, m_, ALU.is_ge), drain=True)
            pR_free = v_msk
            wtr = [v_msk]
            if pbf_free[0] is not None:
                wtr.append(pbf_free[0])
            tr(pbf(0, rows=8), TRASH[0:128, 0:8], CONST["idn_bf"][:, :], waits=wtr)
            a_m8 = act(MASK8[0:8, tt * 128:(tt + 1) * 128], pbf(0, rows=8),
                       AF.Copy, waits=[sPE.at()])
            pbf_free[0] = a_m8
        bar()

        v_z8 = vop(lambda e: e.memset(XST[0:8, 0:T], 0.0))
        P8 = XNF[0:8, 0:T]
        v_scan = vop(lambda e, a=P8, z=XST[0:8, 0:T], m_=MASK8[0:8, 0:T]:
                     e.tensor_tensor_scan(a, z, m_, 0.0, ALU.add, ALU.add),
                     waits=[v_z8], drain=True)
        v_mf = vop(lambda e: e.tensor_copy(XST[0:8, 0:T], MASK8[0:8, 0:T]),
                   waits=[v_scan], drain=True)
        for tc in range(NQC):
            mm(CONST["ones8"][0:8, 2:3], XST[0:8, tc * 512:(tc + 1) * 512],
               PF[0:1, tc * 512:(tc + 1) * 512], True, False,
               waits=[v_mf] if tc == 0 else (), inc=tc == NQC - 1)
        v_mp = vop(lambda e, a=XST[0:8, 0:T], p_=P8:
                   e.tensor_mul(a, a, p_), waits=[sPE.at()], drain=True)
        for tc in range(NQC):
            mm(CONST["ones8"][0:8, 0:1], XST[0:8, tc * 512:(tc + 1) * 512],
               PF[0:1, tc * 512:(tc + 1) * 512], False, True,
               waits=[v_mp] if tc == 0 else (), inc=tc == NQC - 1)
        v_ps = vop(lambda e: e.tensor_copy(XST[64:65, 0:T], PF[0:1, 0:T]),
                   waits=[sPE.at()])
        v_off = vop(lambda e, c=float(1 - b * 8 * CAP):
                    e.tensor_scalar_sub(XST[64:65, 0:T], XST[64:65, 0:T], c),
                    waits=[v_ps], drain=True)
        offb_free = [None, None]
        off_marks = []
        for tt in range(NCHUNK):
            k = tt % 2
            wts = [v_off]
            if offb_free[k] is not None:
                wts.append(offb_free[k])
            tr(pf(4 + k, width=1), XST[64:65, tt * 128:(tt + 1) * 128],
               CONST["idn_f"][64:65, 64:65], waits=wts)
            m_ = vop(lambda e, a=OFFS[:, tt:tt + 1], p_=pf(4 + k, width=1):
                     e.tensor_copy(a, p_), waits=[sPE.at()])
            offb_free[k] = m_
            off_marks.append(m_)
        offs_ready = off_marks[-1]
        for tt in range(NCHUNK):
            od1 = dma("sync", offs_d[b * NCHUNK + tt, :], OFFS[:, tt:tt + 1],
                      s_wts,
                      waits=[off_marks[tt]])
        il = dma("sync", lists_d[b * 8 * CAP:(b + 1) * 8 * CAP, 0].rearrange(
            "(p c) -> p c", p=128), HUGES[:, 0:8 * CAP // 128], s_il)
        pace(s_p1, [od1, il, (s_xn[0], s_xn[0].n), (s_xn[1], s_xn[1].n)])

        wvt_l = dma("sync", ar(WT_OFF, NDT * 1024), wvt_s[:, :], s_wt[0])
        vbank_free = [None, None]
        for jt in range(NDT):
            for tc in range(NQC):
                bank = (jt * NQC + tc) % 2
                for dt_ in range(NDT):
                    wts = []
                    if dt_ == 0:
                        wts.append(wvt_l)
                        if vbank_free[bank] is not None:
                            wts.append(vbank_free[bank])
                    mm(ar(WT_OFF + dt_ * 1024 + jt * 128, 128),
                       XN[:, dt_ * T + tc * 512: dt_ * T + (tc + 1) * 512],
                       pf(bank), dt_ == 0, dt_ == NDT - 1, waits=wts,
                       inc=dt_ == NDT - 1)
                vbank_free[bank] = act(
                    ar(VT_OFF + jt * T + tc * 512, 512), pf(bank), AF.Identity,
                    bias=BVC[:, jt:jt + 1], waits=[sPE.at()])
        bar()
        v_ones = vop(lambda e: e.memset(
            bass.AP(AR, VAUG_OFF + 64, [[ARW, 128], [1040, KT], [65, 16]]), 1.0),
            waits=conv_staging_free)
        pbf_free = [None, None]
        for kt in range(KT):
            for jt in range(NDT):
                k = kt * NDT + jt
                wts = [v_ones] if k == 0 else []
                if pbf_free[k % 2] is not None:
                    wts.append(pbf_free[k % 2])
                tr(pbf(k % 2), ar(VT_OFF + jt * T + kt * 128, 128),
                   CONST["idn_bf"][:, :], waits=wts)
                spe_t = sPE.at()
                pbf_free[k % 2] = vop(
                    lambda e, p_=pbf(k % 2),
                    o=bass.AP(AR, VAUG_OFF + kt * 1040 + 2 * jt * 65,
                              [[ARW, 128], [65, 2], [1, 64]]):
                    e.tensor_copy(o, p_), waits=[spe_t])
        bar()

        PT_OFF = WT_OFF
        pbank_free = [None, None]
        for pi in range(4):
            for tc in range(NQC):
                bank = (pi * NQC + tc) % 2
                for dt_ in range(NDT):
                    wts = []
                    if dt_ == 0 and pbank_free[bank] is not None:
                        wts.append(pbank_free[bank])
                    mm(WQT[:, pi * 512 + dt_ * 64: pi * 512 + (dt_ + 1) * 64],
                       XN[:, dt_ * T + tc * 512: dt_ * T + (tc + 1) * 512],
                       pf(bank, rows=64), dt_ == 0, dt_ == NDT - 1, waits=wts,
                       inc=dt_ == NDT - 1)
                pbank_free[bank] = act(
                    AR[0:64, PT_OFF + pi * T + tc * 512:
                       PT_OFF + pi * T + (tc + 1) * 512],
                    pf(bank, rows=64), AF.Copy, waits=[sPE.at()])
        bar()
        T1_OFF = SD_OFF
        LPRE_OFF = SD_OFF + 2048
        SQ_OFF = LPRE_OFF + 512
        INV_OFF = SQ_OFF + 512
        BINV_OFF = INV_OFF + 512
        selbank_free = [None] * 4
        iter_gate = None
        nbank_free = None
        padbank_free = [None, None]
        binvbank_free = [None, None]
        binvsb_free = [None, None]
        last_pad_use = None
        for side in range(2):
            pad_off = PADQ_OFF if side == 0 else PADK_OFF
            si = CONST["sqi"] if side == 0 else CONST["ski"]
            sj = CONST["sqj"] if side == 0 else CONST["skj"]
            p1 = PT_OFF + (0 if side == 0 else 2 * T)
            p2 = p1 + T
            for tc in range(NQC):
                sels = [(si, p1), (sj, p2), (sj, p1), (si, p2)]
                ev_marks = []
                for k, (smat, poff) in enumerate(sels):
                    wts = []
                    if k == 0 and iter_gate is not None:
                        wts += iter_gate
                    if selbank_free[k] is not None:
                        wts.append(selbank_free[k])
                    mm(smat[:, :], AR[0:64, poff + tc * 512: poff + (tc + 1) * 512],
                       pf(k, rows=96), True, True, waits=wts, inc=True)
                    ev = act(AR[0:96, T1_OFF + k * 512: T1_OFF + (k + 1) * 512],
                             pf(k, rows=96), AF.Copy, waits=[sPE.at()])
                    selbank_free[k] = ev
                    ev_marks.append(ev)
                t_ = [AR[0:96, T1_OFF + k * 512: T1_OFF + (k + 1) * 512]
                      for k in range(4)]
                lpre = AR[0:96, LPRE_OFF:LPRE_OFF + 512]
                sq = AR[0:96, SQ_OFF:SQ_OFF + 512]
                vop(lambda e, a=lpre, x=t_[0], y=t_[1]: e.tensor_mul(a, x, y),
                    waits=[ev_marks[-1]] + ([last_pad_use] if last_pad_use else []),
                    drain=True)
                vop(lambda e, a=sq, x=t_[2], y=t_[3]: e.tensor_mul(a, x, y),
                    drain=True)
                v_lp = vop(lambda e, a=lpre, x=lpre, y=sq: e.tensor_sub(a, x, y),
                           drain=True)
                a_sq = act(sq, lpre, AF.Square, waits=[v_lp], drain=True)
                wts = [a_sq]
                if nbank_free is not None:
                    wts.append(nbank_free)
                mm(CONST["o96"][:, :], sq, pf(4, rows=16), True, True,
                   waits=wts, inc=True)
                inv = AR[0:16, INV_OFF:INV_OFF + 512]
                act(inv, pf(4, rows=16), AF.Ln, bias=EPS2[0:16, 0:1],
                    waits=[sPE.at()], drain=True)
                inv_ready = act(inv, inv, AF.Exp, scale=-0.5, drain=True)
                nbank_free = inv_ready
                for g in range(4):
                    bank = g % 2
                    wts = [v_lp]
                    if padbank_free[bank] is not None:
                        wts.append(padbank_free[bank])
                    mm(CONST["pg"][:, g * 128:(g + 1) * 128], lpre,
                       pf(bank), True, True, waits=wts, inc=True)
                    spe_pad = sPE.at()
                    wts = [inv_ready]
                    if binvbank_free[bank] is not None:
                        wts.append(binvbank_free[bank])
                    mm(CONST["s16"][0:16, g * 128:(g + 1) * 128], inv,
                       pf(2 + bank), True, True, waits=wts, inc=True)
                    binv = AR[0:128, BINV_OFF + bank * 512:
                              BINV_OFF + (bank + 1) * 512]
                    wts = [sPE.at()]
                    if binvsb_free[bank] is not None:
                        wts.append(binvsb_free[bank])
                    a_bi = act(binv, pf(2 + bank), AF.Copy, waits=wts)
                    binvbank_free[bank] = a_bi
                    dst = AR[0:128, pad_off + g * T + tc * 512:
                             pad_off + g * T + (tc + 1) * 512]
                    if side == 0:
                        v_c = vop(lambda e, a=dst, p_=pf(bank), y=binv:
                                  e.tensor_mul(a, p_, y),
                                  waits=[a_bi, spe_pad], drain=True)
                    else:
                        v_c = vop(lambda e, a=dst, p_=pf(bank), y=binv:
                                  e.scalar_tensor_tensor(
                                      a, p_, CONST["sgnpad"][:, 0:1], y,
                                      ALU.mult, ALU.mult),
                                  waits=[a_bi, spe_pad], drain=True)
                    padbank_free[bank] = v_c
                    binvsb_free[bank] = v_c
                last_pad_use = v_c
                iter_gate = [sACT.at(), sDVE.at()]
        bar()

        wot_l = dma("sync", ar(WT_OFF, NDT * 1024), wot_s[:, :], s_wt[1])
        AT_OFF = SD_OFF
        OF_OFF = SD_OFF + 2048
        sc_free = {0: [], 1: []}
        at_free = {0: None, 1: None}
        po_gate = []
        pending_evac = None
        ofull_last = None
        recb_free = None
        proj_free = [None, None]
        for qc in range(NQC):
            nkt = 4 * qc + 4
            ofull_last = None
            for hp in range(H // 2):
                av_last = None
                ready = {}

                def emit_sc(kt):
                    par = kt % 2
                    cb = 2 * par
                    for hi in range(2):
                        h = 2 * hp + hi
                        g, j = divmod(h, 4)
                        base = 32 * j
                        tpos = (96, 0) if base == 96 else None
                        lq = AR[base:base + 6,
                                PADK_OFF + g * T + kt * 128:
                                PADK_OFF + g * T + kt * 128 + 128]
                        rq = AR[base:base + 6,
                                PADQ_OFF + g * T + qc * 512:
                                PADQ_OFF + g * T + (qc + 1) * 512]
                        wts = list(sc_free[par]) if hi == 0 else []
                        mm(lq, rq, pf(cb + hi), True, True, waits=wts,
                           inc=hi == 1, tile_position=tpos)
                    spe_sc = sPE.at()
                    atdst = AR[0:128, AT_OFF + par * 1024:
                               AT_OFF + par * 1024 + 1024]
                    wts = [spe_sc]
                    if at_free[par] is not None:
                        wts.append(at_free[par])
                    a_exp = act(atdst, PF[0:128, cb * 512: cb * 512 + 1024],
                                AF.Exp, scale=SCALE, waits=wts)
                    sc_free[par] = [a_exp]
                    rdy = a_exp
                    if kt >= 4 * qc:
                        d_ = kt - 4 * qc
                        mk = bass.AP(CONST["masks4"], d_ * 512,
                                     [[2048, 128], [0, 2], [1, 512]])
                        rdy = vop(lambda e, a=atdst, m_=mk:
                                  e.tensor_mul(a, a, m_), waits=[a_exp])
                    ready[kt] = rdy

                def emit_av(kt):
                    nonlocal av_last
                    par = kt % 2
                    for hi in range(2):
                        h = 2 * hp + hi
                        va = ar(VAUG_OFF + kt * 1040 + h * 65, 65)
                        at_ = AR[0:128, AT_OFF + par * 1024 + hi * 512:
                                 AT_OFF + par * 1024 + (hi + 1) * 512]
                        wts = []
                        if hi == 0:
                            wts.append(ready[kt])
                            if kt == 0:
                                wts += po_gate
                        av_last = mm(va, at_, pf(4 + hi, rows=65),
                                     kt == 0, kt == nkt - 1,
                                     waits=wts, inc=hi == 1)
                    at_free[par] = av_last

                emit_sc(0)
                emit_sc(1)
                if pending_evac is not None:
                    pending_evac()
                    pending_evac = None
                emit_av(0)
                for kt in range(2, nkt):
                    emit_sc(kt)
                    emit_av(kt - 1)
                emit_av(nkt - 1)

                def make_evac(av_mark, hp_):
                    def run_evac():
                        nonlocal recb_free, po_gate, ofull_last
                        po_marks = []
                        for hi in range(2):
                            h = 2 * hp_ + hi
                            dt_, half = divmod(h, 2)
                            rec = XNF[0:1, hi * 512: hi * 512 + 512]
                            wts = [av_mark]
                            if recb_free is not None:
                                wts.append(recb_free)
                            act(rec,
                                PF[64:65, (4 + hi) * 512:(4 + hi) * 512 + 512],
                                AF.Ln, waits=wts, drain=True)
                            v_rec = act(rec, rec, AF.Exp, scale=-1.0,
                                        drain=True)
                            a_of = vop(
                                lambda e,
                                a=AR[half * 64: half * 64 + 64,
                                     OF_OFF + dt_ * 512:OF_OFF + (dt_ + 1) * 512],
                                p_=PF[0:64, (4 + hi) * 512:(4 + hi) * 512 + 512]:
                                e.tensor_copy(a, p_), waits=[av_mark])
                            mm(ONESF[0:1, 0:64], rec, pf(4 + hi, rows=64),
                               True, True, waits=[v_rec, a_of], inc=True)
                            v_on = vop(lambda e,
                                       a=AR[half * 64: half * 64 + 64,
                                            OF_OFF + dt_ * 512:
                                            OF_OFF + (dt_ + 1) * 512],
                                       p_=pf(4 + hi, rows=64):
                                       e.tensor_mul(a, a, p_),
                                       waits=[sPE.at(), a_of])
                            po_marks.append(v_on)
                        recb_free = sPE.at()
                        po_gate = po_marks
                        ofull_last = po_marks[-1]
                    return run_evac

                pending_evac = make_evac(av_last, hp)
            pending_evac()
            pending_evac = None
            for jt in range(NDT):
                bank = jt % 2
                for dt_ in range(NDT):
                    wts = []
                    if dt_ == 0:
                        wts.append(ofull_last)
                        if jt == 0:
                            wts.append(wot_l)
                            wts += sc_free[0] + sc_free[1]
                        if proj_free[bank] is not None:
                            wts.append(proj_free[bank])
                    mm(ar(WT_OFF + dt_ * 1024 + jt * 128, 128),
                       AR[0:128, OF_OFF + dt_ * 512: OF_OFF + (dt_ + 1) * 512],
                       pf(bank), dt_ == 0, dt_ == NDT - 1, waits=wts,
                       inc=dt_ == NDT - 1)
                proj_free[bank] = vop(
                    lambda e, a=ACC[:, jt * T + qc * 512: jt * T + (qc + 1) * 512],
                    p_=pf(bank), bc=BOC[:, jt:jt + 1]:
                    e.scalar_tensor_tensor(a, p_, bc, a, ALU.add, ALU.add),
                    waits=[sPE.at()], drain=True)
            sc_free[0] = [m for m in proj_free if m is not None]
        bar()

        xg_free = None
        xgt_free = None
        yt_free = [None] * NDT
        yn_free = None
        pbfa = [None, None]
        w1_free = [None, None]
        w2_free = [None, None]
        y_free = {2: None, 3: None}
        for ei in range(E):
            nm = (ei + 1) * (CAP // 128)
            wts = [(s_st2[0], 16 * (b * NCH2 + nm))]
            if xg_free is not None:
                wts.append(xg_free)
            xg_l = dma("sync", ar(XGN_OFF, CPW),
                       xperm_d[b * 8 * CAP + ei * CAP:
                               b * 8 * CAP + (ei + 1) * CAP, :].rearrange(
                           "(c p) f -> p c f", p=128),
                       s_xg[ei % 2], waits=wts)
            tm = []
            for ck in range(CAP // 128):
                for dt_ in range(NDT):
                    k = (ck * NDT + dt_) % 2
                    wts = [xg_l] if (ck == 0 and dt_ == 0) else []
                    if pbfa[k] is not None:
                        wts.append(pbfa[k])
                    if ck == 0 and dt_ == 0 and xgt_free is not None:
                        wts.append(xgt_free)
                    tr(pbf(k), ar(XGN_OFF + ck * 1024 + dt_ * 128, 128),
                       CONST["idn_bf"][:, :], waits=wts)
                    pbfa[k] = vop(lambda e,
                                  a=ar(XGT_OFF + dt_ * CAP + ck * 128, 128),
                                  p_=pbf(k): e.tensor_copy(a, p_),
                                  waits=[sPE.at()])
                    tm.append(pbfa[k])
            xgt_ready = tm[-1]
            xg_free = sPE.at()
            gelu_done = {}
            w1_l = None
            for ft in range(NFT):
                sbi = (ft // 4) % 2
                if ft % 4 == 0:
                    w = [w1_free[sbi]] if w1_free[sbi] is not None else []
                    if b == 0:
                        w = w + w1conv_done[ei]
                    w1_l = dma("sync", ar(W1OFF + sbi * 4096, 4096),
                               w1bf[ei, :, ft * 128:(ft + 4) * 128].rearrange(
                                   "(t p) f -> p t f", p=128),
                               s_w1[sbi], waits=w)
                bank = ft % 2
                for dt_ in range(NDT):
                    wts = []
                    if dt_ == 0:
                        wts.append(w1_l)
                        if ft >= 2:
                            wts.append(gelu_done[ft - 2])
                        if ft == 0:
                            wts.append(xgt_ready)
                    mm(ar(W1OFF + sbi * 4096 + dt_ * 512 + (ft % 4) * 128, 128),
                       ar(XGT_OFF + dt_ * CAP, CAP),
                       pf(bank, width=CAP), dt_ == 0, dt_ == NDT - 1, waits=wts,
                       inc=dt_ == NDT - 1)
                gelu_done[ft] = act(
                    AR[0:128, ft * CAP: ft * CAP + CAP], pf(bank, width=CAP),
                    AF.Gelu, bias=B1C[:, ei * NFT + ft: ei * NFT + ft + 1],
                    waits=[sPE.at()])
                if ft % 4 == 3:
                    w1_free[(ft // 4) % 2] = sPE.at()
            h_ready = gelu_done[NFT - 1]
            for ht in range(NDT):
                sbi = ht % 2
                w = [w2_free[sbi]] if w2_free[sbi] is not None else []
                if b == 0:
                    w = w + w2conv_done[ei]
                w2_l = dma("scalar", ar(W2OFF + sbi * 4096, 4096),
                           w2bf[ei, :, ht * 128:(ht + 1) * 128].rearrange(
                               "(t p) h -> p t h", p=128),
                           s_w2[sbi], waits=w)
                bank = 2 + ht % 2
                for ft in range(NFT):
                    wts = []
                    if ft == 0:
                        wts = [w2_l, h_ready]
                        if y_free[bank] is not None:
                            wts.append(y_free[bank])
                    mm(ar(W2OFF + sbi * 4096 + ft * 128, 128),
                       AR[0:128, ft * CAP: ft * CAP + CAP],
                       pf(bank, width=CAP), ft == 0, ft == NFT - 1, waits=wts,
                       inc=ft == NFT - 1)
                w2_free[sbi] = sPE.at()
                wts = [sPE.at()]
                if yt_free[ht] is not None:
                    wts.append(yt_free[ht])
                y_free[bank] = act(
                    ar(YT_OFF + ht * CAP, CAP), pf(bank, width=CAP),
                    AF.Identity, bias=B2C[:, ei * NDT + ht: ei * NDT + ht + 1],
                    waits=wts)
                yt_free[ht] = y_free[bank]
            tm = []
            for ck in range(CAP // 128):
                for ht in range(NDT):
                    k = (ck * NDT + ht) % 2
                    wts = [yt_free[ht]] if ck == 0 else []
                    if pbfa[k] is not None:
                        wts.append(pbfa[k])
                    if ck == 0 and ht == 0 and yn_free is not None:
                        wts.append(yn_free)
                    tr(pbf(k), ar(YT_OFF + ht * CAP + ck * 128, 128),
                       CONST["idn_bf"][:, :], waits=wts)
                    pbfa[k] = vop(lambda e,
                                  a=ar(YN_OFF + ck * 1024 + ht * 128, 128),
                                  p_=pbf(k): e.tensor_copy(a, p_),
                                  waits=[sPE.at()])
                    tm.append(pbfa[k])
            yn_free = dma("sync",
                          yperm_d[b * 8 * CAP + ei * CAP:
                                  b * 8 * CAP + (ei + 1) * CAP, :].rearrange(
                              "(c p) f -> p c f", p=128),
                          ar(YN_OFF, CPW), s_yp[ei % 2], waits=[tm[-1]])
            for ht in range(NDT):
                yt_free[ht] = sPE.at()
        yperm_done = [s_yp[0].at(), s_yp[1].at()]
        bar()

        pace(s_p3, list(yperm_done) + [sACT.at(), sDVE.at(), sPE.at()])

        ost_free = [None, None]
        pbf_free = [None, None]
        for tt in range(NCHUNK):
            bi = tt % 2
            ost = XST[:, bi * 1024:(bi + 1) * 1024]
            a_last = None
            for jt in range(NDT):
                k = (tt * NDT + jt) % 2
                wts = []
                if ost_free[bi] is not None and jt == 0:
                    wts.append(ost_free[bi])
                if pbf_free[k] is not None:
                    wts.append(pbf_free[k])
                tr(pbf(k), ACC[:, jt * T + tt * 128: jt * T + (tt + 1) * 128],
                   CONST["idn_bf"][:, :], waits=wts)
                a_last = act(ost[:, jt * 128:(jt + 1) * 128], pbf(k), AF.Copy,
                             waits=[sPE.at()])
                pbf_free[k] = a_last
            plan.add("vector",
                     lambda e, a=ost, y_=XNF[:, 0:1024]:
                     e.tensor_add(a, a, y_),
                     waits=[a_last, (s_l3[0], 32 * (b * NCHUNK + tt) + 32)],
                     inc=(s_yc, 1))
            v_add = s_yc.at()
            st = dma("sync", out_t[b * T + tt * 128: b * T + (tt + 1) * 128, :],
                     ost, s_o[bi], waits=[v_add])
            ost_free[bi] = st
        bar()

    plan.add("sync", None, waits=[(s, s.n) for s in sems.values()])

    with nc.Block() as blk:
        plan.emit(blk)
    es.close()
    return nc


def prep_inputs(L0_attn_x, L0_attn_W1_q, L0_attn_W2_q, L0_attn_W1_k, L0_attn_W2_k,
                L0_attn_W_v, L0_attn_b_v, L0_attn_W_out, L0_attn_b_out,
                L0_ffn_wr, L0_ffn_br, L0_ffn_w1, L0_ffn_b1, L0_ffn_w2, L0_ffn_b2,
                T=2048, ncores=8):
    f = lambda a: np.ascontiguousarray(np.asarray(a, dtype=np.float32))
    consts = make_consts(T)
    base = {
        "w1q": f(L0_attn_W1_q), "w2q": f(L0_attn_W2_q),
        "w1k": f(L0_attn_W1_k), "w2k": f(L0_attn_W2_k),
        "wv": f(L0_attn_W_v), "wo": f(L0_attn_W_out),
        "wr": f(np.asarray(L0_ffn_wr, np.float32).reshape(NDT, 128, 8)
                .transpose(1, 0, 2).reshape(128, NDT * 8)),
        "br": f(np.asarray(L0_ffn_br, np.float32).reshape(1, 8)),
        "w1": f(L0_ffn_w1), "w2": f(L0_ffn_w2),
        "bvc": f(np.asarray(L0_attn_b_v, np.float32).reshape(NDT, 128).T),
        "boc": f(np.asarray(L0_attn_b_out, np.float32).reshape(NDT, 128).T),
        "b1c": f(np.asarray(L0_ffn_b1, np.float32).reshape(E, NFT, 128)
                 .transpose(2, 0, 1).reshape(128, E * NFT)),
        "b2c": f(np.asarray(L0_ffn_b2, np.float32).reshape(E, NDT, 128)
                 .transpose(2, 0, 1).reshape(128, E * NDT)),
    }
    for n in CONST_SPECS:
        base["c_" + n] = np.ascontiguousarray(consts[n])
    x = np.asarray(L0_attn_x, np.float32)
    in_maps = []
    bpc = x.shape[0] // ncores
    iot = (np.arange(bpc * T // 128, dtype=np.uint32)[:, None] * 128
           + np.arange(128, dtype=np.uint32)[None, :])
    base["iotat"] = np.ascontiguousarray(iot)
    for c in range(ncores):
        m = dict(base)
        m["x"] = np.ascontiguousarray(x[c * bpc:(c + 1) * bpc].reshape(bpc * T, D))
        in_maps.append(m)
    return in_maps


def kernel(**inputs):
    T, NB, ncores = 2048, 4, 8
    in_maps = prep_inputs(**inputs, T=T, ncores=ncores)
    nc = build_program(T=T, NB=NB)
    res = run_bass_kernel_spmd(nc, in_maps, core_ids=list(range(ncores)))
    out = np.concatenate(
        [np.asarray(r["out"]).reshape(NB, T, D) for r in res.results], axis=0)
    return out.astype(np.float32)


# version 13
# speedup vs baseline: 1.0101x; 1.0101x over previous
import contextlib
import numpy as np
import ml_dtypes
import concourse.bass as bass
import concourse.mybir as mybir
from concourse.bass_utils import run_bass_kernel_spmd
from concourse.bass_core_rust import IndirectOffsetOnAxis

F32 = mybir.dt.float32
BF16 = mybir.dt.bfloat16
U32 = mybir.dt.uint32
AF = mybir.ActivationFunctionType
ALU = mybir.AluOpType
AX = mybir.AxisListType

PAIRS = [(0, 1), (0, 2), (0, 3), (1, 2), (1, 3), (2, 3)]
SGN6 = [1.0, -1.0, 1.0, 1.0, -1.0, 1.0]
SCALE = 6.0 ** -0.5
D = 1024
H = 16
E = 8
FF = 4096
NDT = D // 128
NFT = FF // 128
EPS_LN = 1e-5
EPS_NRM = 1e-24


class Sem:
    def __init__(self, nc, es, name):
        self.h = es.enter_context(nc.semaphore(name))
        self.n = 0
        self.name = name
        import os
        if os.environ.get("SEM_DEBUG"):
            print("SEM", self.h.num, name)

    def at(self):
        return (self, self.n)


class Plan:
    ENGINES = ("sync", "gpsimd", "tensor", "scalar", "vector")

    def __init__(self):
        self.ops = {e: [] for e in self.ENGINES}

    def add(self, eng, fn, waits=(), inc=None, drain=False):
        if inc is not None:
            inc[0].n += inc[1]
        ws = []
        for s, v in waits:
            if v > 0:
                ws.append((s, v))
        self.ops[eng].append((fn, ws, inc, drain))

    def barrier(self, sems):
        marks = [(s, s.n) for s in sems if s.n > 0]
        for e in self.ENGINES:
            self.add(e, None, waits=marks)

    def emit(self, blk):
        def mk(name):
            def run(eng):
                for fn, waits, inc, drain in self.ops[name]:
                    if drain:
                        eng.drain()
                    for s, v in waits:
                        eng.wait_ge(s.h, v)
                    if fn is None:
                        continue
                    instr = fn(eng)
                    if inc is not None:
                        instr.then_inc(inc[0].h, inc[1])
            return run
        blk.sync(mk("sync"))
        blk.gpsimd(mk("gpsimd"))
        blk.tensor(mk("tensor"))
        blk.scalar(mk("scalar"))
        blk.vector(mk("vector"))


def make_consts(T):
    c = {}
    c["idn_f"] = np.eye(128, dtype=np.float32)
    c["idn_bf"] = np.eye(128, dtype=np.float32)
    m4 = np.zeros((128, 4 * 512), np.float32)
    for d_ in range(4):
        delta = d_ * 128
        r = np.arange(128)[:, None]
        q = np.arange(512)[None, :]
        m4[:, d_ * 512:(d_ + 1) * 512] = (r + delta <= q).astype(np.float32)
    c["masks4"] = m4
    sqi = np.zeros((64, 96), np.float32)
    sqj = np.zeros((64, 96), np.float32)
    ski = np.zeros((64, 96), np.float32)
    skj = np.zeros((64, 96), np.float32)
    for h in range(H):
        for cc in range(6):
            i, j = PAIRS[cc]
            ir, jr = PAIRS[5 - cc]
            sqi[4 * h + i, 6 * h + cc] = 1
            sqj[4 * h + j, 6 * h + cc] = 1
            ski[4 * h + ir, 6 * h + cc] = 1
            skj[4 * h + jr, 6 * h + cc] = 1
    c["sqi"], c["sqj"], c["ski"], c["skj"] = sqi, sqj, ski, skj
    o96 = np.zeros((96, 16), np.float32)
    for h in range(H):
        o96[6 * h:6 * h + 6, h] = 1
    c["o96"] = o96
    pg = np.zeros((96, 4 * 128), np.float32)
    s16 = np.zeros((16, 4 * 128), np.float32)
    for g in range(4):
        for j in range(4):
            for cc in range(6):
                pg[6 * (4 * g + j) + cc, g * 128 + 32 * j + cc] = 1
                s16[4 * g + j, g * 128 + 32 * j + cc] = 1
    c["pg"], c["s16"] = pg, s16
    ones8 = np.zeros((8, 3), np.float32)
    ones8[:, 0] = 1
    ones8[:, 1] = np.arange(8)
    ones8[:, 2] = np.arange(8) * (T // 4)
    c["ones8"] = ones8
    c["onescol"] = np.ones((1, 128), np.float32)
    sgn = np.ones((128, 1), np.float32)
    for j in range(4):
        for cc in range(6):
            sgn[32 * j + cc, 0] = SGN6[cc]
    c["sgnpad"] = sgn
    return c


CONST_SPECS = {
    "idn_f": ([128, 128], F32), "idn_bf": ([128, 128], BF16),
    "masks4": ([128, 2048], BF16),
    "sqi": ([64, 96], BF16), "sqj": ([64, 96], BF16),
    "ski": ([64, 96], BF16), "skj": ([64, 96], BF16),
    "o96": ([96, 16], BF16), "pg": ([96, 512], BF16), "s16": ([16, 512], BF16),
    "ones8": ([8, 3], F32),
    "onescol": ([1, 128], BF16), "sgnpad": ([128, 1], F32),
}


def build_program(T=2048, NB=4):
    KT = T // 128
    NQC = T // 512
    NTH = max(1, T // 1024)
    TH = T // NTH
    NTCQ = TH // 512
    NCHUNK = T // 128

    nc = bass.Bass("TRN2", target_bir_lowering=False)

    x_in = nc.dram_tensor("x", [NB * T, D], F32, kind="ExternalInput")
    out_t = nc.dram_tensor("out", [NB * T, D], F32, kind="ExternalOutput")
    wq = {p: nc.dram_tensor(p, [64, D], F32, kind="ExternalInput")
          for p in ("w1q", "w2q", "w1k", "w2k")}
    wv_t = nc.dram_tensor("wv", [D, D], F32, kind="ExternalInput")
    wo_t = nc.dram_tensor("wo", [D, D], F32, kind="ExternalInput")
    wr_t = nc.dram_tensor("wr", [128, NDT * 8], F32, kind="ExternalInput")
    br_t = nc.dram_tensor("br", [1, 8], F32, kind="ExternalInput")
    w1_t = nc.dram_tensor("w1", [E, D, FF], F32, kind="ExternalInput")
    w2_t = nc.dram_tensor("w2", [E, FF, D], F32, kind="ExternalInput")
    bvc_t = nc.dram_tensor("bvc", [128, NDT], F32, kind="ExternalInput")
    boc_t = nc.dram_tensor("boc", [128, NDT], F32, kind="ExternalInput")
    b1c_t = nc.dram_tensor("b1c", [128, E * NFT], F32, kind="ExternalInput")
    b2c_t = nc.dram_tensor("b2c", [128, E * NDT], F32, kind="ExternalInput")
    cin = {n: nc.dram_tensor("c_" + n, [sh[0], sh[1]], F32, kind="ExternalInput")
           for n, (sh, dt) in CONST_SPECS.items()}
    w1bf = nc.dram_tensor("w1bf", [E, D, FF], BF16)
    w2bf = nc.dram_tensor("w2bf", [E, FF, D], BF16)
    wvt_s = nc.dram_tensor("wvt_s", [128, NDT * 1024], BF16)
    wot_s = nc.dram_tensor("wot_s", [128, NDT * 1024], BF16)
    CAP = T // 4
    xnat_d = nc.dram_tensor("xnat_d", [NB * T, D], BF16)
    xperm_d = nc.dram_tensor("xperm_d", [NB * 8 * CAP, D], BF16)
    yperm_d = nc.dram_tensor("yperm_d", [NB * 8 * CAP, D], BF16)
    lists_d = nc.dram_tensor("lists_d", [NB * 8 * CAP, 1], U32)
    offs_d = nc.dram_tensor("offs_d", [NB * (T // 128), 128], U32)
    iotat_t = nc.dram_tensor("iotat", [NB * (T // 128), 128], U32,
                             kind="ExternalInput")

    es = contextlib.ExitStack()

    def sbuf(name, shape, dtype):
        return es.enter_context(nc.sbuf_tensor(name, shape, dtype))

    XN = sbuf("XN", [128, NDT * T], BF16)
    ACC = sbuf("ACC", [128, NDT * T], BF16)
    _CAP = T // 4
    _CPW = (_CAP // 128) * 1024
    ARW = max(8 * T + KT * 1040 + 8192 + 6656,
              32 * _CAP + 2 * _CPW + 2 * NDT * _CAP + 4 * 4096)
    assert ARW <= 49600, ARW
    AR = sbuf("AR", [128, ARW], BF16)
    XST = sbuf("XST", [128, 2 * 1024], F32)
    XNF = sbuf("XNF", [128, 2 * 1024], F32)
    XNLO = sbuf("XNLO", [128, 1024], BF16)
    TRASH = sbuf("TRASH", [128, 1024], BF16)
    LNST = sbuf("LNST", [128, 16], F32)
    MASK8 = sbuf("MASK8", [8, T], BF16)
    XNB = sbuf("XNB", [128, 2 * 1024], BF16)
    OFFS = sbuf("OFFS", [128, 16], U32)
    HUGES = sbuf("HUGES", [128, 32], U32)
    STG = sbuf("STG", [128, 8], U32)
    WQT = sbuf("WQT", [128, 4 * 512], BF16)
    WRHI = sbuf("WRHI", [128, NDT * 8], BF16)
    WRLO = sbuf("WRLO", [128, NDT * 8], BF16)
    WRF = sbuf("WRF", [128, NDT * 8], F32)
    WRT1 = sbuf("WRT1", [128, NDT * 8], F32)
    BRHI = sbuf("BRHI", [1, 8], BF16)
    BRLO = sbuf("BRLO", [1, 8], BF16)
    BRF = sbuf("BRF", [1, 8], F32)
    BRT1 = sbuf("BRT1", [1, 8], F32)
    BVC = sbuf("BVC", [128, NDT], F32)
    BOC = sbuf("BOC", [128, NDT], F32)
    B1C = sbuf("B1C", [128, E * NFT], F32)
    B2C = sbuf("B2C", [128, E * NDT], F32)
    ONESF = sbuf("ONESF", [1, 128], F32)
    EPS1 = sbuf("EPS1", [128, 1], F32)
    EPS2 = sbuf("EPS2", [128, 1], F32)
    CONST = {n: sbuf("k_" + n, sh, dt) for n, (sh, dt) in CONST_SPECS.items()}

    PF = es.enter_context(nc.psum_tensor("PF", [128, 3072], F32))
    PB = es.enter_context(nc.psum_tensor("PB", [128, 2048], BF16))

    def pf(bank, width=512, rows=128):
        return PF[0:rows, bank * 512: bank * 512 + width]

    def pbf(half, rows=128, width=128):
        return PB[0:rows, half * 1024: half * 1024 + width]

    def ar(off, width, rows=128, r0=0):
        return AR[r0:r0 + rows, off:off + width]

    PADQ_OFF = 0
    PADK_OFF = 4 * T
    VT_OFF = 0
    VAUG_OFF = 8 * T
    WT_OFF = VAUG_OFF + KT * 1040
    SD_OFF = WT_OFF + 8192
    HS_OFF = 0
    CPW = (CAP // 128) * 1024
    XGN_OFF = 32 * CAP
    XGT_OFF = XGN_OFF + CPW
    YT_OFF = XGT_OFF + NDT * CAP
    YN_OFF = YT_OFF + NDT * CAP
    W1OFF = YN_OFF + CPW
    W2OFF = W1OFF + 2 * 4096
    assert W2OFF + 2 * 4096 <= ARW, (W2OFF, ARW)


    plan = Plan()
    sems = {}

    def S(name):
        if name not in sems:
            sems[name] = Sem(nc, es, name)
        return sems[name]

    sPE, sACT, sDVE = S("sPE"), S("sACT"), S("sDVE")
    s_cst = S("s_cst")
    s_cstg = S("s_cstg")
    s_cv = [S("s_cv0"), S("s_cv1")]
    s_x = [S("s_x0"), S("s_x1")]
    s_wt = [S("s_wt0"), S("s_wt1")]
    s_w1 = [S("s_w1a"), S("s_w1b")]
    s_w2 = [S("s_w2a"), S("s_w2b")]
    s_o = [S("s_o0"), S("s_o1")]
    s_wts = S("s_wts")
    s_xn = [S("s_xn0"), S("s_xn1")]
    s_stg = S("s_stg")
    s_l1 = S("s_l1")
    s_g2 = [S("s_g2a")]
    s_st2 = [S("s_st2a")]
    s_l3 = [S("s_l3a")]
    s_yc = S("s_yc")
    s_il = S("s_il")
    s_xg = [S("s_xg0"), S("s_xg1")]
    s_yp = [S("s_yp0"), S("s_yp1")]
    s_p1 = S("s_p1")
    s_p3 = S("s_p3")

    NOBAR = {"s_cv0", "s_cv1", "s_sv0", "s_sv1",
             "s_stg", "s_l1", "s_g2a", "s_g2b", "s_st2a", "s_st2b",
             "s_l3a", "s_l3b", "s_il", "s_p1", "s_p3", "s_yc"}

    def bar():
        plan.barrier([s for n, s in sems.items() if n not in NOBAR])

    def dma(q, dst, src, sem, waits=()):
        plan.add(q, lambda e, a=dst, b=src: e.dma_start(a, b), waits=waits,
                 inc=(sem, 16))
        return sem.at()

    def mm(lhsT, rhs, out, start, stop, waits=(), inc=False, tile_position=None):
        kw = {}
        if tile_position is not None:
            kw["tile_position"] = tile_position
        plan.add("tensor",
                 lambda e, a=out, l=lhsT, r=rhs, s1=start, s2=stop, k=dict(kw):
                 e.matmul(a, l, r, start=s1, stop=s2, **k),
                 waits=waits, inc=(sPE, 1) if inc else None)
        return sPE.at()

    def tr(out, in_, idn, waits=(), inc=True):
        plan.add("tensor", lambda e, a=out, b=in_, c=idn: e.transpose(a, b, c),
                 waits=waits, inc=(sPE, 1) if inc else None)
        return sPE.at()

    def act(out, in_, func, scale=1.0, bias=None, accum=None, waits=(),
            inc=True, drain=False):
        def f(e, a=out, b=in_, fn=func, sc=scale, bi=bias, ac=accum):
            kw = {"scale": sc}
            if bi is not None:
                kw["bias"] = bi
            if ac is not None:
                kw["accum_out"] = ac
            return e.activation(a, b, fn, **kw)
        plan.add("scalar", f, waits=waits, inc=(sACT, 1) if inc else None,
                 drain=drain)
        return sACT.at()

    def vop(fn, waits=(), inc=True, drain=False):
        plan.add("vector", fn, waits=waits, inc=(sDVE, 1) if inc else None,
                 drain=drain)
        return sDVE.at()

    for n, (sh, dt) in CONST_SPECS.items():
        q = "sync" if dt == F32 else "gpsimd"
        dma(q, CONST[n][:, :], cin[n][:, :], s_cst if q == "sync" else s_cstg)
    dma("sync", WRF[:, :], wr_t[:, :], s_cst)
    dma("sync", BRF[:, :], br_t[:, :], s_cst)
    dma("sync", BVC[:, :], bvc_t[:, :], s_cst)
    dma("sync", BOC[:, :], boc_t[:, :], s_cst)
    dma("sync", B1C[:, :], b1c_t[:, :], s_cst)
    dma("sync", B2C[:, :], b2c_t[:, :], s_cst)
    bar()
    vop(lambda e: e.memset(HUGES[:, :], 0x0FFFFFFF))
    vop(lambda e: e.memset(XNB[:, :], 0.0))
    bar()
    vop(lambda e: e.memset(ONESF[:, :], 1.0))
    vop(lambda e: e.memset(EPS1[:, :], EPS_LN))
    vop(lambda e: e.memset(EPS2[:, :], EPS_NRM))
    act(WRHI[:, :], WRF[:, :], AF.Copy)
    act(BRHI[:, :], BRF[:, :], AF.Copy)
    bar()
    vop(lambda e: e.tensor_copy(WRT1[:, :], WRHI[:, :]))
    vop(lambda e: e.tensor_copy(BRT1[:, :], BRHI[:, :]), drain=True)
    vop(lambda e: e.tensor_sub(WRT1[:, :], WRF[:, :], WRT1[:, :]), drain=True)
    vop(lambda e: e.tensor_sub(BRT1[:, :], BRF[:, :], BRT1[:, :]), drain=True)
    bar()
    act(WRLO[:, :], WRT1[:, :], AF.Copy)
    act(BRLO[:, :], BRT1[:, :], AF.Copy)
    bar()

    for pi, pn in enumerate(("w1q", "w2q", "w1k", "w2k")):
        w = dma("gpsimd", TRASH[0:64, 0:1024], wq[pn][:, :], s_cstg)
        for dt_ in range(NDT):
            tr(pbf(dt_ % 2, width=64), TRASH[0:64, dt_ * 128:(dt_ + 1) * 128],
               CONST["idn_bf"][0:64, 0:64], waits=[w, sACT.at()])
            act(WQT[:, pi * 512 + dt_ * 64: pi * 512 + (dt_ + 1) * 64],
                pbf(dt_ % 2, width=64), AF.Copy, waits=[sPE.at()])
        bar()


    for wsrc, wdst in ((wv_t, wvt_s), (wo_t, wot_s)):
        for jt in range(NDT):
            dma("gpsimd", ar(VT_OFF + jt * 1024, 1024),
                wsrc[jt * 128:(jt + 1) * 128, :], s_cstg)
        ldw = s_cstg.at()
        for dt_ in range(NDT):
            for jt in range(NDT):
                k = dt_ * NDT + jt
                tr(pbf(k % 2), ar(VT_OFF + jt * 1024 + dt_ * 128, 128),
                   CONST["idn_bf"][:, :], waits=[ldw, sACT.at()])
                act(ar(WT_OFF + dt_ * 1024 + jt * 128, 128), pbf(k % 2), AF.Copy,
                    waits=[sPE.at()])
        bar()
        dma("sync", wdst[:, :], ar(WT_OFF, NDT * 1024), s_wts)
        bar()


    w1conv_done = {}
    w2conv_done = {}
    _gate = [(s_xn[0], (NCHUNK // 2) * 16), (s_xn[1], (NCHUNK // 2) * 16)]
    _last = [None, None]
    _k = 0
    for ei in range(E):
        for (wsrc_t, wdst_t, rows, cols, dones) in (
                (w1_t, w1bf, D, FF, w1conv_done),
                (w2_t, w2bf, FF, D, w2conv_done)):
            rows_per = (8192 // cols) * 128
            for ch in range(rows // rows_per):
                bi = _k % 2
                _k += 1
                r0 = ch * rows_per
                w = [_last[bi]] if _last[bi] is not None else list(_gate)
                _last[bi] = dma("gpsimd", wdst_t[ei, r0:r0 + rows_per, :],
                                wsrc_t[ei, r0:r0 + rows_per, :],
                                s_cv[bi], waits=w)
            dones[ei] = [m for m in _last if m is not None]
    conv_staging_free = []


    NCH = T // 128
    NCH2 = 8 * CAP // 128

    def lall_fn(g):
        g.sem_inc(s_yc.h, 1)
        with g.register("gb") as rb, g.register("gi") as ri, \
             g.register("gw") as rw, g.register("gj") as rj, \
             g.register("gr") as rr:
            with g.Fori(0, NB) as b_:
                g.reg_mov(rb, b_)
                g.reg_add(rw, rb, 1)
                g.wait_ge(s_p1.h, rw)
                with g.Fori(0, NCH) as i_:
                    g.reg_mov(ri, i_)
                    g.reg_mul(rj, rb, NCH)
                    g.reg_add(rj, rj, ri)
                    g.reg_mul(rw, rj, 16)
                    g.wait_ge(s_l1.h, rw)
                    g.reg_mul(rw, rj, 128)
                    g.dma_start(STG[:, 0:1],
                                bass.AP(offs_d, rw, [[1, 128]])
                                ).then_inc(s_stg.h, 16)
                    g.dma_start(STG[:, 1:2],
                                bass.AP(iotat_t, rw, [[1, 128]])
                                ).then_inc(s_stg.h, 16)
                    g.reg_mul(rw, rj, 32)
                    g.reg_add(rw, rw, 32)
                    g.wait_ge(s_stg.h, rw)
                    g.indirect_dma_start(
                        lists_d[:, :],
                        IndirectOffsetOnAxis(ap=STG[:, 0:1], axis=0),
                        STG[:, 1:2], None,
                        bounds_check=NB * 8 * CAP - 1,
                        oob_is_err=False).then_inc(s_l1.h, 16)
                g.reg_add(rw, rb, 1)
                g.reg_mul(rw, rw, 16 * NCH)
                g.wait_ge(s_l1.h, rw)
                with g.Fori(0, NCH2) as i_:
                    g.reg_mov(ri, i_)
                    g.reg_mul(rj, rb, NCH2)
                    g.reg_add(rj, rj, ri)
                    g.reg_mul(rw, rj, 16)
                    g.wait_ge(s_st2[0].h, rw)
                    g.reg_mul(rw, rj, 128)
                    g.dma_start(STG[:, 2:3],
                                bass.AP(lists_d, rw, [[1, 128]])
                                ).then_inc(s_g2[0].h, 16)
                    g.reg_mul(rw, rj, 32)
                    g.reg_add(rw, rw, 16)
                    g.wait_ge(s_g2[0].h, rw)
                    g.indirect_dma_start(
                        XST[:, 0:1024], None,
                        xnat_d[:, :],
                        IndirectOffsetOnAxis(ap=STG[:, 2:3], axis=0),
                        bounds_check=NB * T - 1,
                        oob_is_err=False).then_inc(s_g2[0].h, 16)
                    g.reg_mul(rw, rj, 32)
                    g.reg_add(rw, rw, 32)
                    g.wait_ge(s_g2[0].h, rw)
                    g.reg_mul(rr, rj, 128 * 1024)
                    g.dma_start(
                        bass.AP(xperm_d, rr, [[1024, 128], [1, 1024]]),
                        XST[:, 0:1024],
                    ).then_inc(s_st2[0].h, 16)
                g.reg_add(rw, rb, 1)
                g.wait_ge(s_p3.h, rw)
                with g.Fori(0, NCH) as i_:
                    g.reg_mov(ri, i_)
                    g.reg_mul(rj, rb, NCH)
                    g.reg_add(rj, rj, ri)
                    g.reg_add(rw, rj, 1)
                    g.wait_ge(s_yc.h, rw)
                    g.reg_mul(rw, rj, 32)
                    g.wait_ge(s_l3[0].h, rw)
                    g.reg_mul(rw, rj, 128)
                    g.dma_start(
                        STG[:, 4:5],
                        bass.AP(offs_d, rw, [[1, 128]]),
                    ).then_inc(s_l3[0].h, 16)
                    g.reg_mul(rw, rj, 32)
                    g.reg_add(rw, rw, 16)
                    g.wait_ge(s_l3[0].h, rw)
                    g.indirect_dma_start(
                        XNF[:, 0:1024], None,
                        yperm_d[:, :],
                        IndirectOffsetOnAxis(ap=STG[:, 4:5], axis=0),
                        bounds_check=NB * 8 * CAP - 1,
                        oob_is_err=False).then_inc(s_l3[0].h, 16)
    plan.add("gpsimd", lall_fn)
    s_stg.n += 32 * NB * NCH
    s_l1.n += 16 * NB * NCH
    s_g2[0].n += 32 * NB * NCH2
    s_st2[0].n += 16 * NB * NCH2
    s_yc.n += 1
    s_l3[0].n += 32 * NB * NCH

    def pace(sem, waits):
        plan.add("vector", lambda e, s=sem: e.sem_inc(s.h, 1), waits=waits)
        sem.n += 1

    xst_free = [None, None]
    xnb_free = [None, None]
    pR = PF[0:128, 4 * 512: 4 * 512 + 8]
    pR_free = None

    for b in range(NB):
        f32bank_free = [None] * 4
        pbf_free = [None, None]
        for tt in range(NCHUNK):
            bi = tt % 2
            xst = XST[:, bi * 1024:(bi + 1) * 1024]
            xnf = XNF[:, bi * 1024:(bi + 1) * 1024]
            ln = LNST[:, bi * 8:(bi + 1) * 8]
            s1, s2c, mcol, tmpc, varc, rstd = (ln[:, i:i + 1] for i in range(6))
            w = [xst_free[bi]] if xst_free[bi] is not None else []
            ldx = dma("sync", xst,
                      x_in[b * T + tt * 128: b * T + (tt + 1) * 128, :],
                      s_x[bi], waits=w)
            act(TRASH[:, 0:1024], xst, AF.Copy, accum=s1, waits=[ldx], drain=True)
            a_s2 = act(TRASH[:, 0:1024], xst, AF.Square, accum=s2c, drain=True)
            vop(lambda e, a=mcol, b_=s1: e.tensor_scalar_mul(a, b_, 1.0 / D),
                waits=[a_s2], drain=True)
            vop(lambda e, a=tmpc, b_=mcol: e.tensor_mul(a, b_, b_), drain=True)
            v_var = vop(lambda e, a=varc, b_=s2c, c=tmpc:
                        e.scalar_tensor_tensor(a, b_, 1.0 / D, c, ALU.mult,
                                               ALU.subtract), drain=True)
            act(varc, varc, AF.Ln, bias=EPS1[:, 0:1], waits=[v_var], drain=True)
            a_rstd = act(rstd, varc, AF.Exp, scale=-0.5, drain=True)
            v_xnf = vop(lambda e, a=xnf, b_=xst, m_=mcol, r_=rstd:
                        e.tensor_scalar(a, b_, m_, r_, ALU.subtract, ALU.mult),
                        waits=[a_rstd], drain=True)
            xnb = XNB[:, bi * 1024:(bi + 1) * 1024]
            w = [v_xnf]
            if xnb_free[bi] is not None:
                w.append(xnb_free[bi])
            a_xnb = act(xnb, xnf, AF.Copy, waits=w)
            xnb_free[bi] = dma("sync",
                               xnat_d[b * T + tt * 128: b * T + (tt + 1) * 128, :],
                               xnb, s_xn[bi], waits=[a_xnb])
            xn_marks = []
            lo_marks = []
            for k in range(2 * NDT):
                isxn = k < NDT
                dt_ = k % NDT
                bank = k % 4
                src = xnf if isxn else xst
                wt = [v_xnf] if isxn else [ldx]
                if f32bank_free[bank] is not None:
                    wt += f32bank_free[bank]
                tr(pf(bank, 128), src[:, dt_ * 128:(dt_ + 1) * 128],
                   CONST["idn_f"][:, :], waits=wt)
                spe_t = sPE.at()
                dst = (XN if isxn else ACC)[:, dt_ * T + tt * 128:
                                            dt_ * T + (tt + 1) * 128]
                a_ev = act(dst, pf(bank, 128), AF.Copy, waits=[spe_t])
                if isxn:
                    v_lo = vop(lambda e, a=XNLO[:, dt_ * 128:(dt_ + 1) * 128],
                               p_=pf(bank, 128), x_=dst: e.tensor_sub(a, p_, x_),
                               waits=[a_ev], drain=True)
                    f32bank_free[bank] = [a_ev, v_lo]
                    xn_marks.append(a_ev)
                    lo_marks.append(v_lo)
                else:
                    f32bank_free[bank] = [a_ev]
            xn_ready = xn_marks[-1]
            lo_ready = lo_marks[-1]
            xst_free[bi] = sPE.at()
            rw = [xn_ready]
            if pR_free is not None:
                rw.append(pR_free)
            mm(CONST["onescol"][0:1, :], BRHI[0:1, :], pR, True, False, waits=rw)
            mm(CONST["onescol"][0:1, :], BRLO[0:1, :], pR, False, False)
            for dt_ in range(NDT):
                xnhi = XN[:, dt_ * T + tt * 128: dt_ * T + (tt + 1) * 128]
                xnlo = XNLO[:, dt_ * 128:(dt_ + 1) * 128]
                last = dt_ == NDT - 1
                mm(xnhi, WRHI[:, dt_ * 8:(dt_ + 1) * 8], pR, False, False)
                mm(xnhi, WRLO[:, dt_ * 8:(dt_ + 1) * 8], pR, False, False)
                spe_r = mm(xnlo, WRHI[:, dt_ * 8:(dt_ + 1) * 8], pR, False, last,
                           waits=[lo_ready] if dt_ == 0 else (), inc=last)
            v_mx = vop(lambda e, a=tmpc, p_=pR: e.reduce_max(a, p_, axis=AX.X),
                       waits=[spe_r], drain=True)
            v_msk = vop(lambda e, a=TRASH[0:128, 0:8], p_=pR, m_=tmpc:
                        e.tensor_single_scalar(a, p_, m_, ALU.is_ge), drain=True)
            pR_free = v_msk
            wtr = [v_msk]
            if pbf_free[0] is not None:
                wtr.append(pbf_free[0])
            tr(pbf(0, rows=8), TRASH[0:128, 0:8], CONST["idn_bf"][:, :], waits=wtr)
            a_m8 = act(MASK8[0:8, tt * 128:(tt + 1) * 128], pbf(0, rows=8),
                       AF.Copy, waits=[sPE.at()])
            pbf_free[0] = a_m8
        bar()

        v_z8 = vop(lambda e: e.memset(XST[0:8, 0:T], 0.0))
        P8 = XNF[0:8, 0:T]
        v_scan = vop(lambda e, a=P8, z=XST[0:8, 0:T], m_=MASK8[0:8, 0:T]:
                     e.tensor_tensor_scan(a, z, m_, 0.0, ALU.add, ALU.add),
                     waits=[v_z8], drain=True)
        v_mf = vop(lambda e: e.tensor_copy(XST[0:8, 0:T], MASK8[0:8, 0:T]),
                   waits=[v_scan], drain=True)
        for tc in range(NQC):
            mm(CONST["ones8"][0:8, 2:3], XST[0:8, tc * 512:(tc + 1) * 512],
               PF[0:1, tc * 512:(tc + 1) * 512], True, False,
               waits=[v_mf] if tc == 0 else (), inc=tc == NQC - 1)
        v_mp = vop(lambda e, a=XST[0:8, 0:T], p_=P8:
                   e.tensor_mul(a, a, p_), waits=[sPE.at()], drain=True)
        for tc in range(NQC):
            mm(CONST["ones8"][0:8, 0:1], XST[0:8, tc * 512:(tc + 1) * 512],
               PF[0:1, tc * 512:(tc + 1) * 512], False, True,
               waits=[v_mp] if tc == 0 else (), inc=tc == NQC - 1)
        v_ps = vop(lambda e: e.tensor_copy(XST[64:65, 0:T], PF[0:1, 0:T]),
                   waits=[sPE.at()])
        v_off = vop(lambda e, c=float(1 - b * 8 * CAP):
                    e.tensor_scalar_sub(XST[64:65, 0:T], XST[64:65, 0:T], c),
                    waits=[v_ps], drain=True)
        offb_free = [None, None]
        off_marks = []
        for tt in range(NCHUNK):
            k = tt % 2
            wts = [v_off]
            if offb_free[k] is not None:
                wts.append(offb_free[k])
            tr(pf(4 + k, width=1), XST[64:65, tt * 128:(tt + 1) * 128],
               CONST["idn_f"][64:65, 64:65], waits=wts)
            m_ = vop(lambda e, a=OFFS[:, tt:tt + 1], p_=pf(4 + k, width=1):
                     e.tensor_copy(a, p_), waits=[sPE.at()])
            offb_free[k] = m_
            off_marks.append(m_)
        offs_ready = off_marks[-1]
        for tt in range(NCHUNK):
            od1 = dma("sync", offs_d[b * NCHUNK + tt, :], OFFS[:, tt:tt + 1],
                      s_wts,
                      waits=[off_marks[tt]])
        il = dma("sync", lists_d[b * 8 * CAP:(b + 1) * 8 * CAP, 0].rearrange(
            "(p c) -> p c", p=128), HUGES[:, 0:8 * CAP // 128], s_il)
        pace(s_p1, [od1, il, (s_xn[0], s_xn[0].n), (s_xn[1], s_xn[1].n)])

        wvt_l = dma("sync", ar(WT_OFF, NDT * 1024), wvt_s[:, :], s_wt[0])
        vbank_free = [None, None]
        for jt in range(NDT):
            for tc in range(NQC):
                bank = (jt * NQC + tc) % 2
                for dt_ in range(NDT):
                    wts = []
                    if dt_ == 0:
                        wts.append(wvt_l)
                        if vbank_free[bank] is not None:
                            wts.append(vbank_free[bank])
                    mm(ar(WT_OFF + dt_ * 1024 + jt * 128, 128),
                       XN[:, dt_ * T + tc * 512: dt_ * T + (tc + 1) * 512],
                       pf(bank), dt_ == 0, dt_ == NDT - 1, waits=wts,
                       inc=dt_ == NDT - 1)
                vbank_free[bank] = act(
                    ar(VT_OFF + jt * T + tc * 512, 512), pf(bank), AF.Identity,
                    bias=BVC[:, jt:jt + 1], waits=[sPE.at()])
        bar()
        v_ones = vop(lambda e: e.memset(
            bass.AP(AR, VAUG_OFF + 64, [[ARW, 128], [1040, KT], [65, 16]]), 1.0),
            waits=conv_staging_free)
        pbf_free = [None, None]
        for kt in range(KT):
            for jt in range(NDT):
                k = kt * NDT + jt
                wts = [v_ones] if k == 0 else []
                if pbf_free[k % 2] is not None:
                    wts.append(pbf_free[k % 2])
                tr(pbf(k % 2), ar(VT_OFF + jt * T + kt * 128, 128),
                   CONST["idn_bf"][:, :], waits=wts)
                spe_t = sPE.at()
                pbf_free[k % 2] = vop(
                    lambda e, p_=pbf(k % 2),
                    o=bass.AP(AR, VAUG_OFF + kt * 1040 + 2 * jt * 65,
                              [[ARW, 128], [65, 2], [1, 64]]):
                    e.tensor_copy(o, p_), waits=[spe_t])
        bar()

        PT_OFF = WT_OFF
        pbank_free = [None, None]
        for pi in range(4):
            for tc in range(NQC):
                bank = (pi * NQC + tc) % 2
                for dt_ in range(NDT):
                    wts = []
                    if dt_ == 0 and pbank_free[bank] is not None:
                        wts.append(pbank_free[bank])
                    mm(WQT[:, pi * 512 + dt_ * 64: pi * 512 + (dt_ + 1) * 64],
                       XN[:, dt_ * T + tc * 512: dt_ * T + (tc + 1) * 512],
                       pf(bank, rows=64), dt_ == 0, dt_ == NDT - 1, waits=wts,
                       inc=dt_ == NDT - 1)
                pbank_free[bank] = act(
                    AR[0:64, PT_OFF + pi * T + tc * 512:
                       PT_OFF + pi * T + (tc + 1) * 512],
                    pf(bank, rows=64), AF.Copy, waits=[sPE.at()])
        bar()
        T1_OFF = SD_OFF
        LPRE_OFF = SD_OFF + 2048
        SQ_OFF = LPRE_OFF + 512
        INV_OFF = SQ_OFF + 512
        BINV_OFF = INV_OFF + 512
        selbank_free = [None] * 4
        iter_gate = None
        nbank_free = None
        padbank_free = [None, None]
        binvbank_free = [None, None]
        binvsb_free = [None, None]
        last_pad_use = None
        for side in range(2):
            pad_off = PADQ_OFF if side == 0 else PADK_OFF
            si = CONST["sqi"] if side == 0 else CONST["ski"]
            sj = CONST["sqj"] if side == 0 else CONST["skj"]
            p1 = PT_OFF + (0 if side == 0 else 2 * T)
            p2 = p1 + T
            for tc in range(NQC):
                sels = [(si, p1), (sj, p2), (sj, p1), (si, p2)]
                ev_marks = []
                for k, (smat, poff) in enumerate(sels):
                    wts = []
                    if k == 0 and iter_gate is not None:
                        wts += iter_gate
                    if selbank_free[k] is not None:
                        wts.append(selbank_free[k])
                    mm(smat[:, :], AR[0:64, poff + tc * 512: poff + (tc + 1) * 512],
                       pf(k, rows=96), True, True, waits=wts, inc=True)
                    ev = act(AR[0:96, T1_OFF + k * 512: T1_OFF + (k + 1) * 512],
                             pf(k, rows=96), AF.Copy, waits=[sPE.at()])
                    selbank_free[k] = ev
                    ev_marks.append(ev)
                t_ = [AR[0:96, T1_OFF + k * 512: T1_OFF + (k + 1) * 512]
                      for k in range(4)]
                lpre = AR[0:96, LPRE_OFF:LPRE_OFF + 512]
                sq = AR[0:96, SQ_OFF:SQ_OFF + 512]
                vop(lambda e, a=lpre, x=t_[0], y=t_[1]: e.tensor_mul(a, x, y),
                    waits=[ev_marks[-1]] + ([last_pad_use] if last_pad_use else []),
                    drain=True)
                vop(lambda e, a=sq, x=t_[2], y=t_[3]: e.tensor_mul(a, x, y),
                    drain=True)
                v_lp = vop(lambda e, a=lpre, x=lpre, y=sq: e.tensor_sub(a, x, y),
                           drain=True)
                a_sq = act(sq, lpre, AF.Square, waits=[v_lp], drain=True)
                wts = [a_sq]
                if nbank_free is not None:
                    wts.append(nbank_free)
                mm(CONST["o96"][:, :], sq, pf(4, rows=16), True, True,
                   waits=wts, inc=True)
                inv = AR[0:16, INV_OFF:INV_OFF + 512]
                act(inv, pf(4, rows=16), AF.Ln, bias=EPS2[0:16, 0:1],
                    waits=[sPE.at()], drain=True)
                inv_ready = act(inv, inv, AF.Exp, scale=-0.5, drain=True)
                nbank_free = inv_ready
                for g in range(4):
                    bank = g % 2
                    wts = [v_lp]
                    if padbank_free[bank] is not None:
                        wts.append(padbank_free[bank])
                    mm(CONST["pg"][:, g * 128:(g + 1) * 128], lpre,
                       pf(bank), True, True, waits=wts, inc=True)
                    spe_pad = sPE.at()
                    wts = [inv_ready]
                    if binvbank_free[bank] is not None:
                        wts.append(binvbank_free[bank])
                    mm(CONST["s16"][0:16, g * 128:(g + 1) * 128], inv,
                       pf(2 + bank), True, True, waits=wts, inc=True)
                    binv = AR[0:128, BINV_OFF + bank * 512:
                              BINV_OFF + (bank + 1) * 512]
                    wts = [sPE.at()]
                    if binvsb_free[bank] is not None:
                        wts.append(binvsb_free[bank])
                    a_bi = act(binv, pf(2 + bank), AF.Copy, waits=wts)
                    binvbank_free[bank] = a_bi
                    dst = AR[0:128, pad_off + g * T + tc * 512:
                             pad_off + g * T + (tc + 1) * 512]
                    if side == 0:
                        v_c = vop(lambda e, a=dst, p_=pf(bank), y=binv:
                                  e.tensor_mul(a, p_, y),
                                  waits=[a_bi, spe_pad], drain=True)
                    else:
                        v_c = vop(lambda e, a=dst, p_=pf(bank), y=binv:
                                  e.scalar_tensor_tensor(
                                      a, p_, CONST["sgnpad"][:, 0:1], y,
                                      ALU.mult, ALU.mult),
                                  waits=[a_bi, spe_pad], drain=True)
                    padbank_free[bank] = v_c
                    binvsb_free[bank] = v_c
                last_pad_use = v_c
                iter_gate = [sACT.at(), sDVE.at()]
        bar()

        wot_l = dma("sync", ar(WT_OFF, NDT * 1024), wot_s[:, :], s_wt[1])
        AT_OFF = SD_OFF
        OF_OFF = SD_OFF + 2048
        sc_free = {0: [], 1: []}
        at_free = {0: None, 1: None}
        po_gate = []
        pending_evac = None
        ofull_last = None
        recb_free = None
        proj_free = [None, None]
        for qc in range(NQC):
            nkt = 4 * qc + 4
            ofull_last = None
            for hp in range(H // 2):
                av_last = None
                ready = {}

                def emit_sc(kt):
                    par = kt % 2
                    cb = 2 * par
                    for hi in range(2):
                        h = 2 * hp + hi
                        g, j = divmod(h, 4)
                        base = 32 * j
                        tpos = (96, 0) if base == 96 else None
                        lq = AR[base:base + 6,
                                PADK_OFF + g * T + kt * 128:
                                PADK_OFF + g * T + kt * 128 + 128]
                        rq = AR[base:base + 6,
                                PADQ_OFF + g * T + qc * 512:
                                PADQ_OFF + g * T + (qc + 1) * 512]
                        wts = list(sc_free[par]) if hi == 0 else []
                        mm(lq, rq, pf(cb + hi), True, True, waits=wts,
                           inc=hi == 1, tile_position=tpos)
                    spe_sc = sPE.at()
                    atdst = AR[0:128, AT_OFF + par * 1024:
                               AT_OFF + par * 1024 + 1024]
                    wts = [spe_sc]
                    if at_free[par] is not None:
                        wts.append(at_free[par])
                    a_exp = act(atdst, PF[0:128, cb * 512: cb * 512 + 1024],
                                AF.Exp, scale=SCALE, waits=wts)
                    sc_free[par] = [a_exp]
                    rdy = a_exp
                    if kt >= 4 * qc:
                        d_ = kt - 4 * qc
                        mk = bass.AP(CONST["masks4"], d_ * 512,
                                     [[2048, 128], [0, 2], [1, 512]])
                        rdy = vop(lambda e, a=atdst, m_=mk:
                                  e.tensor_mul(a, a, m_), waits=[a_exp])
                    ready[kt] = rdy

                def emit_av(kt):
                    nonlocal av_last
                    par = kt % 2
                    for hi in range(2):
                        h = 2 * hp + hi
                        va = ar(VAUG_OFF + kt * 1040 + h * 65, 65)
                        at_ = AR[0:128, AT_OFF + par * 1024 + hi * 512:
                                 AT_OFF + par * 1024 + (hi + 1) * 512]
                        wts = []
                        if hi == 0:
                            wts.append(ready[kt])
                            if kt == 0:
                                wts += po_gate
                        av_last = mm(va, at_, pf(4 + hi, rows=65),
                                     kt == 0, kt == nkt - 1,
                                     waits=wts, inc=hi == 1)
                    at_free[par] = av_last

                emit_sc(0)
                emit_sc(1)
                if pending_evac is not None:
                    pending_evac()
                    pending_evac = None
                emit_av(0)
                for kt in range(2, nkt):
                    emit_sc(kt)
                    emit_av(kt - 1)
                emit_av(nkt - 1)

                def make_evac(av_mark, hp_):
                    def run_evac():
                        nonlocal recb_free, po_gate, ofull_last
                        po_marks = []
                        for hi in range(2):
                            h = 2 * hp_ + hi
                            dt_, half = divmod(h, 2)
                            rec = XNF[0:1, hi * 512: hi * 512 + 512]
                            wts = [av_mark]
                            if recb_free is not None:
                                wts.append(recb_free)
                            act(rec,
                                PF[64:65, (4 + hi) * 512:(4 + hi) * 512 + 512],
                                AF.Ln, waits=wts, drain=True)
                            v_rec = act(rec, rec, AF.Exp, scale=-1.0,
                                        drain=True)
                            a_of = vop(
                                lambda e,
                                a=AR[half * 64: half * 64 + 64,
                                     OF_OFF + dt_ * 512:OF_OFF + (dt_ + 1) * 512],
                                p_=PF[0:64, (4 + hi) * 512:(4 + hi) * 512 + 512]:
                                e.tensor_copy(a, p_), waits=[av_mark])
                            mm(ONESF[0:1, 0:64], rec, pf(4 + hi, rows=64),
                               True, True, waits=[v_rec, a_of], inc=True)
                            v_on = vop(lambda e,
                                       a=AR[half * 64: half * 64 + 64,
                                            OF_OFF + dt_ * 512:
                                            OF_OFF + (dt_ + 1) * 512],
                                       p_=pf(4 + hi, rows=64):
                                       e.tensor_mul(a, a, p_),
                                       waits=[sPE.at(), a_of])
                            po_marks.append(v_on)
                        recb_free = sPE.at()
                        po_gate = po_marks
                        ofull_last = po_marks[-1]
                    return run_evac

                pending_evac = make_evac(av_last, hp)
            pending_evac()
            pending_evac = None
            for jt in range(NDT):
                bank = jt % 2
                for dt_ in range(NDT):
                    wts = []
                    if dt_ == 0:
                        wts.append(ofull_last)
                        if jt == 0:
                            wts.append(wot_l)
                            wts += sc_free[0] + sc_free[1]
                        if proj_free[bank] is not None:
                            wts.append(proj_free[bank])
                    mm(ar(WT_OFF + dt_ * 1024 + jt * 128, 128),
                       AR[0:128, OF_OFF + dt_ * 512: OF_OFF + (dt_ + 1) * 512],
                       pf(bank), dt_ == 0, dt_ == NDT - 1, waits=wts,
                       inc=dt_ == NDT - 1)
                proj_free[bank] = vop(
                    lambda e, a=ACC[:, jt * T + qc * 512: jt * T + (qc + 1) * 512],
                    p_=pf(bank), bc=BOC[:, jt:jt + 1]:
                    e.scalar_tensor_tensor(a, p_, bc, a, ALU.add, ALU.add),
                    waits=[sPE.at()], drain=True)
            sc_free[0] = [m for m in proj_free if m is not None]
        bar()

        xg_free = None
        xgt_free = None
        yt_free = [None] * NDT
        yn_free = None
        pbfa = [None, None]
        w1_free = [None, None]
        w2_free = [None, None]
        y_free = {2: None, 3: None}
        for ei in range(E):
            nm = (ei + 1) * (CAP // 128)
            wts = [(s_st2[0], 16 * (b * NCH2 + nm))]
            if xg_free is not None:
                wts.append(xg_free)
            xg_l = dma("sync", ar(XGN_OFF, CPW),
                       xperm_d[b * 8 * CAP + ei * CAP:
                               b * 8 * CAP + (ei + 1) * CAP, :].rearrange(
                           "(c p) f -> p c f", p=128),
                       s_xg[ei % 2], waits=wts)
            tm = []
            for ck in range(CAP // 128):
                for dt_ in range(NDT):
                    k = (ck * NDT + dt_) % 2
                    wts = [xg_l] if (ck == 0 and dt_ == 0) else []
                    if pbfa[k] is not None:
                        wts.append(pbfa[k])
                    if ck == 0 and dt_ == 0 and xgt_free is not None:
                        wts.append(xgt_free)
                    tr(pbf(k), ar(XGN_OFF + ck * 1024 + dt_ * 128, 128),
                       CONST["idn_bf"][:, :], waits=wts)
                    pbfa[k] = vop(lambda e,
                                  a=ar(XGT_OFF + dt_ * CAP + ck * 128, 128),
                                  p_=pbf(k): e.tensor_copy(a, p_),
                                  waits=[sPE.at()])
                    tm.append(pbfa[k])
            xgt_ready = tm[-1]
            xg_free = sPE.at()
            gelu_done = {}
            w1_l = None
            for ft in range(NFT):
                sbi = (ft // 4) % 2
                if ft % 4 == 0:
                    w = [w1_free[sbi]] if w1_free[sbi] is not None else []
                    if b == 0:
                        w = w + w1conv_done[ei]
                    w1_l = dma("sync", ar(W1OFF + sbi * 4096, 4096),
                               w1bf[ei, :, ft * 128:(ft + 4) * 128].rearrange(
                                   "(t p) f -> p t f", p=128),
                               s_w1[sbi], waits=w)
                bank = ft % 2
                for dt_ in range(NDT):
                    wts = []
                    if dt_ == 0:
                        wts.append(w1_l)
                        if ft >= 2:
                            wts.append(gelu_done[ft - 2])
                        if ft == 0:
                            wts.append(xgt_ready)
                    mm(ar(W1OFF + sbi * 4096 + dt_ * 512 + (ft % 4) * 128, 128),
                       ar(XGT_OFF + dt_ * CAP, CAP),
                       pf(bank, width=CAP), dt_ == 0, dt_ == NDT - 1, waits=wts,
                       inc=dt_ == NDT - 1)
                gelu_done[ft] = act(
                    AR[0:128, ft * CAP: ft * CAP + CAP], pf(bank, width=CAP),
                    AF.Gelu, bias=B1C[:, ei * NFT + ft: ei * NFT + ft + 1],
                    waits=[sPE.at()])
                if ft % 4 == 3:
                    w1_free[(ft // 4) % 2] = sPE.at()
            h_ready = gelu_done[NFT - 1]
            for ht in range(NDT):
                sbi = ht % 2
                w = [w2_free[sbi]] if w2_free[sbi] is not None else []
                if b == 0:
                    w = w + w2conv_done[ei]
                w2_l = dma("scalar", ar(W2OFF + sbi * 4096, 4096),
                           w2bf[ei, :, ht * 128:(ht + 1) * 128].rearrange(
                               "(t p) h -> p t h", p=128),
                           s_w2[sbi], waits=w)
                bank = 2 + ht % 2
                for ft in range(NFT):
                    wts = []
                    if ft == 0:
                        wts = [w2_l, h_ready]
                        if y_free[bank] is not None:
                            wts.append(y_free[bank])
                    mm(ar(W2OFF + sbi * 4096 + ft * 128, 128),
                       AR[0:128, ft * CAP: ft * CAP + CAP],
                       pf(bank, width=CAP), ft == 0, ft == NFT - 1, waits=wts,
                       inc=ft == NFT - 1)
                w2_free[sbi] = sPE.at()
                wts = [sPE.at()]
                if yt_free[ht] is not None:
                    wts.append(yt_free[ht])
                y_free[bank] = act(
                    ar(YT_OFF + ht * CAP, CAP), pf(bank, width=CAP),
                    AF.Identity, bias=B2C[:, ei * NDT + ht: ei * NDT + ht + 1],
                    waits=wts)
                yt_free[ht] = y_free[bank]
            tm = []
            for ck in range(CAP // 128):
                for ht in range(NDT):
                    k = (ck * NDT + ht) % 2
                    wts = [yt_free[ht]] if ck == 0 else []
                    if pbfa[k] is not None:
                        wts.append(pbfa[k])
                    if ck == 0 and ht == 0 and yn_free is not None:
                        wts.append(yn_free)
                    tr(pbf(k), ar(YT_OFF + ht * CAP + ck * 128, 128),
                       CONST["idn_bf"][:, :], waits=wts)
                    pbfa[k] = vop(lambda e,
                                  a=ar(YN_OFF + ck * 1024 + ht * 128, 128),
                                  p_=pbf(k): e.tensor_copy(a, p_),
                                  waits=[sPE.at()])
                    tm.append(pbfa[k])
            yn_free = dma("sync",
                          yperm_d[b * 8 * CAP + ei * CAP:
                                  b * 8 * CAP + (ei + 1) * CAP, :].rearrange(
                              "(c p) f -> p c f", p=128),
                          ar(YN_OFF, CPW), s_yp[ei % 2], waits=[tm[-1]])
            for ht in range(NDT):
                yt_free[ht] = sPE.at()
        yperm_done = [s_yp[0].at(), s_yp[1].at()]
        bar()

        pace(s_p3, list(yperm_done) + [sACT.at(), sDVE.at(), sPE.at()])

        ost_free = [None, None]
        pbf_free = [None, None]
        for tt in range(NCHUNK):
            bi = tt % 2
            ost = XST[:, bi * 1024:(bi + 1) * 1024]
            a_last = None
            for jt in range(NDT):
                k = (tt * NDT + jt) % 2
                wts = []
                if ost_free[bi] is not None and jt == 0:
                    wts.append(ost_free[bi])
                if pbf_free[k] is not None:
                    wts.append(pbf_free[k])
                tr(pbf(k), ACC[:, jt * T + tt * 128: jt * T + (tt + 1) * 128],
                   CONST["idn_bf"][:, :], waits=wts)
                a_last = act(ost[:, jt * 128:(jt + 1) * 128], pbf(k), AF.Copy,
                             waits=[sPE.at()])
                pbf_free[k] = a_last
            plan.add("vector",
                     lambda e, a=ost, y_=XNF[:, 0:1024]:
                     e.tensor_add(a, a, y_),
                     waits=[a_last, (s_l3[0], 32 * (b * NCHUNK + tt) + 32)],
                     inc=(s_yc, 1))
            v_add = s_yc.at()
            st = dma("sync", out_t[b * T + tt * 128: b * T + (tt + 1) * 128, :],
                     ost, s_o[bi], waits=[v_add])
            ost_free[bi] = st
        bar()

    plan.add("sync", None, waits=[(s, s.n) for s in sems.values()])

    with nc.Block() as blk:
        plan.emit(blk)
    es.close()
    return nc


def prep_inputs(L0_attn_x, L0_attn_W1_q, L0_attn_W2_q, L0_attn_W1_k, L0_attn_W2_k,
                L0_attn_W_v, L0_attn_b_v, L0_attn_W_out, L0_attn_b_out,
                L0_ffn_wr, L0_ffn_br, L0_ffn_w1, L0_ffn_b1, L0_ffn_w2, L0_ffn_b2,
                T=2048, ncores=8):
    f = lambda a: np.ascontiguousarray(np.asarray(a, dtype=np.float32))
    consts = make_consts(T)
    base = {
        "w1q": f(L0_attn_W1_q), "w2q": f(L0_attn_W2_q),
        "w1k": f(L0_attn_W1_k), "w2k": f(L0_attn_W2_k),
        "wv": f(L0_attn_W_v), "wo": f(L0_attn_W_out),
        "wr": f(np.asarray(L0_ffn_wr, np.float32).reshape(NDT, 128, 8)
                .transpose(1, 0, 2).reshape(128, NDT * 8)),
        "br": f(np.asarray(L0_ffn_br, np.float32).reshape(1, 8)),
        "w1": f(L0_ffn_w1), "w2": f(L0_ffn_w2),
        "bvc": f(np.asarray(L0_attn_b_v, np.float32).reshape(NDT, 128).T),
        "boc": f(np.asarray(L0_attn_b_out, np.float32).reshape(NDT, 128).T),
        "b1c": f(np.asarray(L0_ffn_b1, np.float32).reshape(E, NFT, 128)
                 .transpose(2, 0, 1).reshape(128, E * NFT)),
        "b2c": f(np.asarray(L0_ffn_b2, np.float32).reshape(E, NDT, 128)
                 .transpose(2, 0, 1).reshape(128, E * NDT)),
    }
    for n in CONST_SPECS:
        base["c_" + n] = np.ascontiguousarray(consts[n])
    x = np.asarray(L0_attn_x, np.float32)
    in_maps = []
    bpc = x.shape[0] // ncores
    iot = (np.arange(bpc * T // 128, dtype=np.uint32)[:, None] * 128
           + np.arange(128, dtype=np.uint32)[None, :])
    base["iotat"] = np.ascontiguousarray(iot)
    for c in range(ncores):
        m = dict(base)
        m["x"] = np.ascontiguousarray(x[c * bpc:(c + 1) * bpc].reshape(bpc * T, D))
        in_maps.append(m)
    return in_maps


def kernel(**inputs):
    T, NB, ncores = 2048, 4, 8
    in_maps = prep_inputs(**inputs, T=T, ncores=ncores)
    nc = build_program(T=T, NB=NB)
    res = run_bass_kernel_spmd(nc, in_maps, core_ids=list(range(ncores)))
    out = np.concatenate(
        [np.asarray(r["out"]).reshape(NB, T, D) for r in res.results], axis=0)
    return out.astype(np.float32)


# version 15
# speedup vs baseline: 1.0750x; 1.0643x over previous
import contextlib
import numpy as np
import ml_dtypes
import concourse.bass as bass
import concourse.mybir as mybir
from concourse.bass_utils import run_bass_kernel_spmd
from concourse.bass_core_rust import IndirectOffsetOnAxis

F32 = mybir.dt.float32
BF16 = mybir.dt.bfloat16
U32 = mybir.dt.uint32
AF = mybir.ActivationFunctionType
ALU = mybir.AluOpType
AX = mybir.AxisListType

PAIRS = [(0, 1), (0, 2), (0, 3), (1, 2), (1, 3), (2, 3)]
SGN6 = [1.0, -1.0, 1.0, 1.0, -1.0, 1.0]
SCALE = 6.0 ** -0.5
D = 1024
H = 16
E = 8
FF = 4096
NDT = D // 128
NFT = FF // 128
EPS_LN = 1e-5
EPS_NRM = 1e-24


class Sem:
    def __init__(self, nc, es, name):
        self.h = es.enter_context(nc.semaphore(name))
        self.n = 0
        self.name = name
        import os
        if os.environ.get("SEM_DEBUG"):
            print("SEM", self.h.num, name)

    def at(self):
        return (self, self.n)


class Plan:
    ENGINES = ("sync", "gpsimd", "tensor", "scalar", "vector")

    def __init__(self):
        self.ops = {e: [] for e in self.ENGINES}

    def add(self, eng, fn, waits=(), inc=None, drain=False):
        if inc is not None:
            inc[0].n += inc[1]
        ws = []
        for s, v in waits:
            if v > 0:
                ws.append((s, v))
        self.ops[eng].append((fn, ws, inc, drain))

    def barrier(self, sems):
        marks = [(s, s.n) for s in sems if s.n > 0]
        for e in self.ENGINES:
            self.add(e, None, waits=marks)

    def emit(self, blk):
        def mk(name):
            def run(eng):
                for fn, waits, inc, drain in self.ops[name]:
                    if drain:
                        eng.drain()
                    for s, v in waits:
                        eng.wait_ge(s.h, v)
                    if fn is None:
                        continue
                    instr = fn(eng)
                    if inc is not None:
                        instr.then_inc(inc[0].h, inc[1])
            return run
        blk.sync(mk("sync"))
        blk.gpsimd(mk("gpsimd"))
        blk.tensor(mk("tensor"))
        blk.scalar(mk("scalar"))
        blk.vector(mk("vector"))


def make_consts(T):
    c = {}
    c["idn_f"] = np.eye(128, dtype=np.float32)
    c["idn_bf"] = np.eye(128, dtype=np.float32)
    m4 = np.zeros((128, 4 * 512), np.float32)
    for d_ in range(4):
        delta = d_ * 128
        r = np.arange(128)[:, None]
        q = np.arange(512)[None, :]
        m4[:, d_ * 512:(d_ + 1) * 512] = (r + delta <= q).astype(np.float32)
    c["masks4"] = m4
    sqi = np.zeros((64, 96), np.float32)
    sqj = np.zeros((64, 96), np.float32)
    ski = np.zeros((64, 96), np.float32)
    skj = np.zeros((64, 96), np.float32)
    for h in range(H):
        for cc in range(6):
            i, j = PAIRS[cc]
            ir, jr = PAIRS[5 - cc]
            sqi[4 * h + i, 6 * h + cc] = 1
            sqj[4 * h + j, 6 * h + cc] = 1
            ski[4 * h + ir, 6 * h + cc] = 1
            skj[4 * h + jr, 6 * h + cc] = 1
    c["sqi"], c["sqj"], c["ski"], c["skj"] = sqi, sqj, ski, skj
    o96 = np.zeros((96, 16), np.float32)
    for h in range(H):
        o96[6 * h:6 * h + 6, h] = 1
    c["o96"] = o96
    pg = np.zeros((96, 4 * 128), np.float32)
    s16 = np.zeros((16, 4 * 128), np.float32)
    for g in range(4):
        for j in range(4):
            for cc in range(6):
                pg[6 * (4 * g + j) + cc, g * 128 + 32 * j + cc] = 1
                s16[4 * g + j, g * 128 + 32 * j + cc] = 1
    c["pg"], c["s16"] = pg, s16
    ones8 = np.zeros((8, 3), np.float32)
    ones8[:, 0] = 1
    ones8[:, 1] = np.arange(8)
    ones8[:, 2] = np.arange(8) * (T // 4)
    c["ones8"] = ones8
    c["onescol"] = np.ones((1, 128), np.float32)
    sgn = np.ones((128, 1), np.float32)
    for j in range(4):
        for cc in range(6):
            sgn[32 * j + cc, 0] = SGN6[cc]
    c["sgnpad"] = sgn
    return c


CONST_SPECS = {
    "idn_f": ([128, 128], F32), "idn_bf": ([128, 128], BF16),
    "masks4": ([128, 2048], BF16),
    "sqi": ([64, 96], BF16), "sqj": ([64, 96], BF16),
    "ski": ([64, 96], BF16), "skj": ([64, 96], BF16),
    "o96": ([96, 16], BF16), "pg": ([96, 512], BF16), "s16": ([16, 512], BF16),
    "ones8": ([8, 3], F32),
    "onescol": ([1, 128], BF16), "sgnpad": ([128, 1], F32),
}


def build_program(T=2048, NB=4):
    KT = T // 128
    NQC = T // 512
    NTH = max(1, T // 1024)
    TH = T // NTH
    NTCQ = TH // 512
    NCHUNK = T // 128

    nc = bass.Bass("TRN2", target_bir_lowering=False)

    x_in = nc.dram_tensor("x", [NB * T, D], F32, kind="ExternalInput")
    out_t = nc.dram_tensor("out", [NB * T, D], F32, kind="ExternalOutput")
    wq = {p: nc.dram_tensor(p, [64, D], F32, kind="ExternalInput")
          for p in ("w1q", "w2q", "w1k", "w2k")}
    wv_t = nc.dram_tensor("wv", [D, D], F32, kind="ExternalInput")
    wo_t = nc.dram_tensor("wo", [D, D], F32, kind="ExternalInput")
    wr_t = nc.dram_tensor("wr", [128, NDT * 8], F32, kind="ExternalInput")
    br_t = nc.dram_tensor("br", [1, 8], F32, kind="ExternalInput")
    w1_t = nc.dram_tensor("w1", [E, D, FF], F32, kind="ExternalInput")
    w2_t = nc.dram_tensor("w2", [E, FF, D], F32, kind="ExternalInput")
    bvc_t = nc.dram_tensor("bvc", [128, NDT], F32, kind="ExternalInput")
    boc_t = nc.dram_tensor("boc", [128, NDT], F32, kind="ExternalInput")
    b1c_t = nc.dram_tensor("b1c", [128, E * NFT], F32, kind="ExternalInput")
    b2c_t = nc.dram_tensor("b2c", [128, E * NDT], F32, kind="ExternalInput")
    b2r_t = nc.dram_tensor("b2r", [E, D], F32, kind="ExternalInput")
    cin = {n: nc.dram_tensor("c_" + n, [sh[0], sh[1]], F32, kind="ExternalInput")
           for n, (sh, dt) in CONST_SPECS.items()}
    w1bf = nc.dram_tensor("w1bf", [E, D, FF], BF16)
    w2bf = nc.dram_tensor("w2bf", [E, FF, D], BF16)
    wvt_s = nc.dram_tensor("wvt_s", [128, NDT * 1024], BF16)
    b2rbf = nc.dram_tensor("b2rbf", [E, D], BF16)
    wot_s = nc.dram_tensor("wot_s", [128, NDT * 1024], BF16)
    CAP = T // 4
    xnat_d = nc.dram_tensor("xnat_d", [NB * T, D], BF16)
    xperm_d = nc.dram_tensor("xperm_d", [NB * 8 * CAP, D], BF16)
    yperm_d = nc.dram_tensor("yperm_d", [NB * 8 * CAP, D], BF16)
    lists_d = nc.dram_tensor("lists_d", [NB * 8 * CAP, 1], U32)
    offs_d = nc.dram_tensor("offs_d", [NB * (T // 128), 128], U32)
    iotat_t = nc.dram_tensor("iotat", [NB * (T // 128), 128], U32,
                             kind="ExternalInput")

    es = contextlib.ExitStack()

    def sbuf(name, shape, dtype):
        return es.enter_context(nc.sbuf_tensor(name, shape, dtype))

    XN = sbuf("XN", [128, NDT * T], BF16)
    ACC = sbuf("ACC", [128, NDT * T], BF16)
    _CAP = T // 4
    _CPW = (_CAP // 128) * 1024
    ARW = max(8 * T + KT * 1040 + 8192 + 6656,
              32 * _CAP + 2 * _CPW + 2 * NDT * _CAP + 4 * 4096)
    assert ARW <= 49600, ARW
    AR = sbuf("AR", [128, ARW], BF16)
    XST = sbuf("XST", [128, 2 * 1024], F32)
    XNF = sbuf("XNF", [128, 2 * 1024], F32)
    XNLO = sbuf("XNLO", [128, 1024], BF16)
    TRASH = sbuf("TRASH", [128, 1024], BF16)
    LNST = sbuf("LNST", [128, 16], F32)
    MASK8 = sbuf("MASK8", [8, T], BF16)
    XNB = sbuf("XNB", [128, 2 * 1024], BF16)
    OFFS = sbuf("OFFS", [128, 16], U32)
    HUGES = sbuf("HUGES", [128, 32], U32)
    STG = sbuf("STG", [128, 8], U32)
    WQT = sbuf("WQT", [128, 4 * 512], BF16)
    WRHI = sbuf("WRHI", [128, NDT * 8], BF16)
    WRLO = sbuf("WRLO", [128, NDT * 8], BF16)
    WRF = sbuf("WRF", [128, NDT * 8], F32)
    WRT1 = sbuf("WRT1", [128, NDT * 8], F32)
    BRHI = sbuf("BRHI", [1, 8], BF16)
    BRLO = sbuf("BRLO", [1, 8], BF16)
    BRF = sbuf("BRF", [1, 8], F32)
    BRT1 = sbuf("BRT1", [1, 8], F32)
    BVC = sbuf("BVC", [128, NDT], F32)
    BOC = sbuf("BOC", [128, NDT], F32)
    B1C = sbuf("B1C", [128, E * NFT], F32)
    B2C = sbuf("B2C", [128, E * NDT], F32)
    ONESF = sbuf("ONESF", [1, 128], F32)
    EPS1 = sbuf("EPS1", [128, 1], F32)
    EPS2 = sbuf("EPS2", [128, 1], F32)
    CONST = {n: sbuf("k_" + n, sh, dt) for n, (sh, dt) in CONST_SPECS.items()}

    PF = es.enter_context(nc.psum_tensor("PF", [128, 3072], F32))
    PB = es.enter_context(nc.psum_tensor("PB", [128, 2048], BF16))

    def pf(bank, width=512, rows=128):
        return PF[0:rows, bank * 512: bank * 512 + width]

    def pbf(half, rows=128, width=128):
        return PB[0:rows, half * 1024: half * 1024 + width]

    def ar(off, width, rows=128, r0=0):
        return AR[r0:r0 + rows, off:off + width]

    PADQ_OFF = 0
    PADK_OFF = 4 * T
    VT_OFF = 0
    VAUG_OFF = 8 * T
    WT_OFF = VAUG_OFF + KT * 1040
    SD_OFF = WT_OFF + 8192
    HS_OFF = 0
    CPW = (CAP // 128) * 1024
    XGN_OFF = 32 * CAP
    XGT_OFF = XGN_OFF + CPW
    YT_OFF = XGT_OFF + NDT * CAP
    YN_OFF = YT_OFF + NDT * CAP
    W1OFF = YN_OFF + CPW
    W2OFF = W1OFF + 2 * 4096
    assert W2OFF + 2 * 4096 <= ARW, (W2OFF, ARW)


    plan = Plan()
    sems = {}

    def S(name):
        if name not in sems:
            sems[name] = Sem(nc, es, name)
        return sems[name]

    sPE, sACT, sDVE = S("sPE"), S("sACT"), S("sDVE")
    s_cst = S("s_cst")
    s_cstg = S("s_cstg")
    s_cv = [S("s_cv0"), S("s_cv1")]
    s_x = [S("s_x0"), S("s_x1")]
    s_wt = [S("s_wt0"), S("s_wt1")]
    s_w1 = [S("s_w1a"), S("s_w1b")]
    s_w2 = [S("s_w2a"), S("s_w2b")]
    s_o = [S("s_o0"), S("s_o1")]
    s_wts = S("s_wts")
    s_xn = [S("s_xn0"), S("s_xn1")]
    s_stg = S("s_stg")
    s_l1 = S("s_l1")
    s_g2 = [S("s_g2a")]
    s_st2 = [S("s_st2a")]
    s_l3 = [S("s_l3a")]
    s_yc = S("s_yc")
    s_il = S("s_il")
    s_xg = [S("s_xg0"), S("s_xg1")]
    s_b2 = S("s_b2")
    s_yp = [S("s_yp0"), S("s_yp1")]
    s_p1 = S("s_p1")
    s_p3 = S("s_p3")

    NOBAR = {"s_cv0", "s_cv1", "s_sv0", "s_sv1",
             "s_stg", "s_l1", "s_g2a", "s_g2b", "s_st2a", "s_st2b",
             "s_l3a", "s_l3b", "s_il", "s_p1", "s_p3", "s_yc"}

    def bar():
        plan.barrier([s for n, s in sems.items() if n not in NOBAR])

    def dma(q, dst, src, sem, waits=()):
        plan.add(q, lambda e, a=dst, b=src: e.dma_start(a, b), waits=waits,
                 inc=(sem, 16))
        return sem.at()

    def mm(lhsT, rhs, out, start, stop, waits=(), inc=False, tile_position=None):
        kw = {}
        if tile_position is not None:
            kw["tile_position"] = tile_position
        plan.add("tensor",
                 lambda e, a=out, l=lhsT, r=rhs, s1=start, s2=stop, k=dict(kw):
                 e.matmul(a, l, r, start=s1, stop=s2, **k),
                 waits=waits, inc=(sPE, 1) if inc else None)
        return sPE.at()

    def tr(out, in_, idn, waits=(), inc=True):
        plan.add("tensor", lambda e, a=out, b=in_, c=idn: e.transpose(a, b, c),
                 waits=waits, inc=(sPE, 1) if inc else None)
        return sPE.at()

    def act(out, in_, func, scale=1.0, bias=None, accum=None, waits=(),
            inc=True, drain=False):
        def f(e, a=out, b=in_, fn=func, sc=scale, bi=bias, ac=accum):
            kw = {"scale": sc}
            if bi is not None:
                kw["bias"] = bi
            if ac is not None:
                kw["accum_out"] = ac
            return e.activation(a, b, fn, **kw)
        plan.add("scalar", f, waits=waits, inc=(sACT, 1) if inc else None,
                 drain=drain)
        return sACT.at()

    def vop(fn, waits=(), inc=True, drain=False):
        plan.add("vector", fn, waits=waits, inc=(sDVE, 1) if inc else None,
                 drain=drain)
        return sDVE.at()

    for n, (sh, dt) in CONST_SPECS.items():
        q = "sync" if dt == F32 else "gpsimd"
        dma(q, CONST[n][:, :], cin[n][:, :], s_cst if q == "sync" else s_cstg)
    dma("sync", WRF[:, :], wr_t[:, :], s_cst)
    dma("sync", BRF[:, :], br_t[:, :], s_cst)
    dma("sync", BVC[:, :], bvc_t[:, :], s_cst)
    dma("sync", BOC[:, :], boc_t[:, :], s_cst)
    dma("sync", B1C[:, :], b1c_t[:, :], s_cst)
    dma("sync", B2C[:, :], b2c_t[:, :], s_cst)
    dma("gpsimd", b2rbf[:, :], b2r_t[:, :], s_cstg)
    bar()
    vop(lambda e: e.memset(HUGES[:, :], 0x0FFFFFFF))
    vop(lambda e: e.memset(XNB[:, :], 0.0))
    bar()
    vop(lambda e: e.memset(ONESF[:, :], 1.0))
    vop(lambda e: e.memset(EPS1[:, :], EPS_LN))
    vop(lambda e: e.memset(EPS2[:, :], EPS_NRM))
    act(WRHI[:, :], WRF[:, :], AF.Copy)
    act(BRHI[:, :], BRF[:, :], AF.Copy)
    bar()
    vop(lambda e: e.tensor_copy(WRT1[:, :], WRHI[:, :]))
    vop(lambda e: e.tensor_copy(BRT1[:, :], BRHI[:, :]), drain=True)
    vop(lambda e: e.tensor_sub(WRT1[:, :], WRF[:, :], WRT1[:, :]), drain=True)
    vop(lambda e: e.tensor_sub(BRT1[:, :], BRF[:, :], BRT1[:, :]), drain=True)
    bar()
    act(WRLO[:, :], WRT1[:, :], AF.Copy)
    act(BRLO[:, :], BRT1[:, :], AF.Copy)
    bar()

    for pi, pn in enumerate(("w1q", "w2q", "w1k", "w2k")):
        w = dma("gpsimd", TRASH[0:64, 0:1024], wq[pn][:, :], s_cstg)
        for dt_ in range(NDT):
            tr(pbf(dt_ % 2, width=64), TRASH[0:64, dt_ * 128:(dt_ + 1) * 128],
               CONST["idn_bf"][0:64, 0:64], waits=[w, sACT.at()])
            act(WQT[:, pi * 512 + dt_ * 64: pi * 512 + (dt_ + 1) * 64],
                pbf(dt_ % 2, width=64), AF.Copy, waits=[sPE.at()])
        bar()


    for wsrc, wdst in ((wv_t, wvt_s), (wo_t, wot_s)):
        for jt in range(NDT):
            dma("gpsimd", ar(VT_OFF + jt * 1024, 1024),
                wsrc[jt * 128:(jt + 1) * 128, :], s_cstg)
        ldw = s_cstg.at()
        for dt_ in range(NDT):
            for jt in range(NDT):
                k = dt_ * NDT + jt
                tr(pbf(k % 2), ar(VT_OFF + jt * 1024 + dt_ * 128, 128),
                   CONST["idn_bf"][:, :], waits=[ldw, sACT.at()])
                act(ar(WT_OFF + dt_ * 1024 + jt * 128, 128), pbf(k % 2), AF.Copy,
                    waits=[sPE.at()])
        bar()
        dma("sync", wdst[:, :], ar(WT_OFF, NDT * 1024), s_wts)
        bar()


    w1conv_done = {}
    w2conv_done = {}
    _gate = [(s_xn[0], (NCHUNK // 2) * 16), (s_xn[1], (NCHUNK // 2) * 16),
             (s_wt[0], 16), (s_wt[1], 16)]
    _last = [None, None]
    _k = 0
    for ei in range(E):
        for (wsrc_t, wdst_t, rows, cols, dones) in (
                (w1_t, w1bf, D, FF, w1conv_done),
                (w2_t, w2bf, FF, D, w2conv_done)):
            rows_per = (8192 // cols) * 128
            for ch in range(rows // rows_per):
                bi = _k % 2
                _k += 1
                r0 = ch * rows_per
                w = [_last[bi]] if _last[bi] is not None else list(_gate)
                _last[bi] = dma("gpsimd", wdst_t[ei, r0:r0 + rows_per, :],
                                wsrc_t[ei, r0:r0 + rows_per, :],
                                s_cv[bi], waits=w)
            dones[ei] = [m for m in _last if m is not None]
    conv_staging_free = []


    NCH = T // 128
    NCH2 = 8 * CAP // 128

    def lall_fn(g):
        g.sem_inc(s_yc.h, 1)
        with g.register("gb") as rb, g.register("gi") as ri, \
             g.register("gw") as rw, g.register("gj") as rj, \
             g.register("gr") as rr:
            with g.Fori(0, NB) as b_:
                g.reg_mov(rb, b_)
                g.reg_add(rw, rb, 1)
                g.wait_ge(s_p1.h, rw)
                with g.Fori(0, NCH) as i_:
                    g.reg_mov(ri, i_)
                    g.reg_mul(rj, rb, NCH)
                    g.reg_add(rj, rj, ri)
                    g.reg_mul(rw, rj, 16)
                    g.wait_ge(s_l1.h, rw)
                    g.reg_mul(rw, rj, 128)
                    g.dma_start(STG[:, 0:1],
                                bass.AP(offs_d, rw, [[1, 128]])
                                ).then_inc(s_stg.h, 16)
                    g.dma_start(STG[:, 1:2],
                                bass.AP(iotat_t, rw, [[1, 128]])
                                ).then_inc(s_stg.h, 16)
                    g.reg_mul(rw, rj, 32)
                    g.reg_add(rw, rw, 32)
                    g.wait_ge(s_stg.h, rw)
                    g.indirect_dma_start(
                        lists_d[:, :],
                        IndirectOffsetOnAxis(ap=STG[:, 0:1], axis=0),
                        STG[:, 1:2], None,
                        bounds_check=NB * 8 * CAP - 1,
                        oob_is_err=False).then_inc(s_l1.h, 16)
                g.reg_add(rw, rb, 1)
                g.reg_mul(rw, rw, 16 * NCH)
                g.wait_ge(s_l1.h, rw)
                with g.Fori(0, NCH2) as i_:
                    g.reg_mov(ri, i_)
                    g.reg_mul(rj, rb, NCH2)
                    g.reg_add(rj, rj, ri)
                    g.reg_mul(rw, rj, 16)
                    g.wait_ge(s_st2[0].h, rw)
                    g.reg_mul(rw, rj, 128)
                    g.dma_start(STG[:, 2:3],
                                bass.AP(lists_d, rw, [[1, 128]])
                                ).then_inc(s_g2[0].h, 16)
                    g.reg_mul(rw, rj, 32)
                    g.reg_add(rw, rw, 16)
                    g.wait_ge(s_g2[0].h, rw)
                    g.indirect_dma_start(
                        XST[:, 0:1024], None,
                        xnat_d[:, :],
                        IndirectOffsetOnAxis(ap=STG[:, 2:3], axis=0),
                        bounds_check=NB * T - 1,
                        oob_is_err=False).then_inc(s_g2[0].h, 16)
                    g.reg_mul(rw, rj, 32)
                    g.reg_add(rw, rw, 32)
                    g.wait_ge(s_g2[0].h, rw)
                    g.reg_mul(rr, rj, 128 * 1024)
                    g.dma_start(
                        bass.AP(xperm_d, rr, [[1024, 128], [1, 1024]]),
                        XST[:, 0:1024],
                    ).then_inc(s_st2[0].h, 16)
                g.reg_add(rw, rb, 1)
                g.wait_ge(s_p3.h, rw)
                with g.Fori(0, NCH) as i_:
                    g.reg_mov(ri, i_)
                    g.reg_mul(rj, rb, NCH)
                    g.reg_add(rj, rj, ri)
                    g.reg_add(rw, rj, 1)
                    g.wait_ge(s_yc.h, rw)
                    g.reg_mul(rw, rj, 32)
                    g.wait_ge(s_l3[0].h, rw)
                    g.reg_mul(rw, rj, 128)
                    g.dma_start(
                        STG[:, 4:5],
                        bass.AP(offs_d, rw, [[1, 128]]),
                    ).then_inc(s_l3[0].h, 16)
                    g.reg_mul(rw, rj, 32)
                    g.reg_add(rw, rw, 16)
                    g.wait_ge(s_l3[0].h, rw)
                    g.indirect_dma_start(
                        XNF[:, 0:1024], None,
                        yperm_d[:, :],
                        IndirectOffsetOnAxis(ap=STG[:, 4:5], axis=0),
                        bounds_check=NB * 8 * CAP - 1,
                        oob_is_err=False).then_inc(s_l3[0].h, 16)
    plan.add("gpsimd", lall_fn)
    s_stg.n += 32 * NB * NCH
    s_l1.n += 16 * NB * NCH
    s_g2[0].n += 32 * NB * NCH2
    s_st2[0].n += 16 * NB * NCH2
    s_yc.n += 1
    s_l3[0].n += 32 * NB * NCH

    def pace(sem, waits):
        plan.add("vector", lambda e, s=sem: e.sem_inc(s.h, 1), waits=waits)
        sem.n += 1

    xst_free = [None, None]
    xnb_free = [None, None]
    pR = PF[0:128, 4 * 512: 4 * 512 + 8]
    pR_free = None

    for b in range(NB):
        f32bank_free = [None] * 4
        pbf_free = [None, None]
        for tt in range(NCHUNK):
            bi = tt % 2
            xst = XST[:, bi * 1024:(bi + 1) * 1024]
            xnf = XNF[:, bi * 1024:(bi + 1) * 1024]
            ln = LNST[:, bi * 8:(bi + 1) * 8]
            s1, s2c, mcol, tmpc, varc, rstd = (ln[:, i:i + 1] for i in range(6))
            w = [xst_free[bi]] if xst_free[bi] is not None else []
            ldx = dma("sync", xst,
                      x_in[b * T + tt * 128: b * T + (tt + 1) * 128, :],
                      s_x[bi], waits=w)
            act(TRASH[:, 0:1024], xst, AF.Copy, accum=s1, waits=[ldx], drain=True)
            a_s2 = act(TRASH[:, 0:1024], xst, AF.Square, accum=s2c, drain=True)
            vop(lambda e, a=mcol, b_=s1: e.tensor_scalar_mul(a, b_, 1.0 / D),
                waits=[a_s2], drain=True)
            vop(lambda e, a=tmpc, b_=mcol: e.tensor_mul(a, b_, b_), drain=True)
            v_var = vop(lambda e, a=varc, b_=s2c, c=tmpc:
                        e.scalar_tensor_tensor(a, b_, 1.0 / D, c, ALU.mult,
                                               ALU.subtract), drain=True)
            act(varc, varc, AF.Ln, bias=EPS1[:, 0:1], waits=[v_var], drain=True)
            a_rstd = act(rstd, varc, AF.Exp, scale=-0.5, drain=True)
            v_xnf = vop(lambda e, a=xnf, b_=xst, m_=mcol, r_=rstd:
                        e.tensor_scalar(a, b_, m_, r_, ALU.subtract, ALU.mult),
                        waits=[a_rstd], drain=True)
            xnb = XNB[:, bi * 1024:(bi + 1) * 1024]
            w = [v_xnf]
            if xnb_free[bi] is not None:
                w.append(xnb_free[bi])
            a_xnb = act(xnb, xnf, AF.Copy, waits=w)
            xnb_free[bi] = dma("sync",
                               xnat_d[b * T + tt * 128: b * T + (tt + 1) * 128, :],
                               xnb, s_xn[bi], waits=[a_xnb])
            xn_marks = []
            lo_marks = []
            for k in range(2 * NDT):
                isxn = k < NDT
                dt_ = k % NDT
                bank = k % 4
                src = xnf if isxn else xst
                wt = [v_xnf] if isxn else [ldx]
                if f32bank_free[bank] is not None:
                    wt += f32bank_free[bank]
                tr(pf(bank, 128), src[:, dt_ * 128:(dt_ + 1) * 128],
                   CONST["idn_f"][:, :], waits=wt)
                spe_t = sPE.at()
                dst = (XN if isxn else ACC)[:, dt_ * T + tt * 128:
                                            dt_ * T + (tt + 1) * 128]
                a_ev = act(dst, pf(bank, 128), AF.Copy, waits=[spe_t])
                if isxn:
                    v_lo = vop(lambda e, a=XNLO[:, dt_ * 128:(dt_ + 1) * 128],
                               p_=pf(bank, 128), x_=dst: e.tensor_sub(a, p_, x_),
                               waits=[a_ev], drain=True)
                    f32bank_free[bank] = [a_ev, v_lo]
                    xn_marks.append(a_ev)
                    lo_marks.append(v_lo)
                else:
                    f32bank_free[bank] = [a_ev]
            xn_ready = xn_marks[-1]
            lo_ready = lo_marks[-1]
            xst_free[bi] = sPE.at()
            rw = [xn_ready]
            if pR_free is not None:
                rw.append(pR_free)
            mm(CONST["onescol"][0:1, :], BRHI[0:1, :], pR, True, False, waits=rw)
            mm(CONST["onescol"][0:1, :], BRLO[0:1, :], pR, False, False)
            for dt_ in range(NDT):
                xnhi = XN[:, dt_ * T + tt * 128: dt_ * T + (tt + 1) * 128]
                xnlo = XNLO[:, dt_ * 128:(dt_ + 1) * 128]
                last = dt_ == NDT - 1
                mm(xnhi, WRHI[:, dt_ * 8:(dt_ + 1) * 8], pR, False, False)
                mm(xnhi, WRLO[:, dt_ * 8:(dt_ + 1) * 8], pR, False, False)
                spe_r = mm(xnlo, WRHI[:, dt_ * 8:(dt_ + 1) * 8], pR, False, last,
                           waits=[lo_ready] if dt_ == 0 else (), inc=last)
            v_mx = vop(lambda e, a=tmpc, p_=pR: e.reduce_max(a, p_, axis=AX.X),
                       waits=[spe_r], drain=True)
            v_msk = vop(lambda e, a=TRASH[0:128, 0:8], p_=pR, m_=tmpc:
                        e.tensor_single_scalar(a, p_, m_, ALU.is_ge), drain=True)
            pR_free = v_msk
            wtr = [v_msk]
            if pbf_free[0] is not None:
                wtr.append(pbf_free[0])
            tr(pbf(0, rows=8), TRASH[0:128, 0:8], CONST["idn_bf"][:, :], waits=wtr)
            a_m8 = act(MASK8[0:8, tt * 128:(tt + 1) * 128], pbf(0, rows=8),
                       AF.Copy, waits=[sPE.at()])
            pbf_free[0] = a_m8
        bar()

        v_z8 = vop(lambda e: e.memset(XST[0:8, 0:T], 0.0))
        P8 = XNF[0:8, 0:T]
        v_scan = vop(lambda e, a=P8, z=XST[0:8, 0:T], m_=MASK8[0:8, 0:T]:
                     e.tensor_tensor_scan(a, z, m_, 0.0, ALU.add, ALU.add),
                     waits=[v_z8], drain=True)
        v_mf = vop(lambda e: e.tensor_copy(XST[0:8, 0:T], MASK8[0:8, 0:T]),
                   waits=[v_scan], drain=True)
        for tc in range(NQC):
            mm(CONST["ones8"][0:8, 2:3], XST[0:8, tc * 512:(tc + 1) * 512],
               PF[0:1, tc * 512:(tc + 1) * 512], True, False,
               waits=[v_mf] if tc == 0 else (), inc=tc == NQC - 1)
        v_mp = vop(lambda e, a=XST[0:8, 0:T], p_=P8:
                   e.tensor_mul(a, a, p_), waits=[sPE.at()], drain=True)
        for tc in range(NQC):
            mm(CONST["ones8"][0:8, 0:1], XST[0:8, tc * 512:(tc + 1) * 512],
               PF[0:1, tc * 512:(tc + 1) * 512], False, True,
               waits=[v_mp] if tc == 0 else (), inc=tc == NQC - 1)
        v_ps = vop(lambda e: e.tensor_copy(XST[64:65, 0:T], PF[0:1, 0:T]),
                   waits=[sPE.at()])
        v_off = vop(lambda e, c=float(1 - b * 8 * CAP):
                    e.tensor_scalar_sub(XST[64:65, 0:T], XST[64:65, 0:T], c),
                    waits=[v_ps], drain=True)
        offb_free = [None, None]
        off_marks = []
        for tt in range(NCHUNK):
            k = tt % 2
            wts = [v_off]
            if offb_free[k] is not None:
                wts.append(offb_free[k])
            tr(pf(4 + k, width=1), XST[64:65, tt * 128:(tt + 1) * 128],
               CONST["idn_f"][64:65, 64:65], waits=wts)
            m_ = vop(lambda e, a=OFFS[:, tt:tt + 1], p_=pf(4 + k, width=1):
                     e.tensor_copy(a, p_), waits=[sPE.at()])
            offb_free[k] = m_
            off_marks.append(m_)
        offs_ready = off_marks[-1]
        for tt in range(NCHUNK):
            od1 = dma("sync", offs_d[b * NCHUNK + tt, :], OFFS[:, tt:tt + 1],
                      s_wts,
                      waits=[off_marks[tt]])
        il = dma("sync", lists_d[b * 8 * CAP:(b + 1) * 8 * CAP, 0].rearrange(
            "(p c) -> p c", p=128), HUGES[:, 0:8 * CAP // 128], s_il)
        pace(s_p1, [od1, il, (s_xn[0], s_xn[0].n), (s_xn[1], s_xn[1].n)])

        wvt_l = dma("sync", ar(WT_OFF, NDT * 1024), wvt_s[:, :], s_wt[0])
        vbank_free = [None, None]
        for jt in range(NDT):
            for tc in range(NQC):
                bank = (jt * NQC + tc) % 2
                for dt_ in range(NDT):
                    wts = []
                    if dt_ == 0:
                        wts.append(wvt_l)
                        if vbank_free[bank] is not None:
                            wts.append(vbank_free[bank])
                    mm(ar(WT_OFF + dt_ * 1024 + jt * 128, 128),
                       XN[:, dt_ * T + tc * 512: dt_ * T + (tc + 1) * 512],
                       pf(bank), dt_ == 0, dt_ == NDT - 1, waits=wts,
                       inc=dt_ == NDT - 1)
                vbank_free[bank] = act(
                    ar(VT_OFF + jt * T + tc * 512, 512), pf(bank), AF.Identity,
                    bias=BVC[:, jt:jt + 1], waits=[sPE.at()])
        bar()
        v_ones = vop(lambda e: e.memset(
            bass.AP(AR, VAUG_OFF + 64, [[ARW, 128], [1040, KT], [65, 16]]), 1.0),
            waits=conv_staging_free)
        pbf_free = [None, None]
        for kt in range(KT):
            for jt in range(NDT):
                k = kt * NDT + jt
                wts = [v_ones] if k == 0 else []
                if pbf_free[k % 2] is not None:
                    wts.append(pbf_free[k % 2])
                tr(pbf(k % 2), ar(VT_OFF + jt * T + kt * 128, 128),
                   CONST["idn_bf"][:, :], waits=wts)
                spe_t = sPE.at()
                pbf_free[k % 2] = vop(
                    lambda e, p_=pbf(k % 2),
                    o=bass.AP(AR, VAUG_OFF + kt * 1040 + 2 * jt * 65,
                              [[ARW, 128], [65, 2], [1, 64]]):
                    e.tensor_copy(o, p_), waits=[spe_t])
        bar()

        PT_OFF = WT_OFF
        pbank_free = [None, None]
        for pi in range(4):
            for tc in range(NQC):
                bank = (pi * NQC + tc) % 2
                for dt_ in range(NDT):
                    wts = []
                    if dt_ == 0 and pbank_free[bank] is not None:
                        wts.append(pbank_free[bank])
                    mm(WQT[:, pi * 512 + dt_ * 64: pi * 512 + (dt_ + 1) * 64],
                       XN[:, dt_ * T + tc * 512: dt_ * T + (tc + 1) * 512],
                       pf(bank, rows=64), dt_ == 0, dt_ == NDT - 1, waits=wts,
                       inc=dt_ == NDT - 1)
                pbank_free[bank] = act(
                    AR[0:64, PT_OFF + pi * T + tc * 512:
                       PT_OFF + pi * T + (tc + 1) * 512],
                    pf(bank, rows=64), AF.Copy, waits=[sPE.at()])
        bar()
        T1_OFF = SD_OFF
        LPRE_OFF = SD_OFF + 2048
        SQ_OFF = LPRE_OFF + 512
        INV_OFF = SQ_OFF + 512
        BINV_OFF = INV_OFF + 512
        selbank_free = [None] * 4
        iter_gate = None
        nbank_free = None
        padbank_free = [None, None]
        binvbank_free = [None, None]
        binvsb_free = [None, None]
        last_pad_use = None
        for side in range(2):
            pad_off = PADQ_OFF if side == 0 else PADK_OFF
            si = CONST["sqi"] if side == 0 else CONST["ski"]
            sj = CONST["sqj"] if side == 0 else CONST["skj"]
            p1 = PT_OFF + (0 if side == 0 else 2 * T)
            p2 = p1 + T
            for tc in range(NQC):
                sels = [(si, p1), (sj, p2), (sj, p1), (si, p2)]
                ev_marks = []
                for k, (smat, poff) in enumerate(sels):
                    wts = []
                    if k == 0 and iter_gate is not None:
                        wts += iter_gate
                    if selbank_free[k] is not None:
                        wts.append(selbank_free[k])
                    mm(smat[:, :], AR[0:64, poff + tc * 512: poff + (tc + 1) * 512],
                       pf(k, rows=96), True, True, waits=wts, inc=True)
                    ev = act(AR[0:96, T1_OFF + k * 512: T1_OFF + (k + 1) * 512],
                             pf(k, rows=96), AF.Copy, waits=[sPE.at()])
                    selbank_free[k] = ev
                    ev_marks.append(ev)
                t_ = [AR[0:96, T1_OFF + k * 512: T1_OFF + (k + 1) * 512]
                      for k in range(4)]
                lpre = AR[0:96, LPRE_OFF:LPRE_OFF + 512]
                sq = AR[0:96, SQ_OFF:SQ_OFF + 512]
                vop(lambda e, a=lpre, x=t_[0], y=t_[1]: e.tensor_mul(a, x, y),
                    waits=[ev_marks[-1]] + ([last_pad_use] if last_pad_use else []),
                    drain=True)
                vop(lambda e, a=sq, x=t_[2], y=t_[3]: e.tensor_mul(a, x, y),
                    drain=True)
                v_lp = vop(lambda e, a=lpre, x=lpre, y=sq: e.tensor_sub(a, x, y),
                           drain=True)
                a_sq = act(sq, lpre, AF.Square, waits=[v_lp], drain=True)
                wts = [a_sq]
                if nbank_free is not None:
                    wts.append(nbank_free)
                mm(CONST["o96"][:, :], sq, pf(4, rows=16), True, True,
                   waits=wts, inc=True)
                inv = AR[0:16, INV_OFF:INV_OFF + 512]
                act(inv, pf(4, rows=16), AF.Ln, bias=EPS2[0:16, 0:1],
                    waits=[sPE.at()], drain=True)
                inv_ready = act(inv, inv, AF.Exp, scale=-0.5, drain=True)
                nbank_free = inv_ready
                for g in range(4):
                    bank = g % 2
                    wts = [v_lp]
                    if padbank_free[bank] is not None:
                        wts.append(padbank_free[bank])
                    mm(CONST["pg"][:, g * 128:(g + 1) * 128], lpre,
                       pf(bank), True, True, waits=wts, inc=True)
                    spe_pad = sPE.at()
                    wts = [inv_ready]
                    if binvbank_free[bank] is not None:
                        wts.append(binvbank_free[bank])
                    mm(CONST["s16"][0:16, g * 128:(g + 1) * 128], inv,
                       pf(2 + bank), True, True, waits=wts, inc=True)
                    binv = AR[0:128, BINV_OFF + bank * 512:
                              BINV_OFF + (bank + 1) * 512]
                    wts = [sPE.at()]
                    if binvsb_free[bank] is not None:
                        wts.append(binvsb_free[bank])
                    a_bi = act(binv, pf(2 + bank), AF.Copy, waits=wts)
                    binvbank_free[bank] = a_bi
                    dst = AR[0:128, pad_off + g * T + tc * 512:
                             pad_off + g * T + (tc + 1) * 512]
                    if side == 0:
                        v_c = vop(lambda e, a=dst, p_=pf(bank), y=binv:
                                  e.tensor_mul(a, p_, y),
                                  waits=[a_bi, spe_pad], drain=True)
                    else:
                        v_c = vop(lambda e, a=dst, p_=pf(bank), y=binv:
                                  e.scalar_tensor_tensor(
                                      a, p_, CONST["sgnpad"][:, 0:1], y,
                                      ALU.mult, ALU.mult),
                                  waits=[a_bi, spe_pad], drain=True)
                    padbank_free[bank] = v_c
                    binvsb_free[bank] = v_c
                last_pad_use = v_c
                iter_gate = [sACT.at(), sDVE.at()]
        bar()

        wot_l = dma("sync", ar(WT_OFF, NDT * 1024), wot_s[:, :], s_wt[1])
        AT_OFF = SD_OFF
        OF_OFF = SD_OFF + 2048
        sc_free = {0: [], 1: []}
        at_free = {0: None, 1: None}
        po_gate = []
        pending_evac = None
        ofull_last = None
        recb_free = None
        proj_free = [None, None]
        for qc in range(NQC):
            nkt = 4 * qc + 4
            ofull_last = None
            for hp in range(H // 2):
                av_last = None
                ready = {}

                def emit_sc(kt):
                    par = kt % 2
                    cb = 2 * par
                    for hi in range(2):
                        h = 2 * hp + hi
                        g, j = divmod(h, 4)
                        base = 32 * j
                        tpos = (96, 0) if base == 96 else None
                        lq = AR[base:base + 6,
                                PADK_OFF + g * T + kt * 128:
                                PADK_OFF + g * T + kt * 128 + 128]
                        rq = AR[base:base + 6,
                                PADQ_OFF + g * T + qc * 512:
                                PADQ_OFF + g * T + (qc + 1) * 512]
                        wts = list(sc_free[par]) if hi == 0 else []
                        mm(lq, rq, pf(cb + hi), True, True, waits=wts,
                           inc=hi == 1, tile_position=tpos)
                    spe_sc = sPE.at()
                    atdst = AR[0:128, AT_OFF + par * 1024:
                               AT_OFF + par * 1024 + 1024]
                    wts = [spe_sc]
                    if at_free[par] is not None:
                        wts.append(at_free[par])
                    a_exp = act(atdst, PF[0:128, cb * 512: cb * 512 + 1024],
                                AF.Exp, scale=SCALE, waits=wts)
                    sc_free[par] = [a_exp]
                    rdy = a_exp
                    if kt >= 4 * qc:
                        d_ = kt - 4 * qc
                        mk = bass.AP(CONST["masks4"], d_ * 512,
                                     [[2048, 128], [0, 2], [1, 512]])
                        rdy = vop(lambda e, a=atdst, m_=mk:
                                  e.tensor_mul(a, a, m_), waits=[a_exp])
                    ready[kt] = rdy

                def emit_av(kt):
                    nonlocal av_last
                    par = kt % 2
                    for hi in range(2):
                        h = 2 * hp + hi
                        va = ar(VAUG_OFF + kt * 1040 + h * 65, 65)
                        at_ = AR[0:128, AT_OFF + par * 1024 + hi * 512:
                                 AT_OFF + par * 1024 + (hi + 1) * 512]
                        wts = []
                        if hi == 0:
                            wts.append(ready[kt])
                            if kt == 0:
                                wts += po_gate
                        av_last = mm(va, at_, pf(4 + hi, rows=65),
                                     kt == 0, kt == nkt - 1,
                                     waits=wts, inc=hi == 1)
                    at_free[par] = av_last

                emit_sc(0)
                emit_sc(1)
                if pending_evac is not None:
                    pending_evac()
                    pending_evac = None
                emit_av(0)
                for kt in range(2, nkt):
                    emit_sc(kt)
                    emit_av(kt - 1)
                emit_av(nkt - 1)

                def make_evac(av_mark, hp_):
                    def run_evac():
                        nonlocal recb_free, po_gate, ofull_last
                        po_marks = []
                        for hi in range(2):
                            h = 2 * hp_ + hi
                            dt_, half = divmod(h, 2)
                            rec = XNF[0:1, hi * 512: hi * 512 + 512]
                            recb = XNLO[0:1, hi * 512: hi * 512 + 512]
                            wts = [av_mark]
                            if recb_free is not None:
                                wts.append(recb_free)
                            act(rec,
                                PF[64:65, (4 + hi) * 512:(4 + hi) * 512 + 512],
                                AF.Ln, waits=wts, drain=True)
                            v_rec = act(recb, rec, AF.Exp, scale=-1.0,
                                        drain=True)
                            a_of = vop(
                                lambda e,
                                a=AR[half * 64: half * 64 + 64,
                                     OF_OFF + dt_ * 512:OF_OFF + (dt_ + 1) * 512],
                                p_=PF[0:64, (4 + hi) * 512:(4 + hi) * 512 + 512]:
                                e.tensor_copy(a, p_), waits=[av_mark])
                            mm(CONST["onescol"][0:1, 0:64], recb,
                               pf(4 + hi, rows=64),
                               True, True, waits=[v_rec, a_of], inc=True)
                            v_on = vop(lambda e,
                                       a=AR[half * 64: half * 64 + 64,
                                            OF_OFF + dt_ * 512:
                                            OF_OFF + (dt_ + 1) * 512],
                                       p_=pf(4 + hi, rows=64):
                                       e.tensor_mul(a, a, p_),
                                       waits=[sPE.at(), a_of])
                            po_marks.append(v_on)
                        recb_free = sPE.at()
                        po_gate = po_marks
                        ofull_last = po_marks[-1]
                    return run_evac

                pending_evac = make_evac(av_last, hp)
            pending_evac()
            pending_evac = None
            for jt in range(NDT):
                bank = jt % 2
                for dt_ in range(NDT):
                    wts = []
                    if dt_ == 0:
                        wts.append(ofull_last)
                        if jt == 0:
                            wts.append(wot_l)
                            wts += sc_free[0] + sc_free[1]
                        if proj_free[bank] is not None:
                            wts.append(proj_free[bank])
                    mm(ar(WT_OFF + dt_ * 1024 + jt * 128, 128),
                       AR[0:128, OF_OFF + dt_ * 512: OF_OFF + (dt_ + 1) * 512],
                       pf(bank), dt_ == 0, dt_ == NDT - 1, waits=wts,
                       inc=dt_ == NDT - 1)
                proj_free[bank] = vop(
                    lambda e, a=ACC[:, jt * T + qc * 512: jt * T + (qc + 1) * 512],
                    p_=pf(bank), bc=BOC[:, jt:jt + 1]:
                    e.scalar_tensor_tensor(a, p_, bc, a, ALU.add, ALU.add),
                    waits=[sPE.at()], drain=True)
            sc_free[0] = [m for m in proj_free if m is not None]
        bar()

        xg_free = None
        xgt_free = None
        yn_free = [None, None]
        b2row_free = None
        pbfa = [None, None]
        w1_free = [None, None]
        w2_free = [None, None]
        y_free = {2: None, 3: None, 4: None, 5: None}
        for ei in range(E):
            nm = (ei + 1) * (CAP // 128)
            wts = [(s_st2[0], 16 * (b * NCH2 + nm))]
            if xg_free is not None:
                wts.append(xg_free)
            xg_l = dma("sync", ar(XGN_OFF, CPW),
                       xperm_d[b * 8 * CAP + ei * CAP:
                               b * 8 * CAP + (ei + 1) * CAP, :].rearrange(
                           "(c p) f -> p c f", p=128),
                       s_xg[ei % 2], waits=wts)
            tm = []
            for ck in range(CAP // 128):
                for dt_ in range(NDT):
                    k = (ck * NDT + dt_) % 2
                    wts = [xg_l] if (ck == 0 and dt_ == 0) else []
                    if pbfa[k] is not None:
                        wts.append(pbfa[k])
                    if ck == 0 and dt_ == 0 and xgt_free is not None:
                        wts.append(xgt_free)
                    tr(pbf(k), ar(XGN_OFF + ck * 1024 + dt_ * 128, 128),
                       CONST["idn_bf"][:, :], waits=wts)
                    pbfa[k] = vop(lambda e,
                                  a=ar(XGT_OFF + dt_ * CAP + ck * 128, 128),
                                  p_=pbf(k): e.tensor_copy(a, p_),
                                  waits=[sPE.at()])
                    tm.append(pbfa[k])
            xgt_ready = tm[-1]
            xg_free = sPE.at()
            gelu_done = {}
            w1_l = None
            for ft in range(NFT):
                sbi = (ft // 4) % 2
                if ft % 4 == 0:
                    w = [w1_free[sbi]] if w1_free[sbi] is not None else []
                    if b == 0:
                        w = w + w1conv_done[ei]
                    w1_l = dma("sync", ar(W1OFF + sbi * 4096, 4096),
                               w1bf[ei, :, ft * 128:(ft + 4) * 128].rearrange(
                                   "(t p) f -> p t f", p=128),
                               s_w1[sbi], waits=w)
                bank = ft % 2
                for dt_ in range(NDT):
                    wts = []
                    if dt_ == 0:
                        wts.append(w1_l)
                        if ft >= 2:
                            wts.append(gelu_done[ft - 2])
                        if ft == 0:
                            wts.append(xgt_ready)
                    mm(ar(W1OFF + sbi * 4096 + dt_ * 512 + (ft % 4) * 128, 128),
                       ar(XGT_OFF + dt_ * CAP, CAP),
                       pf(bank, width=CAP), dt_ == 0, dt_ == NDT - 1, waits=wts,
                       inc=dt_ == NDT - 1)
                gelu_done[ft] = act(
                    AR[0:128, ft * CAP: ft * CAP + CAP], pf(bank, width=CAP),
                    AF.Gelu, bias=B1C[:, ei * NFT + ft: ei * NFT + ft + 1],
                    waits=[sPE.at()])
                if ft % 4 == 3:
                    w1_free[(ft // 4) % 2] = sPE.at()
            h_ready = gelu_done[NFT - 1]
            NT4 = CAP // 128
            b2_l = dma("sync", XNLO[0:1, 0:1024], b2rbf[ei:ei + 1, :],
                       s_b2, waits=[b2row_free] if b2row_free is not None
                       else [])
            groups = [list(range(g, min(g + 2, NT4)))
                      for g in range(0, NT4, 2)]
            for gi, grp in enumerate(groups):
                for ftg in range(NFT // 4):
                    sbi = ftg % 2
                    w = [w2_free[sbi]] if w2_free[sbi] is not None else []
                    if b == 0 and gi == 0 and ftg < 2:
                        w = w + w2conv_done[ei]
                    w2_l = dma("scalar", ar(W2OFF + sbi * 4096, 4096),
                               w2bf[ei, ftg * 512:(ftg + 1) * 512, :].rearrange(
                                   "(t p) f -> p t f", p=128),
                               s_w2[sbi], waits=w)
                    for t in range(4):
                        ft = ftg * 4 + t
                        for tt4 in grp:
                            for dh in range(2):
                                bank = 2 + (tt4 % 2) * 2 + dh
                                wts = []
                                if t == 0 and tt4 == grp[0] and dh == 0:
                                    wts.append(w2_l)
                                    if ftg == 0:
                                        wts.append(h_ready)
                                        wts.append(b2_l)
                                if ftg == 0 and t == 0 and dh == 0 and \
                                        y_free[bank] is not None:
                                    wts.append(y_free[bank])
                                    wts.append(y_free[bank + 1])
                                mm(AR[0:128, ft * CAP + tt4 * 128:
                                      ft * CAP + tt4 * 128 + 128],
                                   ar(W2OFF + sbi * 4096 + t * 1024 + dh * 512,
                                      512),
                                   pf(bank), ftg == 0 and t == 0, False,
                                   waits=wts,
                                   inc=(t == 3 and tt4 == grp[-1] and dh == 1))
                    w2_free[sbi] = sPE.at()
                for tt4 in grp:
                    for dh in range(2):
                        bank = 2 + (tt4 % 2) * 2 + dh
                        mm(CONST["onescol"][0:1, 0:128],
                           XNLO[0:1, dh * 512:(dh + 1) * 512],
                           pf(bank), False, True, inc=True)
                b2row_free = sPE.at()
                ev = []
                for tt4 in grp:
                    for dh in range(2):
                        bank = 2 + (tt4 % 2) * 2 + dh
                        wts = [b2row_free]
                        if yn_free[gi] is not None:
                            wts.append(yn_free[gi])
                        y_free[bank] = act(
                            ar(YN_OFF + tt4 * 1024 + dh * 512, 512),
                            pf(bank), AF.Copy, waits=wts)
                        ev.append(y_free[bank])
                r0 = b * 8 * CAP + ei * CAP + grp[0] * 128
                nrow = 128 * len(grp)
                yn_free[gi] = dma(
                    "sync",
                    yperm_d[r0:r0 + nrow, :].rearrange("(c p) f -> p c f",
                                                       p=128),
                    ar(YN_OFF + grp[0] * 1024, 1024 * len(grp)),
                    s_yp[(ei * len(groups) + gi) % 2], waits=ev)
        yperm_done = [s_yp[0].at(), s_yp[1].at()]
        bar()

        pace(s_p3, list(yperm_done) + [sACT.at(), sDVE.at(), sPE.at()])

        ost_free = [None, None]
        pbf_free = [None, None]
        for tt in range(NCHUNK):
            bi = tt % 2
            ost = XST[:, bi * 1024:(bi + 1) * 1024]
            a_last = None
            for jt in range(NDT):
                k = (tt * NDT + jt) % 2
                wts = []
                if ost_free[bi] is not None and jt == 0:
                    wts.append(ost_free[bi])
                if pbf_free[k] is not None:
                    wts.append(pbf_free[k])
                tr(pbf(k), ACC[:, jt * T + tt * 128: jt * T + (tt + 1) * 128],
                   CONST["idn_bf"][:, :], waits=wts)
                a_last = act(ost[:, jt * 128:(jt + 1) * 128], pbf(k), AF.Copy,
                             waits=[sPE.at()])
                pbf_free[k] = a_last
            plan.add("vector",
                     lambda e, a=ost, y_=XNF[:, 0:1024]:
                     e.tensor_add(a, a, y_),
                     waits=[a_last, (s_l3[0], 32 * (b * NCHUNK + tt) + 32)],
                     inc=(s_yc, 1))
            v_add = s_yc.at()
            st = dma("sync", out_t[b * T + tt * 128: b * T + (tt + 1) * 128, :],
                     ost, s_o[bi], waits=[v_add])
            ost_free[bi] = st
        bar()

    plan.add("sync", None, waits=[(s, s.n) for s in sems.values()])

    with nc.Block() as blk:
        plan.emit(blk)
    es.close()
    return nc


def prep_inputs(L0_attn_x, L0_attn_W1_q, L0_attn_W2_q, L0_attn_W1_k, L0_attn_W2_k,
                L0_attn_W_v, L0_attn_b_v, L0_attn_W_out, L0_attn_b_out,
                L0_ffn_wr, L0_ffn_br, L0_ffn_w1, L0_ffn_b1, L0_ffn_w2, L0_ffn_b2,
                T=2048, ncores=8):
    f = lambda a: np.ascontiguousarray(np.asarray(a, dtype=np.float32))
    consts = make_consts(T)
    base = {
        "w1q": f(L0_attn_W1_q), "w2q": f(L0_attn_W2_q),
        "w1k": f(L0_attn_W1_k), "w2k": f(L0_attn_W2_k),
        "wv": f(L0_attn_W_v), "wo": f(L0_attn_W_out),
        "wr": f(np.asarray(L0_ffn_wr, np.float32).reshape(NDT, 128, 8)
                .transpose(1, 0, 2).reshape(128, NDT * 8)),
        "br": f(np.asarray(L0_ffn_br, np.float32).reshape(1, 8)),
        "w1": f(L0_ffn_w1), "w2": f(L0_ffn_w2),
        "bvc": f(np.asarray(L0_attn_b_v, np.float32).reshape(NDT, 128).T),
        "boc": f(np.asarray(L0_attn_b_out, np.float32).reshape(NDT, 128).T),
        "b1c": f(np.asarray(L0_ffn_b1, np.float32).reshape(E, NFT, 128)
                 .transpose(2, 0, 1).reshape(128, E * NFT)),
        "b2c": f(np.asarray(L0_ffn_b2, np.float32).reshape(E, NDT, 128)
                 .transpose(2, 0, 1).reshape(128, E * NDT)),
        "b2r": f(np.asarray(L0_ffn_b2, np.float32).reshape(E, D)),
    }
    for n in CONST_SPECS:
        base["c_" + n] = np.ascontiguousarray(consts[n])
    x = np.asarray(L0_attn_x, np.float32)
    in_maps = []
    bpc = x.shape[0] // ncores
    iot = (np.arange(bpc * T // 128, dtype=np.uint32)[:, None] * 128
           + np.arange(128, dtype=np.uint32)[None, :])
    base["iotat"] = np.ascontiguousarray(iot)
    for c in range(ncores):
        m = dict(base)
        m["x"] = np.ascontiguousarray(x[c * bpc:(c + 1) * bpc].reshape(bpc * T, D))
        in_maps.append(m)
    return in_maps


def kernel(**inputs):
    T, NB, ncores = 2048, 4, 8
    in_maps = prep_inputs(**inputs, T=T, ncores=ncores)
    nc = build_program(T=T, NB=NB)
    res = run_bass_kernel_spmd(nc, in_maps, core_ids=list(range(ncores)))
    out = np.concatenate(
        [np.asarray(r["out"]).reshape(NB, T, D) for r in res.results], axis=0)
    return out.astype(np.float32)


# version 16
# speedup vs baseline: 1.2166x; 1.1317x over previous
import contextlib
import numpy as np
import ml_dtypes
import concourse.bass as bass
import concourse.mybir as mybir
from concourse.bass_utils import run_bass_kernel_spmd
from concourse.bass_core_rust import IndirectOffsetOnAxis

F32 = mybir.dt.float32
BF16 = mybir.dt.bfloat16
U32 = mybir.dt.uint32
AF = mybir.ActivationFunctionType
ALU = mybir.AluOpType
AX = mybir.AxisListType

PAIRS = [(0, 1), (0, 2), (0, 3), (1, 2), (1, 3), (2, 3)]
SGN6 = [1.0, -1.0, 1.0, 1.0, -1.0, 1.0]
SCALE = 6.0 ** -0.5
D = 1024
H = 16
E = 8
FF = 4096
NDT = D // 128
NFT = FF // 128
EPS_LN = 1e-5
EPS_NRM = 1e-24


class Sem:
    def __init__(self, nc, es, name):
        self.h = es.enter_context(nc.semaphore(name))
        self.n = 0
        self.name = name
        import os
        if os.environ.get("SEM_DEBUG"):
            print("SEM", self.h.num, name)

    def at(self):
        return (self, self.n)


class Plan:
    ENGINES = ("sync", "gpsimd", "tensor", "scalar", "vector")

    def __init__(self):
        self.ops = {e: [] for e in self.ENGINES}

    def add(self, eng, fn, waits=(), inc=None, drain=False):
        if inc is not None:
            inc[0].n += inc[1]
        ws = []
        for s, v in waits:
            if v > 0:
                ws.append((s, v))
        self.ops[eng].append((fn, ws, inc, drain))

    def barrier(self, sems):
        marks = [(s, s.n) for s in sems if s.n > 0]
        for e in self.ENGINES:
            self.add(e, None, waits=marks)

    def emit(self, blk):
        def mk(name):
            def run(eng):
                for fn, waits, inc, drain in self.ops[name]:
                    if drain:
                        eng.drain()
                    for s, v in waits:
                        eng.wait_ge(s.h, v)
                    if fn is None:
                        continue
                    instr = fn(eng)
                    if inc is not None:
                        instr.then_inc(inc[0].h, inc[1])
            return run
        blk.sync(mk("sync"))
        blk.gpsimd(mk("gpsimd"))
        blk.tensor(mk("tensor"))
        blk.scalar(mk("scalar"))
        blk.vector(mk("vector"))


def make_consts(T):
    c = {}
    c["idn_f"] = np.eye(128, dtype=np.float32)
    c["idn_bf"] = np.eye(128, dtype=np.float32)
    m4 = np.zeros((128, 4 * 512), np.float32)
    for d_ in range(4):
        delta = d_ * 128
        r = np.arange(128)[:, None]
        q = np.arange(512)[None, :]
        m4[:, d_ * 512:(d_ + 1) * 512] = (r + delta <= q).astype(np.float32)
    c["masks4"] = m4
    sqi = np.zeros((64, 96), np.float32)
    sqj = np.zeros((64, 96), np.float32)
    ski = np.zeros((64, 96), np.float32)
    skj = np.zeros((64, 96), np.float32)
    for h in range(H):
        for cc in range(6):
            i, j = PAIRS[cc]
            ir, jr = PAIRS[5 - cc]
            sqi[4 * h + i, 6 * h + cc] = 1
            sqj[4 * h + j, 6 * h + cc] = 1
            ski[4 * h + ir, 6 * h + cc] = 1
            skj[4 * h + jr, 6 * h + cc] = 1
    c["sqi"], c["sqj"], c["ski"], c["skj"] = sqi, sqj, ski, skj
    o96 = np.zeros((96, 16), np.float32)
    for h in range(H):
        o96[6 * h:6 * h + 6, h] = 1
    c["o96"] = o96
    pg = np.zeros((96, 4 * 128), np.float32)
    s16 = np.zeros((16, 4 * 128), np.float32)
    for g in range(4):
        for j in range(4):
            for cc in range(6):
                pg[6 * (4 * g + j) + cc, g * 128 + 32 * j + cc] = 1
                s16[4 * g + j, g * 128 + 32 * j + cc] = 1
    c["pg"], c["s16"] = pg, s16
    ones8 = np.zeros((8, 3), np.float32)
    ones8[:, 0] = 1
    ones8[:, 1] = np.arange(8)
    ones8[:, 2] = np.arange(8) * (T // 4)
    c["ones8"] = ones8
    c["onescol"] = np.ones((1, 128), np.float32)
    sgn = np.ones((128, 1), np.float32)
    for j in range(4):
        for cc in range(6):
            sgn[32 * j + cc, 0] = SGN6[cc]
    c["sgnpad"] = sgn
    return c


CONST_SPECS = {
    "idn_f": ([128, 128], F32), "idn_bf": ([128, 128], BF16),
    "masks4": ([128, 2048], BF16),
    "sqi": ([64, 96], BF16), "sqj": ([64, 96], BF16),
    "ski": ([64, 96], BF16), "skj": ([64, 96], BF16),
    "o96": ([96, 16], BF16), "pg": ([96, 512], BF16), "s16": ([16, 512], BF16),
    "ones8": ([8, 3], F32),
    "onescol": ([1, 128], BF16), "sgnpad": ([128, 1], F32),
}


def build_program(T=2048, NB=4):
    KT = T // 128
    NQC = T // 512
    NTH = max(1, T // 1024)
    TH = T // NTH
    NTCQ = TH // 512
    NCHUNK = T // 128

    nc = bass.Bass("TRN2", target_bir_lowering=False)

    x_in = nc.dram_tensor("x", [NB * T, D], F32, kind="ExternalInput")
    out_t = nc.dram_tensor("out", [NB * T, D], F32, kind="ExternalOutput")
    wq = {p: nc.dram_tensor(p, [64, D], F32, kind="ExternalInput")
          for p in ("w1q", "w2q", "w1k", "w2k")}
    wv_t = nc.dram_tensor("wv", [D, D], F32, kind="ExternalInput")
    wo_t = nc.dram_tensor("wo", [D, D], F32, kind="ExternalInput")
    wr_t = nc.dram_tensor("wr", [128, NDT * 8], F32, kind="ExternalInput")
    br_t = nc.dram_tensor("br", [1, 8], F32, kind="ExternalInput")
    w1_t = nc.dram_tensor("w1", [E, D, FF], F32, kind="ExternalInput")
    w2_t = nc.dram_tensor("w2", [E, FF, D], F32, kind="ExternalInput")
    bvc_t = nc.dram_tensor("bvc", [128, NDT], F32, kind="ExternalInput")
    boc_t = nc.dram_tensor("boc", [128, NDT], F32, kind="ExternalInput")
    b1c_t = nc.dram_tensor("b1c", [128, E * NFT], F32, kind="ExternalInput")
    b2c_t = nc.dram_tensor("b2c", [128, E * NDT], F32, kind="ExternalInput")
    b2r_t = nc.dram_tensor("b2r", [E, D], F32, kind="ExternalInput")
    cin = {n: nc.dram_tensor("c_" + n, [sh[0], sh[1]], F32, kind="ExternalInput")
           for n, (sh, dt) in CONST_SPECS.items()}
    w1bf = nc.dram_tensor("w1bf", [E, D, FF], BF16)
    w2bf = nc.dram_tensor("w2bf", [E, FF, D], BF16)
    wvt_s = nc.dram_tensor("wvt_s", [128, NDT * 1024], BF16)
    b2rbf = nc.dram_tensor("b2rbf", [E, D], BF16)
    wot_s = nc.dram_tensor("wot_s", [128, NDT * 1024], BF16)
    CAP = T // 4
    xnat_d = nc.dram_tensor("xnat_d", [NB * T, D], BF16)
    xperm_d = nc.dram_tensor("xperm_d", [NB * 8 * CAP, D], BF16)
    yperm_d = nc.dram_tensor("yperm_d", [NB * 8 * CAP, D], BF16)
    lists_d = nc.dram_tensor("lists_d", [NB * 8 * CAP, 1], U32)
    offs_d = nc.dram_tensor("offs_d", [NB * (T // 128), 128], U32)
    iotat_t = nc.dram_tensor("iotat", [NB * (T // 128), 128], U32,
                             kind="ExternalInput")

    es = contextlib.ExitStack()

    def sbuf(name, shape, dtype):
        return es.enter_context(nc.sbuf_tensor(name, shape, dtype))

    XN = sbuf("XN", [128, NDT * T], BF16)
    ACC = sbuf("ACC", [128, NDT * T], BF16)
    _CAP = T // 4
    _CPW = (_CAP // 128) * 1024
    ARW = max(8 * T + KT * 1040 + 8192 + 6656,
              32 * _CAP + 2 * _CPW + 2 * NDT * _CAP + 4 * 4096)
    assert ARW <= 49600, ARW
    AR = sbuf("AR", [128, ARW], BF16)
    XST = sbuf("XST", [128, 2 * 1024], F32)
    XNF = sbuf("XNF", [128, 2 * 1024], F32)
    XNLO = sbuf("XNLO", [128, 1024], BF16)
    TRASH = sbuf("TRASH", [128, 1024], BF16)
    LNST = sbuf("LNST", [128, 16], F32)
    MASK8 = sbuf("MASK8", [8, T], BF16)
    XNB = sbuf("XNB", [128, 2 * 1024], BF16)
    OFFS = sbuf("OFFS", [128, 16], U32)
    HUGES = sbuf("HUGES", [128, 32], U32)
    STG = sbuf("STG", [128, 8], U32)
    WQT = sbuf("WQT", [128, 4 * 512], BF16)
    WRHI = sbuf("WRHI", [128, NDT * 8], BF16)
    WRLO = sbuf("WRLO", [128, NDT * 8], BF16)
    WRF = sbuf("WRF", [128, NDT * 8], F32)
    WRT1 = sbuf("WRT1", [128, NDT * 8], F32)
    BRHI = sbuf("BRHI", [1, 8], BF16)
    BRLO = sbuf("BRLO", [1, 8], BF16)
    BRF = sbuf("BRF", [1, 8], F32)
    BRT1 = sbuf("BRT1", [1, 8], F32)
    BVC = sbuf("BVC", [128, NDT], F32)
    BOC = sbuf("BOC", [128, NDT], F32)
    B1C = sbuf("B1C", [128, E * NFT], F32)
    B2C = sbuf("B2C", [128, E * NDT], F32)
    ONESF = sbuf("ONESF", [1, 128], F32)
    EPS1 = sbuf("EPS1", [128, 1], F32)
    EPS2 = sbuf("EPS2", [128, 1], F32)
    CONST = {n: sbuf("k_" + n, sh, dt) for n, (sh, dt) in CONST_SPECS.items()}

    PF = es.enter_context(nc.psum_tensor("PF", [128, 3072], F32))
    PB = es.enter_context(nc.psum_tensor("PB", [128, 2048], BF16))

    def pf(bank, width=512, rows=128):
        return PF[0:rows, bank * 512: bank * 512 + width]

    def pbf(half, rows=128, width=128):
        return PB[0:rows, half * 1024: half * 1024 + width]

    def ar(off, width, rows=128, r0=0):
        return AR[r0:r0 + rows, off:off + width]

    PADQ_OFF = 0
    PADK_OFF = 4 * T
    VT_OFF = 0
    VAUG_OFF = 8 * T
    WT_OFF = VAUG_OFF + KT * 1040
    SD_OFF = WT_OFF + 8192
    HS_OFF = 0
    CPW = (CAP // 128) * 1024
    XGN_OFF = 32 * CAP
    XGT_OFF = XGN_OFF + CPW
    YT_OFF = XGT_OFF + NDT * CAP
    YN_OFF = YT_OFF + NDT * CAP
    W1OFF = YN_OFF + CPW
    W2OFF = W1OFF + 2 * 4096
    assert W2OFF + 2 * 4096 <= ARW, (W2OFF, ARW)


    plan = Plan()
    sems = {}

    def S(name):
        if name not in sems:
            sems[name] = Sem(nc, es, name)
        return sems[name]

    sPE, sACT, sDVE = S("sPE"), S("sACT"), S("sDVE")
    s_cst = S("s_cst")
    s_cstg = S("s_cstg")
    s_cv = [S("s_cv0"), S("s_cv1")]
    s_x = [S("s_x0"), S("s_x1")]
    s_wt = [S("s_wt0"), S("s_wt1")]
    s_w1 = [S("s_w1a"), S("s_w1b")]
    s_w2 = [S("s_w2a"), S("s_w2b")]
    s_o = [S("s_o0"), S("s_o1")]
    s_wts = S("s_wts")
    s_xn = [S("s_xn0"), S("s_xn1")]
    s_stg = S("s_stg")
    s_l1 = S("s_l1")
    s_g2 = [S("s_g2a")]
    s_st2 = [S("s_st2a")]
    s_l3 = [S("s_l3a")]
    s_yc = S("s_yc")
    s_il = S("s_il")
    s_xg = [S("s_xg0"), S("s_xg1")]
    s_b2 = S("s_b2")
    s_yp = [S("s_yp0"), S("s_yp1")]
    s_p1 = S("s_p1")
    s_p3 = S("s_p3")

    NOBAR = {"s_cv0", "s_cv1", "s_sv0", "s_sv1",
             "s_stg", "s_l1", "s_g2a", "s_g2b", "s_st2a", "s_st2b",
             "s_l3a", "s_l3b", "s_il", "s_p1", "s_p3", "s_yc"}

    def bar():
        plan.barrier([s for n, s in sems.items() if n not in NOBAR])

    def dma(q, dst, src, sem, waits=()):
        plan.add(q, lambda e, a=dst, b=src: e.dma_start(a, b), waits=waits,
                 inc=(sem, 16))
        return sem.at()

    def mm(lhsT, rhs, out, start, stop, waits=(), inc=False, tile_position=None):
        kw = {}
        if tile_position is not None:
            kw["tile_position"] = tile_position
        plan.add("tensor",
                 lambda e, a=out, l=lhsT, r=rhs, s1=start, s2=stop, k=dict(kw):
                 e.matmul(a, l, r, start=s1, stop=s2, **k),
                 waits=waits, inc=(sPE, 1) if inc else None)
        return sPE.at()

    def tr(out, in_, idn, waits=(), inc=True):
        plan.add("tensor", lambda e, a=out, b=in_, c=idn: e.transpose(a, b, c),
                 waits=waits, inc=(sPE, 1) if inc else None)
        return sPE.at()

    def act(out, in_, func, scale=1.0, bias=None, accum=None, waits=(),
            inc=True, drain=False):
        def f(e, a=out, b=in_, fn=func, sc=scale, bi=bias, ac=accum):
            kw = {"scale": sc}
            if bi is not None:
                kw["bias"] = bi
            if ac is not None:
                kw["accum_out"] = ac
            return e.activation(a, b, fn, **kw)
        plan.add("scalar", f, waits=waits, inc=(sACT, 1) if inc else None,
                 drain=drain)
        return sACT.at()

    def vop(fn, waits=(), inc=True, drain=False):
        plan.add("vector", fn, waits=waits, inc=(sDVE, 1) if inc else None,
                 drain=drain)
        return sDVE.at()

    for n, (sh, dt) in CONST_SPECS.items():
        q = "sync" if dt == F32 else "gpsimd"
        dma(q, CONST[n][:, :], cin[n][:, :], s_cst if q == "sync" else s_cstg)
    dma("sync", WRF[:, :], wr_t[:, :], s_cst)
    dma("sync", BRF[:, :], br_t[:, :], s_cst)
    dma("sync", BVC[:, :], bvc_t[:, :], s_cst)
    dma("sync", BOC[:, :], boc_t[:, :], s_cst)
    dma("sync", B1C[:, :], b1c_t[:, :], s_cst)
    dma("sync", B2C[:, :], b2c_t[:, :], s_cst)
    dma("gpsimd", b2rbf[:, :], b2r_t[:, :], s_cstg)
    bar()
    vop(lambda e: e.memset(HUGES[:, :], 0x0FFFFFFF))
    vop(lambda e: e.memset(XNB[:, :], 0.0))
    bar()
    vop(lambda e: e.memset(ONESF[:, :], 1.0))
    vop(lambda e: e.memset(EPS1[:, :], EPS_LN))
    vop(lambda e: e.memset(EPS2[:, :], EPS_NRM))
    act(WRHI[:, :], WRF[:, :], AF.Copy)
    act(BRHI[:, :], BRF[:, :], AF.Copy)
    bar()
    vop(lambda e: e.tensor_copy(WRT1[:, :], WRHI[:, :]))
    vop(lambda e: e.tensor_copy(BRT1[:, :], BRHI[:, :]), drain=True)
    vop(lambda e: e.tensor_sub(WRT1[:, :], WRF[:, :], WRT1[:, :]), drain=True)
    vop(lambda e: e.tensor_sub(BRT1[:, :], BRF[:, :], BRT1[:, :]), drain=True)
    bar()
    act(WRLO[:, :], WRT1[:, :], AF.Copy)
    act(BRLO[:, :], BRT1[:, :], AF.Copy)
    bar()

    for pi, pn in enumerate(("w1q", "w2q", "w1k", "w2k")):
        w = dma("gpsimd", TRASH[0:64, 0:1024], wq[pn][:, :], s_cstg)
        for dt_ in range(NDT):
            tr(pbf(dt_ % 2, width=64), TRASH[0:64, dt_ * 128:(dt_ + 1) * 128],
               CONST["idn_bf"][0:64, 0:64], waits=[w, sACT.at()])
            act(WQT[:, pi * 512 + dt_ * 64: pi * 512 + (dt_ + 1) * 64],
                pbf(dt_ % 2, width=64), AF.Copy, waits=[sPE.at()])
        bar()


    for wsrc, wdst in ((wv_t, wvt_s), (wo_t, wot_s)):
        for jt in range(NDT):
            dma("gpsimd", ar(VT_OFF + jt * 1024, 1024),
                wsrc[jt * 128:(jt + 1) * 128, :], s_cstg)
        ldw = s_cstg.at()
        for dt_ in range(NDT):
            for jt in range(NDT):
                k = dt_ * NDT + jt
                tr(pbf(k % 2), ar(VT_OFF + jt * 1024 + dt_ * 128, 128),
                   CONST["idn_bf"][:, :], waits=[ldw, sACT.at()])
                act(ar(WT_OFF + dt_ * 1024 + jt * 128, 128), pbf(k % 2), AF.Copy,
                    waits=[sPE.at()])
        bar()
        dma("sync", wdst[:, :], ar(WT_OFF, NDT * 1024), s_wts)
        bar()


    w1conv_done = {}
    w2conv_done = {}
    _gate = [(s_xn[0], (NCHUNK // 2) * 16), (s_xn[1], (NCHUNK // 2) * 16),
             (s_wt[0], 16), (s_wt[1], 16)]
    _last = [None, None]
    _k = 0
    for ei in range(E):
        for (wsrc_t, wdst_t, rows, cols, dones) in (
                (w1_t, w1bf, D, FF, w1conv_done),
                (w2_t, w2bf, FF, D, w2conv_done)):
            rows_per = (8192 // cols) * 128
            for ch in range(rows // rows_per):
                bi = _k % 2
                _k += 1
                r0 = ch * rows_per
                w = [_last[bi]] if _last[bi] is not None else list(_gate)
                _last[bi] = dma("gpsimd", wdst_t[ei, r0:r0 + rows_per, :],
                                wsrc_t[ei, r0:r0 + rows_per, :],
                                s_cv[bi], waits=w)
            dones[ei] = [m for m in _last if m is not None]
    conv_staging_free = []


    NCH = T // 128
    NCH2 = 8 * CAP // 128

    def lall_fn(g):
        g.sem_inc(s_yc.h, 1)
        with g.register("gb") as rb, g.register("gi") as ri, \
             g.register("gw") as rw, g.register("gj") as rj, \
             g.register("gr") as rr:
            with g.Fori(0, NB) as b_:
                g.reg_mov(rb, b_)
                g.reg_add(rw, rb, 1)
                g.wait_ge(s_p1.h, rw)
                with g.Fori(0, NCH) as i_:
                    g.reg_mov(ri, i_)
                    g.reg_mul(rj, rb, NCH)
                    g.reg_add(rj, rj, ri)
                    g.reg_mul(rw, rj, 16)
                    g.wait_ge(s_l1.h, rw)
                    g.reg_mul(rw, rj, 128)
                    g.dma_start(STG[:, 0:1],
                                bass.AP(offs_d, rw, [[1, 128]])
                                ).then_inc(s_stg.h, 16)
                    g.dma_start(STG[:, 1:2],
                                bass.AP(iotat_t, rw, [[1, 128]])
                                ).then_inc(s_stg.h, 16)
                    g.reg_mul(rw, rj, 32)
                    g.reg_add(rw, rw, 32)
                    g.wait_ge(s_stg.h, rw)
                    g.indirect_dma_start(
                        lists_d[:, :],
                        IndirectOffsetOnAxis(ap=STG[:, 0:1], axis=0),
                        STG[:, 1:2], None,
                        bounds_check=NB * 8 * CAP - 1,
                        oob_is_err=False).then_inc(s_l1.h, 16)
                g.reg_add(rw, rb, 1)
                g.reg_mul(rw, rw, 16 * NCH)
                g.wait_ge(s_l1.h, rw)
                with g.Fori(0, NCH2) as i_:
                    g.reg_mov(ri, i_)
                    g.reg_mul(rj, rb, NCH2)
                    g.reg_add(rj, rj, ri)
                    g.reg_mul(rw, rj, 16)
                    g.wait_ge(s_st2[0].h, rw)
                    g.reg_mul(rw, rj, 128)
                    g.dma_start(STG[:, 2:3],
                                bass.AP(lists_d, rw, [[1, 128]])
                                ).then_inc(s_g2[0].h, 16)
                    g.reg_mul(rw, rj, 32)
                    g.reg_add(rw, rw, 16)
                    g.wait_ge(s_g2[0].h, rw)
                    g.indirect_dma_start(
                        XST[:, 0:1024], None,
                        xnat_d[:, :],
                        IndirectOffsetOnAxis(ap=STG[:, 2:3], axis=0),
                        bounds_check=NB * T - 1,
                        oob_is_err=False).then_inc(s_g2[0].h, 16)
                    g.reg_mul(rw, rj, 32)
                    g.reg_add(rw, rw, 32)
                    g.wait_ge(s_g2[0].h, rw)
                    g.reg_mul(rr, rj, 128 * 1024)
                    g.dma_start(
                        bass.AP(xperm_d, rr, [[1024, 128], [1, 1024]]),
                        XST[:, 0:1024],
                    ).then_inc(s_st2[0].h, 16)
                g.reg_add(rw, rb, 1)
                g.wait_ge(s_p3.h, rw)
                with g.Fori(0, NCH) as i_:
                    g.reg_mov(ri, i_)
                    g.reg_mul(rj, rb, NCH)
                    g.reg_add(rj, rj, ri)
                    g.reg_add(rw, rj, 1)
                    g.wait_ge(s_yc.h, rw)
                    g.reg_mul(rw, rj, 32)
                    g.wait_ge(s_l3[0].h, rw)
                    g.reg_mul(rw, rj, 128)
                    g.dma_start(
                        STG[:, 4:5],
                        bass.AP(offs_d, rw, [[1, 128]]),
                    ).then_inc(s_l3[0].h, 16)
                    g.reg_mul(rw, rj, 32)
                    g.reg_add(rw, rw, 16)
                    g.wait_ge(s_l3[0].h, rw)
                    g.indirect_dma_start(
                        XNF[:, 0:1024], None,
                        yperm_d[:, :],
                        IndirectOffsetOnAxis(ap=STG[:, 4:5], axis=0),
                        bounds_check=NB * 8 * CAP - 1,
                        oob_is_err=False).then_inc(s_l3[0].h, 16)
    plan.add("gpsimd", lall_fn)
    s_stg.n += 32 * NB * NCH
    s_l1.n += 16 * NB * NCH
    s_g2[0].n += 32 * NB * NCH2
    s_st2[0].n += 16 * NB * NCH2
    s_yc.n += 1
    s_l3[0].n += 32 * NB * NCH

    def pace(sem, waits):
        plan.add("vector", lambda e, s=sem: e.sem_inc(s.h, 1), waits=waits)
        sem.n += 1

    xst_free = [None, None]
    xnb_free = [None, None]
    pR = PF[0:128, 4 * 512: 4 * 512 + 8]
    pR_free = None

    for b in range(NB):
        f32bank_free = [None] * 4
        pbf_free = [None, None]
        for tt in range(NCHUNK):
            bi = tt % 2
            xst = XST[:, bi * 1024:(bi + 1) * 1024]
            xnf = XNF[:, bi * 1024:(bi + 1) * 1024]
            ln = LNST[:, bi * 8:(bi + 1) * 8]
            s1, s2c, mcol, tmpc, varc, rstd = (ln[:, i:i + 1] for i in range(6))
            w = [xst_free[bi]] if xst_free[bi] is not None else []
            ldx = dma("sync", xst,
                      x_in[b * T + tt * 128: b * T + (tt + 1) * 128, :],
                      s_x[bi], waits=w)
            act(TRASH[:, 0:1024], xst, AF.Copy, accum=s1, waits=[ldx], drain=True)
            a_s2 = act(TRASH[:, 0:1024], xst, AF.Square, accum=s2c, drain=True)
            vop(lambda e, a=mcol, b_=s1: e.tensor_scalar_mul(a, b_, 1.0 / D),
                waits=[a_s2], drain=True)
            vop(lambda e, a=tmpc, b_=mcol: e.tensor_mul(a, b_, b_), drain=True)
            v_var = vop(lambda e, a=varc, b_=s2c, c=tmpc:
                        e.scalar_tensor_tensor(a, b_, 1.0 / D, c, ALU.mult,
                                               ALU.subtract), drain=True)
            act(varc, varc, AF.Ln, bias=EPS1[:, 0:1], waits=[v_var], drain=True)
            a_rstd = act(rstd, varc, AF.Exp, scale=-0.5, drain=True)
            v_xnf = vop(lambda e, a=xnf, b_=xst, m_=mcol, r_=rstd:
                        e.tensor_scalar(a, b_, m_, r_, ALU.subtract, ALU.mult),
                        waits=[a_rstd], drain=True)
            xnb = XNB[:, bi * 1024:(bi + 1) * 1024]
            w = [v_xnf]
            if xnb_free[bi] is not None:
                w.append(xnb_free[bi])
            a_xnb = act(xnb, xnf, AF.Copy, waits=w)
            xnb_free[bi] = dma("sync",
                               xnat_d[b * T + tt * 128: b * T + (tt + 1) * 128, :],
                               xnb, s_xn[bi], waits=[a_xnb])
            xn_marks = []
            lo_marks = []
            for k in range(2 * NDT):
                isxn = k < NDT
                dt_ = k % NDT
                bank = k % 4
                src = xnf if isxn else xst
                wt = [v_xnf] if isxn else [ldx]
                if f32bank_free[bank] is not None:
                    wt += f32bank_free[bank]
                tr(pf(bank, 128), src[:, dt_ * 128:(dt_ + 1) * 128],
                   CONST["idn_f"][:, :], waits=wt)
                spe_t = sPE.at()
                dst = (XN if isxn else ACC)[:, dt_ * T + tt * 128:
                                            dt_ * T + (tt + 1) * 128]
                a_ev = act(dst, pf(bank, 128), AF.Copy, waits=[spe_t])
                if isxn:
                    v_lo = vop(lambda e, a=XNLO[:, dt_ * 128:(dt_ + 1) * 128],
                               p_=pf(bank, 128), x_=dst: e.tensor_sub(a, p_, x_),
                               waits=[a_ev], drain=True)
                    f32bank_free[bank] = [a_ev, v_lo]
                    xn_marks.append(a_ev)
                    lo_marks.append(v_lo)
                else:
                    f32bank_free[bank] = [a_ev]
            xn_ready = xn_marks[-1]
            lo_ready = lo_marks[-1]
            xst_free[bi] = sPE.at()
            rw = [xn_ready]
            if pR_free is not None:
                rw.append(pR_free)
            mm(CONST["onescol"][0:1, :], BRHI[0:1, :], pR, True, False, waits=rw)
            mm(CONST["onescol"][0:1, :], BRLO[0:1, :], pR, False, False)
            for dt_ in range(NDT):
                xnhi = XN[:, dt_ * T + tt * 128: dt_ * T + (tt + 1) * 128]
                xnlo = XNLO[:, dt_ * 128:(dt_ + 1) * 128]
                last = dt_ == NDT - 1
                mm(xnhi, WRHI[:, dt_ * 8:(dt_ + 1) * 8], pR, False, False)
                mm(xnhi, WRLO[:, dt_ * 8:(dt_ + 1) * 8], pR, False, False)
                spe_r = mm(xnlo, WRHI[:, dt_ * 8:(dt_ + 1) * 8], pR, False, last,
                           waits=[lo_ready] if dt_ == 0 else (), inc=last)
            v_mx = vop(lambda e, a=tmpc, p_=pR: e.reduce_max(a, p_, axis=AX.X),
                       waits=[spe_r], drain=True)
            v_msk = vop(lambda e, a=TRASH[0:128, 0:8], p_=pR, m_=tmpc:
                        e.tensor_single_scalar(a, p_, m_, ALU.is_ge), drain=True)
            pR_free = v_msk
            wtr = [v_msk]
            if pbf_free[0] is not None:
                wtr.append(pbf_free[0])
            tr(pbf(0, rows=8), TRASH[0:128, 0:8], CONST["idn_bf"][:, :], waits=wtr)
            a_m8 = act(MASK8[0:8, tt * 128:(tt + 1) * 128], pbf(0, rows=8),
                       AF.Copy, waits=[sPE.at()])
            pbf_free[0] = a_m8
        bar()

        v_z8 = vop(lambda e: e.memset(XST[0:8, 0:T], 0.0))
        P8 = XNF[0:8, 0:T]
        v_scan = vop(lambda e, a=P8, z=XST[0:8, 0:T], m_=MASK8[0:8, 0:T]:
                     e.tensor_tensor_scan(a, z, m_, 0.0, ALU.add, ALU.add),
                     waits=[v_z8], drain=True)
        v_mf = vop(lambda e: e.tensor_copy(XST[0:8, 0:T], MASK8[0:8, 0:T]),
                   waits=[v_scan], drain=True)
        for tc in range(NQC):
            mm(CONST["ones8"][0:8, 2:3], XST[0:8, tc * 512:(tc + 1) * 512],
               PF[0:1, tc * 512:(tc + 1) * 512], True, False,
               waits=[v_mf] if tc == 0 else (), inc=tc == NQC - 1)
        v_mp = vop(lambda e, a=XST[0:8, 0:T], p_=P8:
                   e.tensor_mul(a, a, p_), waits=[sPE.at()], drain=True)
        for tc in range(NQC):
            mm(CONST["ones8"][0:8, 0:1], XST[0:8, tc * 512:(tc + 1) * 512],
               PF[0:1, tc * 512:(tc + 1) * 512], False, True,
               waits=[v_mp] if tc == 0 else (), inc=tc == NQC - 1)
        v_ps = vop(lambda e: e.tensor_copy(XST[64:65, 0:T], PF[0:1, 0:T]),
                   waits=[sPE.at()])
        v_off = vop(lambda e, c=float(1 - b * 8 * CAP):
                    e.tensor_scalar_sub(XST[64:65, 0:T], XST[64:65, 0:T], c),
                    waits=[v_ps], drain=True)
        offb_free = [None, None]
        off_marks = []
        for tt in range(NCHUNK):
            k = tt % 2
            wts = [v_off]
            if offb_free[k] is not None:
                wts.append(offb_free[k])
            tr(pf(4 + k, width=1), XST[64:65, tt * 128:(tt + 1) * 128],
               CONST["idn_f"][64:65, 64:65], waits=wts)
            m_ = vop(lambda e, a=OFFS[:, tt:tt + 1], p_=pf(4 + k, width=1):
                     e.tensor_copy(a, p_), waits=[sPE.at()])
            offb_free[k] = m_
            off_marks.append(m_)
        offs_ready = off_marks[-1]
        for tt in range(NCHUNK):
            od1 = dma("sync", offs_d[b * NCHUNK + tt, :], OFFS[:, tt:tt + 1],
                      s_wts,
                      waits=[off_marks[tt]])
        il = dma("sync", lists_d[b * 8 * CAP:(b + 1) * 8 * CAP, 0].rearrange(
            "(p c) -> p c", p=128), HUGES[:, 0:8 * CAP // 128], s_il)
        pace(s_p1, [od1, il, (s_xn[0], s_xn[0].n), (s_xn[1], s_xn[1].n)])

        wvt_l = dma("sync", ar(WT_OFF, NDT * 1024), wvt_s[:, :], s_wt[0])
        vbank_free = [None, None]
        for jt in range(NDT):
            for tc in range(NQC):
                bank = (jt * NQC + tc) % 2
                for dt_ in range(NDT):
                    wts = []
                    if dt_ == 0:
                        wts.append(wvt_l)
                        if vbank_free[bank] is not None:
                            wts.append(vbank_free[bank])
                    mm(ar(WT_OFF + dt_ * 1024 + jt * 128, 128),
                       XN[:, dt_ * T + tc * 512: dt_ * T + (tc + 1) * 512],
                       pf(bank), dt_ == 0, dt_ == NDT - 1, waits=wts,
                       inc=dt_ == NDT - 1)
                vbank_free[bank] = act(
                    ar(VT_OFF + jt * T + tc * 512, 512), pf(bank), AF.Identity,
                    bias=BVC[:, jt:jt + 1], waits=[sPE.at()])
        bar()
        v_ones = vop(lambda e: e.memset(
            bass.AP(AR, VAUG_OFF + 64, [[ARW, 128], [1040, KT], [65, 16]]), 1.0),
            waits=conv_staging_free)
        pbf_free = [None, None]
        for kt in range(KT):
            for jt in range(NDT):
                k = kt * NDT + jt
                wts = [v_ones] if k == 0 else []
                if pbf_free[k % 2] is not None:
                    wts.append(pbf_free[k % 2])
                tr(pbf(k % 2), ar(VT_OFF + jt * T + kt * 128, 128),
                   CONST["idn_bf"][:, :], waits=wts)
                spe_t = sPE.at()
                pbf_free[k % 2] = vop(
                    lambda e, p_=pbf(k % 2),
                    o=bass.AP(AR, VAUG_OFF + kt * 1040 + 2 * jt * 65,
                              [[ARW, 128], [65, 2], [1, 64]]):
                    e.tensor_copy(o, p_), waits=[spe_t])
        bar()

        PT_OFF = WT_OFF
        pbank_free = [None, None]
        for pi in range(4):
            for tc in range(NQC):
                bank = (pi * NQC + tc) % 2
                for dt_ in range(NDT):
                    wts = []
                    if dt_ == 0 and pbank_free[bank] is not None:
                        wts.append(pbank_free[bank])
                    mm(WQT[:, pi * 512 + dt_ * 64: pi * 512 + (dt_ + 1) * 64],
                       XN[:, dt_ * T + tc * 512: dt_ * T + (tc + 1) * 512],
                       pf(bank, rows=64), dt_ == 0, dt_ == NDT - 1, waits=wts,
                       inc=dt_ == NDT - 1)
                pbank_free[bank] = act(
                    AR[0:64, PT_OFF + pi * T + tc * 512:
                       PT_OFF + pi * T + (tc + 1) * 512],
                    pf(bank, rows=64), AF.Copy, waits=[sPE.at()])
        bar()
        T1_OFF = SD_OFF
        LPRE_OFF = SD_OFF + 2048
        SQ_OFF = LPRE_OFF + 512
        INV_OFF = SQ_OFF + 512
        BINV_OFF = INV_OFF + 512
        selbank_free = [None] * 4
        iter_gate = None
        nbank_free = None
        padbank_free = [None, None]
        binvbank_free = [None, None]
        binvsb_free = [None, None]
        last_pad_use = None
        for side in range(2):
            pad_off = PADQ_OFF if side == 0 else PADK_OFF
            si = CONST["sqi"] if side == 0 else CONST["ski"]
            sj = CONST["sqj"] if side == 0 else CONST["skj"]
            p1 = PT_OFF + (0 if side == 0 else 2 * T)
            p2 = p1 + T
            for tc in range(NQC):
                sels = [(si, p1), (sj, p2), (sj, p1), (si, p2)]
                ev_marks = []
                for k, (smat, poff) in enumerate(sels):
                    wts = []
                    if k == 0 and iter_gate is not None:
                        wts += iter_gate
                    if selbank_free[k] is not None:
                        wts.append(selbank_free[k])
                    mm(smat[:, :], AR[0:64, poff + tc * 512: poff + (tc + 1) * 512],
                       pf(k, rows=96), True, True, waits=wts, inc=True)
                    ev = act(AR[0:96, T1_OFF + k * 512: T1_OFF + (k + 1) * 512],
                             pf(k, rows=96), AF.Copy, waits=[sPE.at()])
                    selbank_free[k] = ev
                    ev_marks.append(ev)
                t_ = [AR[0:96, T1_OFF + k * 512: T1_OFF + (k + 1) * 512]
                      for k in range(4)]
                lpre = AR[0:96, LPRE_OFF:LPRE_OFF + 512]
                sq = AR[0:96, SQ_OFF:SQ_OFF + 512]
                vop(lambda e, a=lpre, x=t_[0], y=t_[1]: e.tensor_mul(a, x, y),
                    waits=[ev_marks[-1]] + ([last_pad_use] if last_pad_use else []),
                    drain=True)
                vop(lambda e, a=sq, x=t_[2], y=t_[3]: e.tensor_mul(a, x, y),
                    drain=True)
                v_lp = vop(lambda e, a=lpre, x=lpre, y=sq: e.tensor_sub(a, x, y),
                           drain=True)
                a_sq = act(sq, lpre, AF.Square, waits=[v_lp], drain=True)
                wts = [a_sq]
                if nbank_free is not None:
                    wts.append(nbank_free)
                mm(CONST["o96"][:, :], sq, pf(4, rows=16), True, True,
                   waits=wts, inc=True)
                inv = AR[0:16, INV_OFF:INV_OFF + 512]
                act(inv, pf(4, rows=16), AF.Ln, bias=EPS2[0:16, 0:1],
                    waits=[sPE.at()], drain=True)
                inv_ready = act(inv, inv, AF.Exp, scale=-0.5, drain=True)
                nbank_free = inv_ready
                for g in range(4):
                    bank = g % 2
                    wts = [v_lp]
                    if padbank_free[bank] is not None:
                        wts.append(padbank_free[bank])
                    mm(CONST["pg"][:, g * 128:(g + 1) * 128], lpre,
                       pf(bank), True, True, waits=wts, inc=True)
                    spe_pad = sPE.at()
                    wts = [inv_ready]
                    if binvbank_free[bank] is not None:
                        wts.append(binvbank_free[bank])
                    mm(CONST["s16"][0:16, g * 128:(g + 1) * 128], inv,
                       pf(2 + bank), True, True, waits=wts, inc=True)
                    binv = AR[0:128, BINV_OFF + bank * 512:
                              BINV_OFF + (bank + 1) * 512]
                    wts = [sPE.at()]
                    if binvsb_free[bank] is not None:
                        wts.append(binvsb_free[bank])
                    a_bi = act(binv, pf(2 + bank), AF.Copy, waits=wts)
                    binvbank_free[bank] = a_bi
                    dst = AR[0:128, pad_off + g * T + tc * 512:
                             pad_off + g * T + (tc + 1) * 512]
                    if side == 0:
                        v_c = vop(lambda e, a=dst, p_=pf(bank), y=binv:
                                  e.tensor_mul(a, p_, y),
                                  waits=[a_bi, spe_pad], drain=True)
                    else:
                        v_c = vop(lambda e, a=dst, p_=pf(bank), y=binv:
                                  e.scalar_tensor_tensor(
                                      a, p_, CONST["sgnpad"][:, 0:1], y,
                                      ALU.mult, ALU.mult),
                                  waits=[a_bi, spe_pad], drain=True)
                    padbank_free[bank] = v_c
                    binvsb_free[bank] = v_c
                last_pad_use = v_c
                iter_gate = [sACT.at(), sDVE.at()]
        bar()

        wot_l = dma("sync", ar(WT_OFF, NDT * 1024), wot_s[:, :], s_wt[1])
        AT_OFF = SD_OFF
        OF_OFF = SD_OFF + 2048
        sc_free = {0: [], 1: []}
        at_free = {0: None, 1: None}
        po_gate = []
        pending_evac = None
        ofull_last = None
        recb_free = None
        proj_free = [None, None]
        for qc in range(NQC):
            nkt = 4 * qc + 4
            ofull_last = None
            for hp in range(H // 2):
                av_last = None
                ready = {}

                def emit_sc(kt):
                    par = kt % 2
                    cb = 2 * par
                    for hi in range(2):
                        h = 2 * hp + hi
                        g, j = divmod(h, 4)
                        base = 32 * j
                        tpos = (96, 0) if base == 96 else None
                        lq = AR[base:base + 32,
                                PADK_OFF + g * T + kt * 128:
                                PADK_OFF + g * T + kt * 128 + 128]
                        rq = AR[base:base + 32,
                                PADQ_OFF + g * T + qc * 512:
                                PADQ_OFF + g * T + (qc + 1) * 512]
                        wts = list(sc_free[par]) if hi == 0 else []
                        mm(lq, rq, pf(cb + hi), True, True, waits=wts,
                           inc=hi == 1, tile_position=tpos)
                    spe_sc = sPE.at()
                    atdst = AR[0:128, AT_OFF + par * 1024:
                               AT_OFF + par * 1024 + 1024]
                    wts = [spe_sc]
                    if at_free[par] is not None:
                        wts.append(at_free[par])
                    a_exp = act(atdst, PF[0:128, cb * 512: cb * 512 + 1024],
                                AF.Exp, scale=SCALE, waits=wts)
                    sc_free[par] = [a_exp]
                    rdy = a_exp
                    if kt >= 4 * qc:
                        d_ = kt - 4 * qc
                        mk = bass.AP(CONST["masks4"], d_ * 512,
                                     [[2048, 128], [0, 2], [1, 512]])
                        rdy = vop(lambda e, a=atdst, m_=mk:
                                  e.tensor_mul(a, a, m_), waits=[a_exp])
                    ready[kt] = rdy

                def emit_av(kt):
                    nonlocal av_last
                    par = kt % 2
                    for hi in range(2):
                        h = 2 * hp + hi
                        va = ar(VAUG_OFF + kt * 1040 + h * 65, 65)
                        at_ = AR[0:128, AT_OFF + par * 1024 + hi * 512:
                                 AT_OFF + par * 1024 + (hi + 1) * 512]
                        wts = []
                        if hi == 0:
                            wts.append(ready[kt])
                            if kt == 0:
                                wts += po_gate
                        av_last = mm(va, at_, pf(4 + hi, rows=65),
                                     kt == 0, kt == nkt - 1,
                                     waits=wts, inc=hi == 1)
                    at_free[par] = av_last

                emit_sc(0)
                emit_sc(1)
                if pending_evac is not None:
                    pending_evac()
                    pending_evac = None
                emit_av(0)
                for kt in range(2, nkt):
                    emit_sc(kt)
                    emit_av(kt - 1)
                emit_av(nkt - 1)

                def make_evac(av_mark, hp_):
                    def run_evac():
                        nonlocal recb_free, po_gate, ofull_last
                        po_marks = []
                        for hi in range(2):
                            h = 2 * hp_ + hi
                            dt_, half = divmod(h, 2)
                            rec = XNF[0:1, hi * 512: hi * 512 + 512]
                            recb = XNLO[0:1, hi * 512: hi * 512 + 512]
                            wts = [av_mark]
                            if recb_free is not None:
                                wts.append(recb_free)
                            act(rec,
                                PF[64:65, (4 + hi) * 512:(4 + hi) * 512 + 512],
                                AF.Ln, waits=wts, drain=True)
                            v_rec = act(recb, rec, AF.Exp, scale=-1.0,
                                        drain=True)
                            a_of = vop(
                                lambda e,
                                a=AR[half * 64: half * 64 + 64,
                                     OF_OFF + dt_ * 512:OF_OFF + (dt_ + 1) * 512],
                                p_=PF[0:64, (4 + hi) * 512:(4 + hi) * 512 + 512]:
                                e.tensor_copy(a, p_), waits=[av_mark])
                            mm(CONST["onescol"][0:1, 0:64], recb,
                               pf(4 + hi, rows=64),
                               True, True, waits=[v_rec, a_of], inc=True)
                            v_on = vop(lambda e,
                                       a=AR[half * 64: half * 64 + 64,
                                            OF_OFF + dt_ * 512:
                                            OF_OFF + (dt_ + 1) * 512],
                                       p_=pf(4 + hi, rows=64):
                                       e.tensor_mul(a, a, p_),
                                       waits=[sPE.at(), a_of])
                            po_marks.append(v_on)
                        recb_free = sPE.at()
                        po_gate = po_marks
                        ofull_last = po_marks[-1]
                    return run_evac

                pending_evac = make_evac(av_last, hp)
            pending_evac()
            pending_evac = None
            for jt in range(NDT):
                bank = jt % 2
                for dt_ in range(NDT):
                    wts = []
                    if dt_ == 0:
                        wts.append(ofull_last)
                        if jt == 0:
                            wts.append(wot_l)
                            wts += sc_free[0] + sc_free[1]
                        if proj_free[bank] is not None:
                            wts.append(proj_free[bank])
                    mm(ar(WT_OFF + dt_ * 1024 + jt * 128, 128),
                       AR[0:128, OF_OFF + dt_ * 512: OF_OFF + (dt_ + 1) * 512],
                       pf(bank), dt_ == 0, dt_ == NDT - 1, waits=wts,
                       inc=dt_ == NDT - 1)
                proj_free[bank] = vop(
                    lambda e, a=ACC[:, jt * T + qc * 512: jt * T + (qc + 1) * 512],
                    p_=pf(bank), bc=BOC[:, jt:jt + 1]:
                    e.scalar_tensor_tensor(a, p_, bc, a, ALU.add, ALU.add),
                    waits=[sPE.at()], drain=True)
            sc_free[0] = [m for m in proj_free if m is not None]
        bar()

        xg_free = None
        xgt_free = None
        yn_free = [None, None]
        b2row_free = None
        pbfa = [None, None]
        w1_free = [None, None]
        w2_free = [None, None]
        y_free = {2: None, 3: None, 4: None, 5: None}
        for ei in range(E):
            nm = (ei + 1) * (CAP // 128)
            wts = [(s_st2[0], 16 * (b * NCH2 + nm))]
            if xg_free is not None:
                wts.append(xg_free)
            xg_l = dma("sync", ar(XGN_OFF, CPW),
                       xperm_d[b * 8 * CAP + ei * CAP:
                               b * 8 * CAP + (ei + 1) * CAP, :].rearrange(
                           "(c p) f -> p c f", p=128),
                       s_xg[ei % 2], waits=wts)
            tm = []
            for ck in range(CAP // 128):
                for dt_ in range(NDT):
                    k = (ck * NDT + dt_) % 2
                    wts = [xg_l] if (ck == 0 and dt_ == 0) else []
                    if pbfa[k] is not None:
                        wts.append(pbfa[k])
                    if ck == 0 and dt_ == 0 and xgt_free is not None:
                        wts.append(xgt_free)
                    tr(pbf(k), ar(XGN_OFF + ck * 1024 + dt_ * 128, 128),
                       CONST["idn_bf"][:, :], waits=wts)
                    pbfa[k] = vop(lambda e,
                                  a=ar(XGT_OFF + dt_ * CAP + ck * 128, 128),
                                  p_=pbf(k): e.tensor_copy(a, p_),
                                  waits=[sPE.at()])
                    tm.append(pbfa[k])
            xgt_ready = tm[-1]
            xg_free = sPE.at()
            gelu_done = {}
            w1_l = None
            for ft in range(NFT):
                sbi = (ft // 4) % 2
                if ft % 4 == 0:
                    w = [w1_free[sbi]] if w1_free[sbi] is not None else []
                    if b == 0:
                        w = w + w1conv_done[ei]
                    w1_l = dma("sync", ar(W1OFF + sbi * 4096, 4096),
                               w1bf[ei, :, ft * 128:(ft + 4) * 128].rearrange(
                                   "(t p) f -> p t f", p=128),
                               s_w1[sbi], waits=w)
                bank = ft % 2
                for dt_ in range(NDT):
                    wts = []
                    if dt_ == 0:
                        wts.append(w1_l)
                        if ft >= 2:
                            wts.append(gelu_done[ft - 2])
                        if ft == 0:
                            wts.append(xgt_ready)
                    mm(ar(W1OFF + sbi * 4096 + dt_ * 512 + (ft % 4) * 128, 128),
                       ar(XGT_OFF + dt_ * CAP, CAP),
                       pf(bank, width=CAP), dt_ == 0, dt_ == NDT - 1, waits=wts,
                       inc=dt_ == NDT - 1)
                gelu_done[ft] = act(
                    AR[0:128, ft * CAP: ft * CAP + CAP], pf(bank, width=CAP),
                    AF.Gelu, bias=B1C[:, ei * NFT + ft: ei * NFT + ft + 1],
                    waits=[sPE.at()])
                if ft % 4 == 3:
                    w1_free[(ft // 4) % 2] = sPE.at()
            h_ready = gelu_done[NFT - 1]
            NT4 = CAP // 128
            b2_l = dma("sync", XNLO[0:1, 0:1024], b2rbf[ei:ei + 1, :],
                       s_b2, waits=[b2row_free] if b2row_free is not None
                       else [])
            groups = [list(range(g, min(g + 2, NT4)))
                      for g in range(0, NT4, 2)]
            for gi, grp in enumerate(groups):
                for ftg in range(NFT // 4):
                    sbi = ftg % 2
                    w = [w2_free[sbi]] if w2_free[sbi] is not None else []
                    if b == 0 and gi == 0 and ftg < 2:
                        w = w + w2conv_done[ei]
                    w2_l = dma("scalar", ar(W2OFF + sbi * 4096, 4096),
                               w2bf[ei, ftg * 512:(ftg + 1) * 512, :].rearrange(
                                   "(t p) f -> p t f", p=128),
                               s_w2[sbi], waits=w)
                    for t in range(4):
                        ft = ftg * 4 + t
                        for tt4 in grp:
                            for dh in range(2):
                                bank = 2 + (tt4 % 2) * 2 + dh
                                wts = []
                                if t == 0 and tt4 == grp[0] and dh == 0:
                                    wts.append(w2_l)
                                    if ftg == 0:
                                        wts.append(h_ready)
                                        wts.append(b2_l)
                                if ftg == 0 and t == 0 and dh == 0 and \
                                        y_free[bank] is not None:
                                    wts.append(y_free[bank])
                                    wts.append(y_free[bank + 1])
                                mm(AR[0:128, ft * CAP + tt4 * 128:
                                      ft * CAP + tt4 * 128 + 128],
                                   ar(W2OFF + sbi * 4096 + t * 1024 + dh * 512,
                                      512),
                                   pf(bank), ftg == 0 and t == 0, False,
                                   waits=wts,
                                   inc=(t == 3 and tt4 == grp[-1] and dh == 1))
                    w2_free[sbi] = sPE.at()
                for tt4 in grp:
                    for dh in range(2):
                        bank = 2 + (tt4 % 2) * 2 + dh
                        mm(CONST["onescol"][0:1, 0:128],
                           XNLO[0:1, dh * 512:(dh + 1) * 512],
                           pf(bank), False, True, inc=True)
                b2row_free = sPE.at()
                ev = []
                for tt4 in grp:
                    for dh in range(2):
                        bank = 2 + (tt4 % 2) * 2 + dh
                        wts = [b2row_free]
                        if yn_free[gi] is not None:
                            wts.append(yn_free[gi])
                        y_free[bank] = act(
                            ar(YN_OFF + tt4 * 1024 + dh * 512, 512),
                            pf(bank), AF.Copy, waits=wts)
                        ev.append(y_free[bank])
                r0 = b * 8 * CAP + ei * CAP + grp[0] * 128
                nrow = 128 * len(grp)
                yn_free[gi] = dma(
                    "sync",
                    yperm_d[r0:r0 + nrow, :].rearrange("(c p) f -> p c f",
                                                       p=128),
                    ar(YN_OFF + grp[0] * 1024, 1024 * len(grp)),
                    s_yp[(ei * len(groups) + gi) % 2], waits=ev)
        yperm_done = [s_yp[0].at(), s_yp[1].at()]
        bar()

        pace(s_p3, list(yperm_done) + [sACT.at(), sDVE.at(), sPE.at()])

        ost_free = [None, None]
        pbf_free = [None, None]
        for tt in range(NCHUNK):
            bi = tt % 2
            ost = XST[:, bi * 1024:(bi + 1) * 1024]
            a_last = None
            for jt in range(NDT):
                k = (tt * NDT + jt) % 2
                wts = []
                if ost_free[bi] is not None and jt == 0:
                    wts.append(ost_free[bi])
                if pbf_free[k] is not None:
                    wts.append(pbf_free[k])
                tr(pbf(k), ACC[:, jt * T + tt * 128: jt * T + (tt + 1) * 128],
                   CONST["idn_bf"][:, :], waits=wts)
                a_last = act(ost[:, jt * 128:(jt + 1) * 128], pbf(k), AF.Copy,
                             waits=[sPE.at()])
                pbf_free[k] = a_last
            plan.add("vector",
                     lambda e, a=ost, y_=XNF[:, 0:1024]:
                     e.tensor_add(a, a, y_),
                     waits=[a_last, (s_l3[0], 32 * (b * NCHUNK + tt) + 32)],
                     inc=(s_yc, 1))
            v_add = s_yc.at()
            st = dma("sync", out_t[b * T + tt * 128: b * T + (tt + 1) * 128, :],
                     ost, s_o[bi], waits=[v_add])
            ost_free[bi] = st
        bar()

    plan.add("sync", None, waits=[(s, s.n) for s in sems.values()])

    with nc.Block() as blk:
        plan.emit(blk)
    es.close()
    return nc


def prep_inputs(L0_attn_x, L0_attn_W1_q, L0_attn_W2_q, L0_attn_W1_k, L0_attn_W2_k,
                L0_attn_W_v, L0_attn_b_v, L0_attn_W_out, L0_attn_b_out,
                L0_ffn_wr, L0_ffn_br, L0_ffn_w1, L0_ffn_b1, L0_ffn_w2, L0_ffn_b2,
                T=2048, ncores=8):
    f = lambda a: np.ascontiguousarray(np.asarray(a, dtype=np.float32))
    consts = make_consts(T)
    base = {
        "w1q": f(L0_attn_W1_q), "w2q": f(L0_attn_W2_q),
        "w1k": f(L0_attn_W1_k), "w2k": f(L0_attn_W2_k),
        "wv": f(L0_attn_W_v), "wo": f(L0_attn_W_out),
        "wr": f(np.asarray(L0_ffn_wr, np.float32).reshape(NDT, 128, 8)
                .transpose(1, 0, 2).reshape(128, NDT * 8)),
        "br": f(np.asarray(L0_ffn_br, np.float32).reshape(1, 8)),
        "w1": f(L0_ffn_w1), "w2": f(L0_ffn_w2),
        "bvc": f(np.asarray(L0_attn_b_v, np.float32).reshape(NDT, 128).T),
        "boc": f(np.asarray(L0_attn_b_out, np.float32).reshape(NDT, 128).T),
        "b1c": f(np.asarray(L0_ffn_b1, np.float32).reshape(E, NFT, 128)
                 .transpose(2, 0, 1).reshape(128, E * NFT)),
        "b2c": f(np.asarray(L0_ffn_b2, np.float32).reshape(E, NDT, 128)
                 .transpose(2, 0, 1).reshape(128, E * NDT)),
        "b2r": f(np.asarray(L0_ffn_b2, np.float32).reshape(E, D)),
    }
    for n in CONST_SPECS:
        base["c_" + n] = np.ascontiguousarray(consts[n])
    x = np.asarray(L0_attn_x, np.float32)
    in_maps = []
    bpc = x.shape[0] // ncores
    iot = (np.arange(bpc * T // 128, dtype=np.uint32)[:, None] * 128
           + np.arange(128, dtype=np.uint32)[None, :])
    base["iotat"] = np.ascontiguousarray(iot)
    for c in range(ncores):
        m = dict(base)
        m["x"] = np.ascontiguousarray(x[c * bpc:(c + 1) * bpc].reshape(bpc * T, D))
        in_maps.append(m)
    return in_maps


def kernel(**inputs):
    T, NB, ncores = 2048, 4, 8
    in_maps = prep_inputs(**inputs, T=T, ncores=ncores)
    nc = build_program(T=T, NB=NB)
    res = run_bass_kernel_spmd(nc, in_maps, core_ids=list(range(ncores)))
    out = np.concatenate(
        [np.asarray(r["out"]).reshape(NB, T, D) for r in res.results], axis=0)
    return out.astype(np.float32)
